# version 10
# baseline (speedup 1.0000x reference)
import math
import numpy as np
from contextlib import ExitStack

import concourse.bass as bass
import concourse.mybir as mybir
from concourse.bass_utils import run_bass_kernel_spmd
from concourse.bass_core_rust import IndirectOffsetOnAxis

F32 = mybir.dt.float32
BF16 = mybir.dt.bfloat16
I32 = mybir.dt.int32
U8 = mybir.dt.uint8
AF = mybir.ActivationFunctionType
ALU = mybir.AluOpType
AX = mybir.AxisListType

NCORES = 8
LN_EPS = 1e-5
RMS_EPS = 1e-6
MOE_EPS = 1.1920929e-07


def make_cfg(Tc=4096):
    cap = int(math.ceil(1.25 * (NCORES * Tc * 2) / 8.0))
    G = (cap + 127) // 128
    return dict(Tc=Tc, D=1024, INNER=2048, NA=64,
                EA=4, HID=4096, NE=8, DFF=4096, CAP=cap, G=G,
                DIB=2048, NB=16, KB=4, DTR=64,
                TT=min(512, Tc))



class Buf:
    __slots__ = ("name", "lw", "readers")

    def __init__(self, name):
        self.name = name
        self.lw = None
        self.readers = []


class Prog:
    ENGS = ("T", "V", "S", "P", "Y")
    LANES = ("Y0", "Y1", "Y2", "Y3", "P0", "P1", "P2", "P3", "C0")

    def __init__(self, nc):
        self.nc = nc
        self.es = ExitStack()
        self.stream = {e: [] for e in self.ENGS}
        allch = self.ENGS + self.LANES
        self.sems = {e: self.es.enter_context(nc.semaphore("ch" + e)) for e in allch}
        self.cnt = {e: 0 for e in allch}
        self.satisfied = {e: {} for e in self.ENGS}
        self.rr = {"Y": 0, "P": 0}
        self.nlane = 4

    def sb(self, name, shape, dtype, es=None):
        t = (es or self.es).enter_context(self.nc.sbuf_tensor(name, shape, dtype))
        return t, Buf(name)

    def dram(self, name, shape, dtype=F32):
        t = self.nc.dram_tensor(name, shape, dtype)
        return t.ap(), Buf(name)

    def psum(self, name):
        t = self.es.enter_context(self.nc.psum_tensor(name, [128, 512], F32))
        return t, Buf(name)

    def op(self, eng, fns, reads=(), writes=(), dma=False, coll=False, extra=(), quiet=False):
        if not isinstance(fns, (list, tuple)):
            fns = [fns]
        if eng != "T" and len(fns) > 1 and not (dma or coll or quiet):
            tok = None
            for f in fns:
                tok = self.op(eng, [f], reads=reads, writes=writes, extra=extra)
            return tok
        deps = {}

        def merge(d):
            if d is None:
                return
            ch, c = d
            if deps.get(ch, -1) < c:
                deps[ch] = c

        for b in reads:
            merge(b.lw)
        for b in writes:
            merge(b.lw)
            for r in b.readers:
                merge(r)
        for d in extra:
            merge(d)
        if coll:
            ch_self = "C0"
            inc = 1
            need = max(0, self.cnt[ch_self] - 2)
            if deps.get(ch_self, -1) < need:
                deps[ch_self] = need
        elif dma:
            ch_self = eng + str(self.rr[eng])
            self.rr[eng] = (self.rr[eng] + 1) % self.nlane
            inc = 16
            if deps.get(ch_self, -1) < self.cnt[ch_self]:
                deps[ch_self] = self.cnt[ch_self]
        else:
            ch_self = eng
            inc = 1
        n1 = self.cnt[ch_self] + inc
        waits = []
        sat = self.satisfied[eng]
        for ch in sorted(deps):
            c = deps[ch]
            if eng == "T" and ch == "T":
                continue
            if sat.get(ch, -1) >= c:
                continue
            waits.append((ch, c))
            sat[ch] = c
        if quiet:
            inc = 0
            n1 = self.cnt[ch_self]
        self.stream[eng].append((fns, waits, inc, ch_self,
                                 [b.name for b in reads], [b.name for b in writes]))
        self.cnt[ch_self] = n1
        tok = (ch_self, n1)
        for b in reads:
            b.readers.append(tok)
        for b in writes:
            b.lw = tok
            b.readers = []
        return tok

    def barrier(self):
        allch = self.ENGS + self.LANES
        for e in self.ENGS:
            sat = self.satisfied[e]
            for ch in allch:
                c = self.cnt[ch]
                if c > 0 and sat.get(ch, -1) < c:
                    self.stream[e].append((None, [(ch, c)], 0, None, [], []))
                    sat[ch] = c

    def emit(self):
        engmap = {"T": "tensor", "V": "vector", "S": "scalar", "P": "gpsimd", "Y": "sync"}
        with self.nc.Block() as block:
            for e in self.ENGS:
                items = self.stream[e]
                if not items:
                    continue

                def body(engobj, items=items, sems=self.sems):
                    for fns, waits, inc, ch_self, _rn, _wn in items:
                        for ch, c in waits:
                            engobj.wait_ge(sems[ch], c)
                        if fns is None:
                            continue
                        inst = None
                        for f in fns:
                            inst = f(engobj)
                        if inc:
                            inst.then_inc(sems[ch_self], inc)
                getattr(block, engmap[e])(body)



def build_program(cfg):
    nc = bass.Bass("TRN2", target_bir_lowering=False)
    Tc, D, INNER, NA = cfg["Tc"], cfg["D"], cfg["INNER"], cfg["NA"]
    TT = cfg["TT"]
    NCH = Tc // TT
    KD = D // 128
    KI = INNER // 128

    P = Prog(nc)

    def inp(name, shape, dtype=F32):
        return nc.dram_tensor(name, shape, dtype, kind="ExternalInput").ap(), Buf(name)

    def outp(name, shape, dtype=F32):
        return nc.dram_tensor(name, shape, dtype, kind="ExternalOutput").ap(), Buf(name)

    x_fm = inp("x_fm", [D, Tc])
    halo_x = inp("halo_x", [D, 2])
    oddmask = inp("oddmask", [128, 1])
    coreoh = inp("coreoh", [8, 1])
    n1w = inp("n1w", [128, KD])
    n2w = inp("n2w", [128, KD])
    w_ip = inp("w_ip", [D, 2 * INNER])
    b_ip = inp("b_ip", [128, 2 * KI])
    cwa = inp("cwa", [128, KI * 3])
    cba = inp("cba", [128, KI])
    w_dt = inp("w_dt", [INNER, NA])
    w_bp = inp("w_bp", [INNER, NA])
    w_cp = inp("w_cp", [INNER, NA])
    b_dtc = inp("b_dtc", [128, 3])
    w_s2i = inp("w_s2i", [NA, INNER])
    b_s2i = inp("b_s2i", [128, KI])
    dpar = inp("dpar", [128, KI])
    w_oa = inp("w_oa", [INNER, D])
    b_oa = inp("b_oa", [128, KD])
    w_ga = inp("w_ga", [D, 4])
    b_gab = inp("b_gab", [128, 4])
    e_w1 = inp("e_w1", [4 * D, cfg["HID"]])
    e_b1 = inp("e_b1", [128, 4 * cfg["HID"] // 128])
    e_w2 = inp("e_w2", [4 * cfg["HID"], D])
    e_b2 = inp("e_b2", [4, D])
    idn = inp("idn", [128, 128])
    sel4 = inp("sel4", [4, 512])
    w_rt = inp("w_rt", [D, cfg["NE"]])
    wf1 = inp("wf1", [D, cfg["DFF"]])
    wfg = inp("wfg", [D, cfg["DFF"]])
    wf2 = inp("wf2", [cfg["DFF"], D])
    ecol = inp("ecol", [128, 1])
    cbase = inp("cbase", [128, 1])
    qiw = inp("qiw", [128, 2])
    DIB, NB_, KB, DTR = cfg["DIB"], cfg["NB"], cfg["KB"], cfg["DTR"]
    DH = DIB // 2
    KH2 = DH // 128
    w_inb = inp("w_inb", [D, 2 * DH])
    cwb = inp("cwb", [128, KH2 * KB])
    cbb = inp("cbb", [128, KH2])
    w_xb = inp("w_xb", [DH, DTR + 2 * NB_])
    w_dtb = inp("w_dtb", [DTR, DH])
    bdtb = inp("bdtb", [128, KH2])
    arawt = inp("arawt", [128, KH2 * NB_])
    dparb = inp("dparb", [128, KH2])
    w_ob = inp("w_ob", [DH, D])
    evenm = inp("evenm", [128, 1])

    out_fm = outp("out_fm", [D, Tc])

    s_u = P.dram("s_u", [D, Tc])
    s_gate = P.dram("s_gate", [INNER, Tc])
    s_xm = P.dram("s_xm", [INNER, Tc])
    s_dtb = P.dram("s_dtb", [NA, 3 * Tc])
    s_u1 = P.dram("s_u1", [D, Tc])
    s_x1 = P.dram("s_x1", [D, Tc])
    s_r2b = P.dram("s_r2b", [D, Tc], BF16)
    s_hid = P.dram("s_hid", [cfg["HID"], Tc], BF16)
    NE, DFF, CAP, G = cfg["NE"], cfg["DFF"], cfg["CAP"], cfg["G"]
    CAPP = G * 128
    NJ = NCORES * Tc // 128
    JW = Tc // 128
    s_x2 = P.dram("s_x2", [D, Tc])
    f_vtok = P.dram("f_vtok", [Tc, D], BF16)
    f_hall = P.dram("f_hall", [NCORES * Tc, D], BF16)
    f_rtl = P.dram("f_rtl", [Tc, 4])
    f_rta = P.dram("f_rta", [NCORES * Tc, 4])
    f_t6 = P.dram("f_t6", [4 * NJ, 128])
    f_lin = P.dram("f_lin", [NE * 128 * G, 1])
    f_lout = P.dram("f_lout", [NE * 128 * G, 1])
    f_ei = P.dram("f_ei", [CAPP, D], BF16)
    f_eifm = P.dram("f_eifm", [D, CAPP], BF16)
    f_f1 = P.dram("f_f1", [DFF, CAPP], BF16)
    f_eo = P.dram("f_eo", [CAPP, D], BF16)
    f_eall = P.dram("f_eall", [NE * CAPP, D], BF16)
    TB2 = 2 * Tc
    s_x3 = P.dram("s_x3", [D, Tc])
    s_x4 = P.dram("s_x4", [D, Tc])
    b_lnb = P.dram("b_lnb", [D, Tc], BF16)
    b_lnall = P.dram("b_lnall", [2 * D, Tc], BF16)
    b_xc = P.dram("b_xc", [DH, TB2])
    b_z = P.dram("b_z", [DH, TB2])
    b_dbcp = P.dram("b_dbcp", [(DTR + 2 * NB_) * (TB2 // TT), TT])
    b_dbc = P.dram("b_dbc", [(DTR + 2 * NB_) * (TB2 // TT), TT])
    b_dlt = P.dram("b_dlt", [DH, TB2])
    b_y = P.dram("b_y", [DH, TB2])
    b_op = P.dram("b_op", [D * (TB2 // TT), TT], BF16)
    b_opr = P.dram("b_opr", [D * (TB2 // TT), TT], BF16)
    agh_in = P.dram("agh_in", [1, NA])
    agh_out = P.dram("agh_out", [8, NA])
    sin_rt = P.dram("sin_rt", [NA, 1])

    ones1, b_ones1 = P.sb("ones1", [1, 128], F32)
    onesc, b_onesc = P.sb("onesc", [128, 1], F32)
    P.op("V", lambda v: v.memset(ones1[:, :], 1.0), writes=[b_ones1])
    P.op("V", lambda v: v.memset(onesc[:, :], 1.0), writes=[b_onesc])
    c_eps, b_ceps = P.sb("c_eps", [128, 2], F32)
    P.op("V", lambda v: v.memset(c_eps[:, 0:1], LN_EPS), writes=[b_ceps])
    P.op("V", lambda v: v.memset(c_eps[:, 1:2], RMS_EPS), writes=[b_ceps])
    PSB = [P.psum("pb%d" % i) for i in range(8)]

    class Rot:
        def __init__(self, banks):
            self.banks = banks
            self.i = 0

        def __call__(self):
            b = self.banks[self.i % len(self.banks)]
            self.i += 1
            return b

    def dma_in(q, tile, tb, src_ap, src_buf, cast=False):
        eng = "P" if cast else q
        return P.op(eng, lambda e, t=tile, a=src_ap: e.dma_start(out=t, in_=a),
                    reads=[src_buf], writes=[tb], dma=True)

    def dma_out(q, dst_ap, dst_buf, tile, tb, cast=False):
        eng = "P" if cast else q
        return P.op(eng, lambda e, t=tile, a=dst_ap: e.dma_start(out=a, in_=t),
                    reads=[tb], writes=[dst_buf], dma=True)

    def mm_group(bank, bankbuf, mms, reads):
        fns = []
        n = len(mms)
        for i, (l, r) in enumerate(mms):
            fns.append(lambda t, l=l, r=r, s=(i == 0), e=(i == n - 1):
                       t.matmul(bank, l, r, start=s, stop=e))
        return P.op("T", fns, reads=list(reads), writes=[bankbuf])

    def fm3(tile_ap, kk):
        return tile_ap.rearrange("p (k t) -> p k t", k=kk)

    def dr3(dram_slice):
        return dram_slice.rearrange("(k p) t -> p k t", p=128)

    def bcast(rot_, row_ap, row_buf, dst_tile_ap, dst_buf, W, eng="V"):
        bk, bkb = rot_()
        mm_group(bk[0:128, 0:W], bkb, [(ones1[:, :], row_ap)], [b_ones1, row_buf])
        if eng == "S":
            return P.op("S", lambda s, bk=bk: s.activation(dst_tile_ap, bk[0:128, 0:W],
                                                           AF.Copy),
                        reads=[bkb], writes=[dst_buf])
        return P.op(eng, lambda v, bk=bk: v.tensor_copy(dst_tile_ap, bk[0:128, 0:W]),
                    reads=[bkb], writes=[dst_buf])

    rot = Rot(PSB)

    def f32res(name, src, shape, es=None):
        t, b = P.sb(name, shape, F32, es=es)
        P.op("Y", lambda e, t=t, s=src: e.dma_start(out=t[:, :], in_=s[0][:, :]),
             reads=[src[1]], writes=[b], dma=True)
        return t, b

    r_n1w = f32res("r_n1w", n1w, [128, KD])
    r_n2w = f32res("r_n2w", n2w, [128, KD])
    r_bip = f32res("r_bip", b_ip, [128, 2 * KI])
    r_cwa = f32res("r_cwa", cwa, [128, KI * 3])
    r_cba = f32res("r_cba", cba, [128, KI])
    r_bdtc = f32res("r_bdtc", b_dtc, [128, 3])
    r_bs2i = f32res("r_bs2i", b_s2i, [128, KI])
    r_dpar = f32res("r_dpar", dpar, [128, KI])
    r_boa = f32res("r_boa", b_oa, [128, KD])
    r_odd = f32res("r_odd", oddmask, [128, 1])
    r_coh, b_coh = P.sb("r_coh", [8, 1], F32)
    P.op("Y", lambda e: e.dma_start(out=r_coh[:, :], in_=coreoh[0][:, :]),
         reads=[coreoh[1]], writes=[b_coh], dma=True)
    rowt = P.sb("rowt", [1, 6 * TT], F32)
    r_idn, b_idn = P.sb("r_idn", [128, 128], F32)
    P.op("Y", lambda e: e.dma_start(out=r_idn[:, :], in_=idn[0][:, :]),
         reads=[idn[1]], writes=[b_idn], dma=True)
    ROW = rowt[0]
    ROWB = rowt[1]

    def rows(sl):
        return ROW[0:1, sl]

    esA = ExitStack()
    wip, b_wip = P.sb("wip", [128, KD * 2 * INNER], BF16, es=esA)
    P.op("P", lambda e: e.dma_start(out=wip[:, :].rearrange("p (k m) -> p k m", k=KD),
                                    in_=dr3(w_ip[0])),
         reads=[w_ip[1]], writes=[b_wip], dma=True)
    wdt, b_wdt = P.sb("wdt", [128, KI * NA], BF16, es=esA)
    wbp, b_wbp = P.sb("wbp", [128, KI * NA], BF16, es=esA)
    wcp, b_wcp = P.sb("wcp", [128, KI * NA], BF16, es=esA)
    for (t_, tb_, src_) in ((wdt, b_wdt, w_dt), (wbp, b_wbp, w_bp), (wcp, b_wcp, w_cp)):
        P.op("P", lambda e, t=t_, s=src_, _ki=KI: e.dma_start(
            out=t[:, :].rearrange("p (k m) -> p k m", k=_ki), in_=dr3(s[0])),
            reads=[src_[1]], writes=[tb_], dma=True)

    xt = [P.sb("xt%d" % i, [128, KD * TT], F32, es=esA) for i in range(2)]
    r1t = [P.sb("r1t%d" % i, [128, KD * TT], BF16, es=esA) for i in range(2)]
    sqt = P.sb("sqt", [128, TT], F32, es=esA)
    rstdb = P.sb("rstdb", [128, TT], F32, es=esA)
    ab = P.sb("ab", [128, TT], F32, es=esA)
    nrb = P.sb("nrb", [128, TT], F32, es=esA)
    xzt = [P.sb("xzt%d" % i, [128, TT], F32, es=esA) for i in range(3)]
    xmbt = [P.sb("xmbt%d" % i, [128, KI * TT], BF16, es=esA) for i in range(2)]
    cvt = [P.sb("cvt%d" % i, [128, TT], F32, es=esA) for i in range(2)]
    halo_keep = P.sb("halo_keep", [128, KI * 2], F32, es=esA)
    xmf = P.sb("xmf", [128, TT], F32, es=esA)

    def stats_pair(x_tile, xb, width, sq_scr, do_sum=True, nk=KD):
        ret_s = None
        if do_sum:
            bkS, bkSb = rot()
            mm_group(bkS[0:1, 0:width], bkSb,
                     [(onesc[:, :], x_tile[:, k * width:(k + 1) * width]) for k in range(nk)],
                     [b_onesc, xb])
            ret_s = (bkS, bkSb)
        bkQ, bkQb = rot()
        for k in range(nk):
            xk = x_tile[:, k * width:(k + 1) * width]
            P.op("S", lambda s, xk=xk, w=width: s.activation(sq_scr[0][:, 0:w], xk, AF.Square),
                 reads=[xb], writes=[sq_scr[1]])
            P.op("T", [lambda t, kk=k, bkQ=bkQ, w=width: t.matmul(
                bkQ[0:1, 0:w], onesc[:, :], sq_scr[0][:, 0:w],
                start=(kk == 0), stop=(kk == nk - 1))],
                reads=[b_onesc, sq_scr[1]], writes=[bkQb])
        return ret_s, (bkQ, bkQb)

    def chunkA(tc):
        W = 2 if tc == -1 else TT
        ib = tc % 2
        xT, xB = xt[ib]
        rT, rB = r1t[ib]
        if tc == -1:
            dma_in("Y", fm3(xT[:, 0:KD * 2], KD), xB, dr3(halo_x[0]), halo_x[1])
        else:
            dma_in("Y", fm3(xT[:, :], KD), xB,
                   dr3(x_fm[0][:, tc * TT:(tc + 1) * TT]), x_fm[1])
        (bkS, bkSb), (bkQ, bkQb) = stats_pair(xT, xB, W, sqt)
        mu = rows(slice(0, W))
        msq = rows(slice(TT, TT + W))
        var = rows(slice(2 * TT, 2 * TT + W))
        rstd = rows(slice(3 * TT, 3 * TT + W))
        arow = rows(slice(4 * TT, 4 * TT + W))
        P.op("V", lambda v: v.tensor_scalar(mu, bkS[0:1, 0:W], 1.0 / D, None, ALU.mult),
             reads=[bkSb], writes=[ROWB])
        P.op("V", lambda v: v.tensor_scalar(msq, bkQ[0:1, 0:W], 1.0 / D, None, ALU.mult),
             reads=[bkQb], writes=[ROWB])
        P.op("S", lambda s: s.activation(var, mu, AF.Square), reads=[ROWB], writes=[ROWB])
        P.op("V", lambda v: v.tensor_sub(var, msq, var), reads=[ROWB], writes=[ROWB])
        P.op("S", lambda s: s.activation(rstd, var, AF.Sqrt, bias=c_eps[0:1, 0:1]),
             reads=[ROWB, b_ceps], writes=[ROWB])
        P.op("V", lambda v: v.reciprocal(rstd, rstd), reads=[ROWB], writes=[ROWB])
        P.op("V", lambda v: v.tensor_mul(arow, mu, rstd), reads=[ROWB], writes=[ROWB])
        bcast(rot, rstd, ROWB, rstdb[0][:, 0:W], rstdb[1], W)
        bcast(rot, arow, ROWB, ab[0][:, 0:W], ab[1], W)
        for k in range(KD):
            xk = xT[:, k * W:(k + 1) * W]
            P.op("V", [lambda v, xk=xk: v.tensor_mul(xk, xk, rstdb[0][:, 0:W]),
                       lambda v, xk=xk: v.tensor_sub(xk, xk, ab[0][:, 0:W])],
                 reads=[xB, rstdb[1], ab[1]], writes=[xB])
        if tc >= 0:
            dma_out("Y", dr3(s_u[0][:, tc * TT:(tc + 1) * TT]),
                    s_u[1], fm3(xT[:, :], KD), xB)
        _, (bkR, bkRb) = stats_pair(xT, xB, W, sqt, do_sum=False)
        nr = rows(slice(5 * TT, 5 * TT + W))
        P.op("V", lambda v: v.tensor_scalar(nr, bkR[0:1, 0:W], 1.0 / D, None, ALU.mult),
             reads=[bkRb], writes=[ROWB])
        P.op("S", lambda s: s.activation(nr, nr, AF.Sqrt, bias=c_eps[0:1, 1:2]),
             reads=[ROWB, b_ceps], writes=[ROWB])
        P.op("V", lambda v: v.reciprocal(nr, nr), reads=[ROWB], writes=[ROWB])
        bcast(rot, nr, ROWB, nrb[0][:, 0:W], nrb[1], W)
        for k in range(KD):
            xk = xT[:, k * W:(k + 1) * W]
            rk = rT[:, k * W:(k + 1) * W]
            n1k = r_n1w[0][:, k:k + 1]
            P.op("V", [lambda v, xk=xk: v.tensor_mul(sqt[0][:, 0:W], xk, nrb[0][:, 0:W]),
                       lambda v, rk=rk, n1k=n1k: v.tensor_scalar_mul(rk, sqt[0][:, 0:W], n1k)],
                 reads=[xB, nrb[1], r_n1w[1]], writes=[rB, sqt[1]])
        xmbT, xmbB = xmbt[ib]
        MLIM = KI if tc == -1 else 2 * KI
        for m in range(MLIM):
            bk, bkb = rot()
            bko = bk[0:128, 0:W]
            mm_group(bko, bkb,
                     [(wip[:, (k * 2 * INNER) + m * 128:(k * 2 * INNER) + (m + 1) * 128],
                       rT[:, k * W:(k + 1) * W]) for k in range(KD)],
                     [b_wip, rB])
            zT, zB = xzt[m % 3]
            zo = zT[:, 0:W]
            bipc = r_bip[0][:, m:m + 1]
            P.op("V", lambda v, zo=zo, bko=bko, bipc=bipc: v.tensor_scalar(
                zo, bko, bipc, None, ALU.add), reads=[bkb, r_bip[1]], writes=[zB])
            if tc == -1:
                P.op("V", lambda v, zo=zo: v.tensor_scalar_mul(zo, zo, r_odd[0][:, :]),
                     reads=[zB, r_odd[1]], writes=[zB])
                hk = halo_keep[0][:, 2 * m:2 * m + 2]
                P.op("V", lambda v, hk=hk, zo=zo: v.tensor_copy(hk, zo),
                     reads=[zB], writes=[halo_keep[1]])
                continue
            if m >= KI:
                mg = m - KI
                dma_out("Y", s_gate[0][mg * 128:(mg + 1) * 128, tc * TT:(tc + 1) * TT],
                        s_gate[1], zo, zB)
                continue
            cT, cB = cvt[m % 2]
            co = cT[:, 0:W]
            hk = halo_keep[0][:, 2 * m:2 * m + 2]
            w0 = r_cwa[0][:, 3 * m:3 * m + 1]
            w1 = r_cwa[0][:, 3 * m + 1:3 * m + 2]
            w2 = r_cwa[0][:, 3 * m + 2:3 * m + 3]
            P.op("V", lambda v, co=co, zo=zo, w2=w2: v.tensor_scalar_mul(co, zo, w2),
                 reads=[zB, r_cwa[1]], writes=[cB])
            P.op("V", lambda v, co=co, zo=zo, w1=w1: v.scalar_tensor_tensor(
                co[:, 1:W], zo[:, 0:W - 1], w1, co[:, 1:W], ALU.mult, ALU.add),
                reads=[zB, r_cwa[1]], writes=[cB])
            P.op("V", lambda v, co=co, zo=zo, w0=w0: v.scalar_tensor_tensor(
                co[:, 2:W], zo[:, 0:W - 2], w0, co[:, 2:W], ALU.mult, ALU.add),
                reads=[zB, r_cwa[1]], writes=[cB])
            P.op("V", lambda v, co=co, hk=hk, w1=w1: v.scalar_tensor_tensor(
                co[:, 0:1], hk[:, 1:2], w1, co[:, 0:1], ALU.mult, ALU.add),
                reads=[halo_keep[1], r_cwa[1]], writes=[cB])
            P.op("V", lambda v, co=co, hk=hk, w0=w0: v.scalar_tensor_tensor(
                co[:, 0:2], hk[:, 0:2], w0, co[:, 0:2], ALU.mult, ALU.add),
                reads=[halo_keep[1], r_cwa[1]], writes=[cB])
            P.op("V", lambda v, hk=hk, zo=zo: v.tensor_copy(hk, zo[:, W - 2:W]),
                 reads=[zB], writes=[halo_keep[1]])
            cbk = r_cba[0][:, m:m + 1]
            P.op("S", lambda s, co=co, cbk=cbk: s.activation(
                xmf[0][:, 0:W], co, AF.Silu, bias=cbk), reads=[cB, r_cba[1]], writes=[xmf[1]])
            xmbk = xmbT[:, m * W:(m + 1) * W]
            P.op("V", lambda v, xmbk=xmbk: v.tensor_copy(xmbk, xmf[0][:, 0:W]),
                 reads=[xmf[1]], writes=[xmbB])
            dma_out("Y", s_xm[0][m * 128:(m + 1) * 128, tc * TT:(tc + 1) * TT],
                    s_xm[1], xmf[0][:, 0:W], xmf[1])
        if tc == -1:
            return
        for j, (wtile, wbuf, sig) in enumerate((
                (wdt, b_wdt, True), (wbp, b_wbp, False), (wcp, b_wcp, False))):
            bk, bkb = rot()
            bko = bk[0:64, 0:TT]
            mm_group(bko, bkb,
                     [(wtile[:, k * NA:(k + 1) * NA], xmbT[:, k * TT:(k + 1) * TT])
                      for k in range(KI)],
                     [wbuf, xmbB])
            zT, zB = xzt[j % 3]
            dst = zT[0:64, 0:TT]
            bc = r_bdtc[0][0:64, j:j + 1]
            if sig:
                P.op("S", lambda s, dst=dst, bko=bko, bc=bc: s.activation(
                    dst, bko, AF.Sigmoid, bias=bc), reads=[bkb, r_bdtc[1]], writes=[zB])
            else:
                P.op("V", lambda v, dst=dst, bko=bko, bc=bc: v.tensor_scalar(
                    dst, bko, bc, None, ALU.add), reads=[bkb, r_bdtc[1]], writes=[zB])
            dma_out("Y", s_dtb[0][0:NA, j * Tc + tc * TT: j * Tc + (tc + 1) * TT],
                    s_dtb[1], dst, zB)

    for tc in range(-1, NCH):
        chunkA(tc)

    P.barrier()
    esA.close()

    esC = ExitStack()
    hlo, b_hlo = P.sb("hlo", [64, Tc], F32, es=esC)
    ppr, b_ppr = P.sb("ppr", [64, Tc], F32, es=esC)
    sdt = [P.sb("sdt%d" % i, [64, 2 * TT], F32, es=esC) for i in range(2)]
    zrt = P.sb("zrt", [64, TT], F32, es=esC)
    sin, b_sin = P.sb("sin", [64, 1], F32, es=esC)
    P.op("V", lambda v: v.memset(zrt[0][:, :], 0.0), writes=[zrt[1]])

    for tc in range(NCH):
        ib = tc % 2
        dT, dB = sdt[ib]
        dma_in("Y", dT[:, 0:TT], dB, s_dtb[0][0:NA, tc * TT:(tc + 1) * TT], s_dtb[1])
        dma_in("Y", dT[:, TT:2 * TT], dB, s_dtb[0][0:NA, Tc + tc * TT: Tc + (tc + 1) * TT],
               s_dtb[1])
        P.op("V", lambda v, dT=dT: v.tensor_mul(dT[:, TT:2 * TT], dT[:, 0:TT], dT[:, TT:2 * TT]),
             reads=[dB], writes=[dB])
        P.op("V", lambda v, dT=dT: v.tensor_scalar(dT[:, 0:TT], dT[:, 0:TT], 1.0, -1.0,
                                                   ALU.subtract, ALU.mult),
             reads=[dB], writes=[dB])
        hcol = hlo[:, tc * TT:(tc + 1) * TT]
        pcol = ppr[:, tc * TT:(tc + 1) * TT]
        hprev = 0.0 if tc == 0 else hlo[:, tc * TT - 1: tc * TT]
        pprev = 1.0 if tc == 0 else ppr[:, tc * TT - 1: tc * TT]
        P.op("V", lambda v, dT=dT, hcol=hcol, hprev=hprev: v.tensor_tensor_scan(
            hcol, dT[:, 0:TT], dT[:, TT:2 * TT], hprev, ALU.mult, ALU.add),
            reads=[dB, b_hlo], writes=[b_hlo])
        P.op("V", lambda v, dT=dT, pcol=pcol, pprev=pprev: v.tensor_tensor_scan(
            pcol, dT[:, 0:TT], zrt[0][:, :], pprev, ALU.mult, ALU.add),
            reads=[dB, zrt[1], b_ppr], writes=[b_ppr])

    dma_out("P", agh_in[0][:, :], agh_in[1], hlo[:, Tc - 1:Tc], b_hlo)
    P.op("P", lambda g: g.collective_compute(
        "AllGather", ALU.bypass, replica_groups=[list(range(NCORES))],
        ins=[agh_in[0].opt()], outs=[agh_out[0].opt()]),
        reads=[agh_in[1]], writes=[agh_out[1]], coll=True)
    aght, b_aght = P.sb("aght", [8, NA], F32, es=esC)
    dma_in("Y", aght[:, :], b_aght, agh_out[0][:, :], agh_out[1])
    bkC, bkCb = rot()
    mm_group(bkC[0:1, 0:NA], bkCb, [(r_coh[:, :], aght[:, :])], [b_coh, b_aght])
    sinrow, b_sinrow = P.sb("sinrow", [1, NA], F32, es=esC)
    P.op("V", lambda v: v.tensor_copy(sinrow[:, :], bkC[0:1, 0:NA]), reads=[bkCb],
         writes=[b_sinrow])
    dma_out("P", sin_rt[0][:, :], sin_rt[1], sinrow[:, :], b_sinrow)
    dma_in("Y", sin[:, :], b_sin, sin_rt[0][:, :], sin_rt[1])
    P.op("V", lambda v: v.scalar_tensor_tensor(hlo[:, :], ppr[:, :], sin[:, :], hlo[:, :],
                                               ALU.mult, ALU.add),
         reads=[b_ppr, b_sin, b_hlo], writes=[b_hlo])

    woa, b_woa = P.sb("woa", [128, KI * D], BF16, es=esC)
    P.op("P", lambda e: e.dma_start(out=woa[:, :].rearrange("p (k m) -> p k m", k=KI),
                                    in_=dr3(w_oa[0])),
         reads=[w_oa[1]], writes=[b_woa], dma=True)
    ws2i, b_ws2i = P.sb("ws2i", [64, INNER], BF16, es=esC)
    P.op("P", lambda e: e.dma_start(out=ws2i[:, :], in_=w_s2i[0][:, :]),
         reads=[w_s2i[1]], writes=[b_ws2i], dma=True)

    sq2 = P.sb("sq2", [128, TT], F32, es=esC)
    ynb = P.sb("ynb", [64, TT], BF16, es=esC)
    rst2 = P.sb("rst2", [64, TT], F32, es=esC)
    ab2 = P.sb("ab2", [64, TT], F32, es=esC)
    yst = [P.sb("yst%d" % i, [64, TT], F32, es=esC) for i in range(2)]
    xm1 = [P.sb("xm1_%d" % i, [128, TT], F32, es=esC) for i in range(2)]
    gt1 = [P.sb("gt1_%d" % i, [128, TT], F32, es=esC) for i in range(2)]
    ud1 = [P.sb("ud1_%d" % i, [128, TT], F32, es=esC) for i in range(2)]
    gab = [P.sb("gab%d" % i, [128, KI * TT], BF16, es=esC) for i in range(2)]
    u1t = [P.sb("u1t%d" % i, [128, KD * TT], F32, es=esC) for i in range(2)]
    sg2 = P.sb("sg2", [128, TT], F32, es=esC)

    def chunkC(tc):
        ib = tc % 2
        ycol = slice(tc * TT, (tc + 1) * TT)
        yT, yB = yst[ib]
        dT, dB = sdt[ib]
        dma_in("Y", dT[:, 0:TT], dB, s_dtb[0][0:NA, 2 * Tc + tc * TT: 2 * Tc + (tc + 1) * TT],
               s_dtb[1])
        P.op("V", lambda v, yT=yT, dT=dT: v.tensor_mul(yT[:, :], dT[:, 0:TT], hlo[:, ycol]),
             reads=[dB, b_hlo], writes=[yB])
        bkS, bkSb = rot()
        bkQ, bkQb = rot()
        mm_group(bkS[0:1, 0:TT], bkSb, [(onesc[0:64, :], yT[:, :])], [b_onesc, yB])
        P.op("S", lambda s, yT=yT: s.activation(sq2[0][0:64, 0:TT], yT[:, :], AF.Square),
             reads=[yB], writes=[sq2[1]])
        mm_group(bkQ[0:1, 0:TT], bkQb, [(onesc[0:64, :], sq2[0][0:64, 0:TT])],
                 [b_onesc, sq2[1]])
        mu = rows(slice(0, TT))
        msq = rows(slice(TT, 2 * TT))
        var = rows(slice(2 * TT, 3 * TT))
        rstd = rows(slice(3 * TT, 4 * TT))
        arow = rows(slice(4 * TT, 5 * TT))
        P.op("V", lambda v: v.tensor_scalar(mu, bkS[0:1, 0:TT], 1.0 / NA, None, ALU.mult),
             reads=[bkSb], writes=[ROWB])
        P.op("V", lambda v: v.tensor_scalar(msq, bkQ[0:1, 0:TT], 1.0 / NA, None, ALU.mult),
             reads=[bkQb], writes=[ROWB])
        P.op("S", lambda s: s.activation(var, mu, AF.Square), reads=[ROWB], writes=[ROWB])
        P.op("V", lambda v: v.tensor_sub(var, msq, var), reads=[ROWB], writes=[ROWB])
        P.op("S", lambda s: s.activation(rstd, var, AF.Sqrt, bias=c_eps[0:1, 0:1]),
             reads=[ROWB, b_ceps], writes=[ROWB])
        P.op("V", lambda v: v.reciprocal(rstd, rstd), reads=[ROWB], writes=[ROWB])
        P.op("V", lambda v: v.tensor_mul(arow, mu, rstd), reads=[ROWB], writes=[ROWB])
        bkA, bkAb = rot()
        mm_group(bkA[0:128, 0:TT], bkAb, [(ones1[:, :], rstd)], [b_ones1, ROWB])
        P.op("V", lambda v, bkA=bkA: v.tensor_copy(rst2[0][:, :], bkA[0:64, 0:TT]),
             reads=[bkAb], writes=[rst2[1]])
        bkB, bkBb = rot()
        mm_group(bkB[0:128, 0:TT], bkBb, [(ones1[:, :], arow)], [b_ones1, ROWB])
        P.op("V", lambda v, bkB=bkB: v.tensor_copy(ab2[0][:, :], bkB[0:64, 0:TT]),
             reads=[bkBb], writes=[ab2[1]])
        P.op("V", [lambda v, yT=yT: v.tensor_mul(sq2[0][0:64, 0:TT], yT[:, :], rst2[0][:, :]),
                   lambda v: v.tensor_sub(ynb[0][:, :], sq2[0][0:64, 0:TT], ab2[0][:, :])],
             reads=[yB, rst2[1], ab2[1]], writes=[ynb[1], sq2[1]])
        gaT, gaB = gab[ib]
        for m in range(KI):
            bk, bkb = rot()
            bko = bk[0:128, 0:TT]
            mm_group(bko, bkb, [(ws2i[:, m * 128:(m + 1) * 128], ynb[0][:, :])],
                     [b_ws2i, ynb[1]])
            xmT, xmB = xm1[m % 2]
            gtT, gtB = gt1[m % 2]
            dma_in("Y", xmT[:, :], xmB,
                   s_xm[0][m * 128:(m + 1) * 128, ycol], s_xm[1])
            dma_in("Y", gtT[:, :], gtB,
                   s_gate[0][m * 128:(m + 1) * 128, ycol], s_gate[1])
            gak = gaT[:, m * TT:(m + 1) * TT]
            bsk = r_bs2i[0][:, m:m + 1]
            dpk = r_dpar[0][:, m:m + 1]
            P.op("S", lambda s, gtT=gtT: s.activation(sg2[0][:, 0:TT], gtT[:, :], AF.Sigmoid),
                 reads=[gtB], writes=[sg2[1]])
            P.op("V", [
                lambda v, bko=bko, bsk=bsk: v.tensor_scalar(
                    sq2[0][:, 0:TT], bko, bsk, None, ALU.add),
                lambda v, xmT=xmT, dpk=dpk: v.scalar_tensor_tensor(
                    sq2[0][:, 0:TT], xmT[:, :], dpk, sq2[0][:, 0:TT], ALU.mult, ALU.add),
                lambda v, gak=gak: v.tensor_mul(gak, sq2[0][:, 0:TT], sg2[0][:, 0:TT])],
                reads=[bkb, r_bs2i[1], r_dpar[1], xmB, sg2[1]], writes=[gaB, sq2[1]])
        u1T, u1B = u1t[ib]
        for m in range(KD):
            bk, bkb = rot()
            bko = bk[0:128, 0:TT]
            mm_group(bko, bkb,
                     [(woa[:, k * D + m * 128: k * D + (m + 1) * 128],
                       gaT[:, k * TT:(k + 1) * TT]) for k in range(KI)],
                     [b_woa, gaB])
            uT, uB = ud1[m % 2]
            dma_in("Y", uT[:, :], uB, s_u[0][m * 128:(m + 1) * 128, ycol], s_u[1])
            u1k = u1T[:, m * TT:(m + 1) * TT]
            bok = r_boa[0][:, m:m + 1]
            P.op("V", [
                lambda v, u1k=u1k, bko=bko, bok=bok: v.tensor_scalar(
                    u1k, bko, bok, None, ALU.add),
                lambda v, u1k=u1k, uT=uT: v.tensor_add(u1k, u1k, uT[:, :])],
                reads=[bkb, r_boa[1], uB], writes=[u1B])
        dma_out("Y", dr3(s_u1[0][:, ycol]), s_u1[1], fm3(u1T[:, :], KD), u1B)

    for tc in range(NCH):
        chunkC(tc)

    P.barrier()
    esC.close()

    HID = cfg["HID"]
    KH = HID // 128
    TD = max(64, TT // 2)
    NCD = Tc // TD
    esDW = ExitStack()
    r_eb1, b_eb1 = P.sb("r_eb1", [128, 4 * KH], F32, es=esDW)
    P.op("Y", lambda e: e.dma_start(out=r_eb1[:, :], in_=e_b1[0][:, :]),
         reads=[e_b1[1]], writes=[b_eb1], dma=True)
    r_sel4, b_sel4 = P.sb("r_sel4", [4, 512], F32, es=esDW)
    P.op("Y", lambda e: e.dma_start(out=r_sel4[:, :], in_=sel4[0][:, :]),
         reads=[sel4[1]], writes=[b_sel4], dma=True)
    wfm, b_wfm = P.sb("wfm", [4, Tc], F32, es=esDW)
    esD = ExitStack()
    wga, b_wga = P.sb("wga", [128, KD * 4], F32, es=esD)
    P.op("Y", lambda e: e.dma_start(out=wga[:, :].rearrange("p (k m) -> p k m", k=KD),
                                    in_=dr3(w_ga[0])),
         reads=[w_ga[1]], writes=[b_wga], dma=True)
    r_gab, b_gab_r = P.sb("r_gab", [128, 4], F32, es=esD)
    P.op("Y", lambda e: e.dma_start(out=r_gab[:, :], in_=b_gab[0][:, :]),
         reads=[b_gab[1]], writes=[b_gab_r], dma=True)
    r_eb2, b_eb2 = P.sb("r_eb2", [4, D], F32, es=esD)
    P.op("Y", lambda e: e.dma_start(out=r_eb2[:, :], in_=e_b2[0][:, :]),
         reads=[e_b2[1]], writes=[b_eb2], dma=True)

    u1d = [P.sb("u1d%d" % i, [128, KD * TD], F32, es=esD) for i in range(2)]
    xd = [P.sb("xd%d" % i, [128, KD * TD], F32, es=esD) for i in range(2)]
    r2d = [P.sb("r2d%d" % i, [128, KD * TD], BF16, es=esD) for i in range(2)]
    r2f = [P.sb("r2f%d" % i, [128, KD * TD], F32, es=esD) for i in range(2)]
    sqd = P.sb("sqd", [128, TD], F32, es=esD)
    nrbd = P.sb("nrbd", [128, TD], F32, es=esD)
    lgt = P.sb("lgt", [128, 8], F32, es=esD)
    m8t = P.sb("m8t", [128, 8], F32, es=esD)
    ohd = P.sb("ohd", [128, 4], F32, es=esD)
    wfd = P.sb("wfd", [128, 4], F32, es=esD)
    wrow = P.sb("wrow", [128, 2], F32, es=esD)

    P.op("V", lambda v: v.memset(lgt[0][:, 4:8], -1e30), writes=[lgt[1]])

    def stats_sq_only(x_tile, xb, width, nk):
        bkQ, bkQb = rot()
        for k in range(nk):
            xk = x_tile[:, k * width:(k + 1) * width]
            P.op("S", lambda s, xk=xk: s.activation(sqd[0][:, 0:width], xk, AF.Square),
                 reads=[xb], writes=[sqd[1]])
            P.op("T", [lambda t, kk=k: t.matmul(
                bkQ[0:1, 0:width], onesc[:, :], sqd[0][:, 0:width],
                start=(kk == 0), stop=(kk == nk - 1))],
                reads=[b_onesc, sqd[1]], writes=[bkQb])
        return bkQ, bkQb

    def chunkD0(tc):
        ib = tc % 2
        col = slice(tc * TD, (tc + 1) * TD)
        u1T, u1B = u1d[ib]
        xT, xB = xd[ib]
        r2T, r2B = r2d[ib]
        rfT, rfB = r2f[ib]
        dma_in("Y", fm3(u1T[:, :], KD), u1B, dr3(s_u1[0][:, col]), s_u1[1])
        dma_in("Y", fm3(xT[:, :], KD), xB, dr3(x_fm[0][:, col]), x_fm[1])
        bkR, bkRb = stats_sq_only(u1T, u1B, TD, KD)
        nr = rows(slice(0, TD))
        P.op("V", lambda v: v.tensor_scalar(nr, bkR[0:1, 0:TD], 1.0 / D, None, ALU.mult),
             reads=[bkRb], writes=[ROWB])
        P.op("S", lambda s: s.activation(nr, nr, AF.Sqrt, bias=c_eps[0:1, 1:2]),
             reads=[ROWB, b_ceps], writes=[ROWB])
        P.op("V", lambda v: v.reciprocal(nr, nr), reads=[ROWB], writes=[ROWB])
        bcast(rot, nr, ROWB, nrbd[0][:, 0:TD], nrbd[1], TD)
        for k in range(KD):
            uk = u1T[:, k * TD:(k + 1) * TD]
            rk = r2T[:, k * TD:(k + 1) * TD]
            rfk = rfT[:, k * TD:(k + 1) * TD]
            n2k = r_n2w[0][:, k:k + 1]
            P.op("V", [lambda v, uk=uk: v.tensor_mul(sqd[0][:, 0:TD], uk, nrbd[0][:, 0:TD]),
                       lambda v, rfk=rfk, n2k=n2k: v.tensor_scalar_mul(rfk, sqd[0][:, 0:TD], n2k),
                       lambda v, rk=rk, rfk=rfk: v.tensor_copy(rk, rfk)],
                 reads=[u1B, nrbd[1], r_n2w[1]], writes=[r2B, rfB, sqd[1]])
        dma_out("Y", dr3(s_r2b[0][:, col]), s_r2b[1], fm3(r2T[:, :], KD), r2B)
        TB = min(128, TD)
        for tb in range(TD // TB):
            bkL, bkLb = rot()
            mm_group(bkL[0:TB, 0:4], bkLb,
                     [(rfT[:, k * TD + tb * TB: k * TD + (tb + 1) * TB],
                       wga[:, k * 4:(k + 1) * 4]) for k in range(KD)],
                     [rfB, b_wga])
            P.op("V", lambda v, bkL=bkL: v.tensor_tensor(lgt[0][0:TB, 0:4], bkL[0:TB, 0:4],
                                                r_gab[0:TB, :], ALU.add),
                 reads=[bkLb, b_gab_r], writes=[lgt[1]])
            P.op("V", lambda v: v.max(out=m8t[0][0:TB, :], in_=lgt[0][0:TB, :]),
                 reads=[lgt[1]], writes=[m8t[1]])
            P.op("V", lambda v: v.tensor_sub(wrow[0][0:TB, 0:1], m8t[0][0:TB, 0:1], m8t[0][0:TB, 1:2]),
                 reads=[m8t[1]], writes=[wrow[1]])
            P.op("S", lambda s: s.activation(wrow[0][0:TB, 0:1], wrow[0][0:TB, 0:1], AF.Sigmoid),
                 reads=[wrow[1]], writes=[wrow[1]])
            P.op("V", lambda v: v.tensor_scalar(wrow[0][0:TB, 1:2], wrow[0][0:TB, 0:1], -1.0, 1.0,
                                                ALU.mult, ALU.add),
                 reads=[wrow[1]], writes=[wrow[1]])
            P.op("V", lambda v: v.tensor_scalar(ohd[0][0:TB, :], lgt[0][0:TB, 0:4],
                                                m8t[0][0:TB, 0:1], 1.0, ALU.is_equal, ALU.mult),
                 reads=[lgt[1], m8t[1]], writes=[ohd[1]])
            P.op("V", lambda v: v.tensor_scalar_mul(wfd[0][0:TB, :], ohd[0][0:TB, :], wrow[0][0:TB, 0:1]),
                 reads=[ohd[1], wrow[1]], writes=[wfd[1]])
            P.op("V", lambda v: v.tensor_scalar(ohd[0][0:TB, :], lgt[0][0:TB, 0:4],
                                                m8t[0][0:TB, 1:2], 1.0, ALU.is_equal, ALU.mult),
                 reads=[lgt[1], m8t[1]], writes=[ohd[1]])
            P.op("V", lambda v: v.scalar_tensor_tensor(wfd[0][0:TB, :], ohd[0][0:TB, :],
                                                       wrow[0][0:TB, 1:2], wfd[0][0:TB, :],
                                                       ALU.mult, ALU.add),
                 reads=[ohd[1], wrow[1]], writes=[wfd[1]])
            bkT, bkTb = rot()
            P.op("T", lambda t, bkT=bkT: t.transpose(bkT[0:4, 0:TB], wfd[0][0:TB, 0:4], r_idn[0:TB, 0:TB]),
                 reads=[wfd[1], b_idn], writes=[bkTb])
            wcolm = wfm[0:4, tc * TD + tb * TB: tc * TD + (tb + 1) * TB]
            P.op("V", lambda v, wcolm=wcolm, bkT=bkT: v.tensor_copy(wcolm, bkT[0:4, 0:TB]),
                 reads=[bkTb], writes=[b_wfm])
        for m in range(KD):
            bkA, bkAb = rot()
            mm_group(bkA[0:128, 0:TD], bkAb,
                     [(r_eb2[:, m * 128:(m + 1) * 128], wfm[0:4, col])],
                     [b_eb2, b_wfm])
            um = u1T[:, m * TD:(m + 1) * TD]
            xm_ = xT[:, m * TD:(m + 1) * TD]
            P.op("V", [lambda v, xm_=xm_, um=um: v.tensor_add(xm_, xm_, um),
                       lambda v, xm_=xm_, bkA=bkA: v.tensor_add(xm_, xm_, bkA[0:128, 0:TD])],
                 reads=[bkAb, u1B], writes=[xB])
        dma_out("Y", dr3(s_x1[0][:, col]), s_x1[1], fm3(xT[:, :], KD), xB)

    for tc in range(NCD):
        chunkD0(tc)

    P.barrier()
    esD.close()
    TDE = TT
    NCE = Tc // TDE
    esD2 = ExitStack()
    wei, b_wei = P.sb("wei", [128, KD * HID], BF16, es=esD2)
    hidt = [P.sb("hidt%d" % i, [128, KH * TDE], BF16, es=esD2) for i in range(2)]
    wbc = P.sb("wbc", [128, TDE], BF16, es=esD2)
    r2e = [P.sb("r2e%d" % i, [128, KD * TDE], BF16, es=esD2) for i in range(2)]
    acce1 = P.sb("acce", [128, KD * TDE], F32, es=esD2)

    def expertD(e):
        P.op("P", lambda g, e=e: g.dma_start(
            out=wei[:, :].rearrange("p (k m) -> p k m", k=KD),
            in_=dr3(e_w1[0][e * D:(e + 1) * D, :])),
            reads=[e_w1[1]], writes=[b_wei], dma=True)
        for tc in range(NCE):
            ib = tc % 2
            col = slice(tc * TDE, (tc + 1) * TDE)
            r2T, r2B = r2e[ib]
            hT, hB = hidt[ib]
            dma_in("Y", fm3(r2T[:, :], KD), r2B, dr3(s_r2b[0][:, col]), s_r2b[1])
            bkW, bkWb = rot()
            mm_group(bkW[0:128, 0:TDE], bkWb,
                     [(r_sel4[0:4, e * 128:(e + 1) * 128], wfm[0:4, col])],
                     [b_sel4, b_wfm])
            P.op("V", lambda v, bkW=bkW: v.tensor_copy(wbc[0][:, 0:TDE], bkW[0:128, 0:TDE]),
                 reads=[bkWb], writes=[wbc[1]])
            for m in range(KH):
                bk, bkb = rot()
                mm_group(bk[0:128, 0:TDE], bkb,
                         [(wei[:, k * HID + m * 128: k * HID + (m + 1) * 128],
                           r2T[:, k * TDE:(k + 1) * TDE]) for k in range(KD)],
                         [b_wei, r2B])
                hm = hT[:, m * TDE:(m + 1) * TDE]
                eb1c = r_eb1[:, e * KH + m: e * KH + m + 1]
                P.op("S", lambda s, hm=hm, eb1c=eb1c, bk=bk: s.activation(
                    hm, bk[0:128, 0:TDE], AF.Gelu, bias=eb1c),
                    reads=[bkb, b_eb1], writes=[hB])
                P.op("V", lambda v, hm=hm: v.tensor_mul(hm, hm, wbc[0][:, 0:TDE]),
                     reads=[wbc[1]], writes=[hB])
            dma_out("Y", dr3(s_hid[0][:, col]), s_hid[1], fm3(hT[:, :], KH), hB)
        P.op("P", lambda g, e=e: g.dma_start(
            out=wei[:, 0:KH * D].rearrange("p (k m) -> p k m", k=KH),
            in_=dr3(e_w2[0][e * HID:(e + 1) * HID, :])),
            reads=[e_w2[1]], writes=[b_wei], dma=True)
        for tc in range(NCE):
            ib = tc % 2
            col = slice(tc * TDE, (tc + 1) * TDE)
            hT, hB = hidt[ib]
            aT, aB = acce1
            dma_in("Y", fm3(hT[:, :], KH), hB, dr3(s_hid[0][:, col]), s_hid[1])
            dma_in("Y", fm3(aT[:, :], KD), aB, dr3(s_x1[0][:, col]), s_x1[1])
            for m in range(KD):
                bk, bkb = rot()
                mm_group(bk[0:128, 0:TDE], bkb,
                         [(wei[:, k * D + m * 128: k * D + (m + 1) * 128],
                           hT[:, k * TDE:(k + 1) * TDE]) for k in range(KH)],
                         [b_wei, hB])
                am = aT[:, m * TDE:(m + 1) * TDE]
                P.op("V", lambda v, am=am, bk=bk: v.tensor_add(am, am, bk[0:128, 0:TDE]),
                     reads=[bkb], writes=[aB])
            dma_out("Y", dr3(s_x1[0][:, col]), s_x1[1], fm3(aT[:, :], KD), aB)

    for e in range(4):
        expertD(e)

    P.barrier()
    esD2.close()
    esDW.close()

    r_ecol = f32res("r_ecol", ecol, [128, 1])
    r_cbase = f32res("r_cbase", cbase, [128, 1])
    r_qiw = f32res("r_qiw", qiw, [128, 2])
    NR_BIS = 28
    rg_lin = nc.gpsimd.alloc_register()
    rg_hall = nc.gpsimd.alloc_register()
    rg_lrow = nc.gpsimd.alloc_register()
    rg_eall = nc.gpsimd.alloc_register()
    rg_t6 = nc.gpsimd.alloc_register()
    P.op("P", [lambda g: g.reg_mov(rg_lin, NE * 128 * G - 1),
               lambda g: g.reg_mov(rg_hall, NCORES * Tc - 1),
               lambda g: g.reg_mov(rg_lrow, NE * G - 1),
               lambda g: g.reg_mov(rg_eall, NE * CAPP - 1),
               lambda g: g.reg_mov(rg_t6, 4 * NJ - 1)], quiet=True)
    TFC = 128 * max(d for d in (4, 2, 1) if G % d == 0)
    NCF = CAPP // TFC
    assert NCF * TFC == CAPP
    KF = DFF // 128

    def ffn_phase(x_in, x_out, tag):
        import os
        STOP = os.environ.get("FFN_STOP", "")

        def _dodump(dump):
            dmap = {"vtok": (f_vtok, BF16, Tc), "hall": (f_hall, BF16, NCORES * Tc),
                    "ei": (f_ei, BF16, CAPP), "eo": (f_eo, BF16, CAPP),
                    "eall": (f_eall, BF16, NE * CAPP)}
            src_t, sdt, nrows = dmap[dump]
            dbg = outp("dbg", [4096, D])
            esdbg = ExitStack()
            cpd = P.sb("cpdbg", [128, D], F32, es=esdbg)
            for i in range(32):
                dma_in("P", cpd[0][:, :], cpd[1],
                       src_t[0][(i * 128) % nrows:(i * 128) % nrows + 128, :], src_t[1])
                dma_out("Y", dbg[0][i * 128:(i + 1) * 128, :], dbg[1], cpd[0][:, :], cpd[1])
            P.barrier()
            esdbg.close()

        def stopper(point):
            if STOP == point:
                if os.environ.get("FFN_DUMP", "") and tag == "f1_":
                    _dodump(os.environ["FFN_DUMP"])
                cpt = P.sb(tag + "cpt" + point, [128, KD * 128], F32, es=esE)
                for tb2 in range(Tc // 128):
                    tc2 = slice(tb2 * 128, (tb2 + 1) * 128)
                    dma_in("Y", fm3(cpt[0][:, :], KD), cpt[1], dr3(x_in[0][:, tc2]), x_in[1])
                    dma_out("Y", dr3(x_out[0][:, tc2]), x_out[1], fm3(cpt[0][:, :], KD), cpt[1])
                P.barrier()
                return True
            return False
        esE = ExitStack()
        sbE = lambda n, sh, dt: P.sb(tag + n, sh, dt, es=esE)
        iop = sbE("iop", [128, 4], F32)
        idxs = sbE("idxs", [128, 4], I32)
        owns = sbE("owns", [128, 128], F32)
        ownc = sbE("ownc", [128, 128], F32)
        ownT = sbE("ownT", [128, 256], F32)
        ownci = sbE("ownci", [128, 128], I32)
        tokv = sbE("tokv", [128, JW], F32)
        lstf = sbE("lstf", [128, G], F32)
        lsti = sbE("lsti", [128, G], I32)
        lstrw = sbE("lstrw", [128, 128], F32)
        zb16 = sbE("zb16", [128, D], BF16)
        weif, b_weif = sbE("weif", [128, KD * DFF], BF16)
        gtile = [P.sb(tag + "gt%d" % i, [128, D], BF16, es=esE) for i in range(2)]
        es1 = ExitStack()
        sb = lambda n, sh, dt: P.sb(tag + n, sh, dt, es=es1)
        wrt, b_wrt = sb("wrt", [128, KD * NE], F32)
        P.op("Y", lambda e: e.dma_start(out=wrt[:, :].rearrange("p (k m) -> p k m", k=KD),
                                        in_=dr3(w_rt[0])),
             reads=[w_rt[1]], writes=[b_wrt], dma=True)
        xv = [sb("xv%d" % i, [128, KD * 128], F32) for i in range(2)]
        vtk = [sb("vtk%d" % i, [128, KD * 128], BF16) for i in range(2)]
        sqe = sb("sqe", [128, 128], F32)
        rse = sb("rse", [128, 128], F32)
        aae = sb("aae", [128, 128], F32)
        lgte = sb("lgte", [128, NE], F32)
        m8e = sb("m8e", [128, 8], F32)
        ohe = sb("ohe", [128, NE], F32)
        wre = sb("wre", [128, 8], F32)
        rte = sb("rte", [128, 4], F32)
        iota8 = sb("iota8", [128, 8], F32)
        P.op("P", lambda g: g.iota(iota8[0][:, :], [[1, 8]], channel_multiplier=0,
                                   allow_small_or_imprecise_dtypes=True), writes=[iota8[1]])
        NB = Tc // 128

        def e1_block(tb):
            ib = tb % 2
            xT, xB = xv[ib]
            vT, vB = vtk[ib]
            tcol = slice(tb * 128, (tb + 1) * 128)
            dma_in("Y", fm3(xT[:, :], KD), xB, dr3(x_in[0][:, tcol]), x_in[1])
            if os.environ.get("E1DBG", "") == "xraw":
                dma_out("Y", f_vtok[0][tcol, :], f_vtok[1], fm3(xT[:, :], KD), xB)
            (bkS, bkSb), (bkQ, bkQb) = stats_pair(xT, xB, 128, sqt_e)
            mu = rows(slice(0, 128))
            msq = rows(slice(TT, TT + 128))
            var = rows(slice(2 * TT, 2 * TT + 128))
            rstd = rows(slice(3 * TT, 3 * TT + 128))
            arow = rows(slice(4 * TT, 4 * TT + 128))
            P.op("V", lambda v: v.tensor_scalar(mu, bkS[0:1, 0:128], 1.0 / D, None, ALU.mult),
                 reads=[bkSb], writes=[ROWB])
            P.op("V", lambda v: v.tensor_scalar(msq, bkQ[0:1, 0:128], 1.0 / D, None, ALU.mult),
                 reads=[bkQb], writes=[ROWB])
            P.op("S", lambda s: s.activation(var, mu, AF.Square), reads=[ROWB], writes=[ROWB])
            P.op("V", lambda v: v.tensor_sub(var, msq, var), reads=[ROWB], writes=[ROWB])
            P.op("S", lambda s: s.activation(rstd, var, AF.Sqrt, bias=c_eps[0:1, 0:1]),
                 reads=[ROWB, b_ceps], writes=[ROWB])
            P.op("V", lambda v: v.reciprocal(rstd, rstd), reads=[ROWB], writes=[ROWB])
            P.op("V", lambda v: v.tensor_mul(arow, mu, rstd), reads=[ROWB], writes=[ROWB])
            if os.environ.get("E1DBG3") == "rows":
                for seg in range(6):
                    dma_out("Y", f_t6[0][tb * 8 + seg: tb * 8 + seg + 1, 0:128],
                            f_t6[1], ROW[0:1, seg * TT: seg * TT + 128], ROWB)
            bcast(rot, rstd, ROWB, rse[0][:, :], rse[1], 128)
            bcast(rot, arow, ROWB, aae[0][:, :], aae[1], 128)
            for k in range(KD):
                xk = xT[:, k * 128:(k + 1) * 128]
                vk = vT[:, k * 128:(k + 1) * 128]
                P.op("V", [lambda v, xk=xk: v.tensor_mul(xk, xk, rse[0][:, :]),
                           lambda v, xk=xk: v.tensor_sub(xk, xk, aae[0][:, :]),
                           lambda v, xk=xk, vk=vk: v.tensor_copy(vk, xk)],
                     reads=[xB, rse[1], aae[1]], writes=[xB, vB])
            for k in ([] if os.environ.get("E1DBG") == "notr" else range(KD)):
                xk = xT[:, k * 128:(k + 1) * 128]
                bkT_, bkTb_ = rot()
                P.op("T", lambda t, xk=xk, bkT_=bkT_: t.transpose(
                    bkT_[0:128, 0:128], xk, r_idn[:, :]), reads=[xB, b_idn], writes=[bkTb_])
                trk = vtr[0][:, k * 128:(k + 1) * 128]
                P.op("V", lambda v, trk=trk, bkT_=bkT_: v.tensor_copy(trk, bkT_[0:128, 0:128]),
                     reads=[bkTb_], writes=[vtr[1]])
            if os.environ.get("E1DBG", "") in ("lnfm", "notr"):
                dma_out("Y", f_vtok[0][tcol, :], f_vtok[1],
                        fm3(vT[:, :], KD), vB)
            else:
                dma_out("Y", f_vtok[0][tcol, :], f_vtok[1],
                        vtr[0][:, :].rearrange("p (k t) -> p k t", k=KD), vtr[1])
            bkL, bkLb = rot()
            mm_group(bkL[0:128, 0:NE], bkLb,
                     [(xT[:, k * 128:(k + 1) * 128], wrt[:, k * NE:(k + 1) * NE])
                      for k in range(KD)],
                     [xB, b_wrt])
            P.op("V", lambda v: v.tensor_copy(lgte[0][:, :], bkL[0:128, 0:NE]),
                 reads=[bkLb], writes=[lgte[1]])
            P.op("V", lambda v: v.max(out=m8e[0][:, :], in_=lgte[0][:, :]),
                 reads=[lgte[1]], writes=[m8e[1]])
            P.op("V", lambda v: v.tensor_scalar(ohe[0][:, :], lgte[0][:, :],
                                                m8e[0][:, 0:1], 1.0, ALU.is_equal, ALU.mult),
                 reads=[lgte[1], m8e[1]], writes=[ohe[1]])
            P.op("V", lambda v: v.tensor_mul(ohe[0][:, :], ohe[0][:, :], iota8[0][:, :]),
                 reads=[ohe[1], iota8[1]], writes=[ohe[1]])
            P.op("V", lambda v: v.reduce_sum(rte[0][:, 0:1], ohe[0][:, :], axis=AX.X),
                 reads=[ohe[1]], writes=[rte[1]])
            P.op("V", lambda v: v.tensor_scalar(ohe[0][:, :], lgte[0][:, :],
                                                m8e[0][:, 1:2], 1.0, ALU.is_equal, ALU.mult),
                 reads=[lgte[1], m8e[1]], writes=[ohe[1]])
            P.op("V", lambda v: v.tensor_mul(ohe[0][:, :], ohe[0][:, :], iota8[0][:, :]),
                 reads=[ohe[1], iota8[1]], writes=[ohe[1]])
            P.op("V", lambda v: v.reduce_sum(rte[0][:, 1:2], ohe[0][:, :], axis=AX.X),
                 reads=[ohe[1]], writes=[rte[1]])
            P.op("V", lambda v: v.tensor_sub(wre[0][:, 0:1], m8e[0][:, 0:1], m8e[0][:, 1:2]),
                 reads=[m8e[1]], writes=[wre[1]])
            P.op("S", lambda s: s.activation(wre[0][:, 1:2], wre[0][:, 0:1], AF.Sigmoid),
                 reads=[wre[1]], writes=[wre[1]])
            P.op("S", lambda s: s.activation(wre[0][:, 2:3], wre[0][:, 0:1], AF.Sigmoid,
                                             scale=-1.0),
                 reads=[wre[1]], writes=[wre[1]])
            P.op("V", lambda v: v.tensor_add(wre[0][:, 3:4], wre[0][:, 1:2], wre[0][:, 2:3]),
                 reads=[wre[1]], writes=[wre[1]])
            P.op("V", lambda v: v.tensor_scalar(wre[0][:, 3:4], wre[0][:, 3:4], MOE_EPS, None,
                                                ALU.max), reads=[wre[1]], writes=[wre[1]])
            P.op("V", lambda v: v.reciprocal(wre[0][:, 3:4], wre[0][:, 3:4]),
                 reads=[wre[1]], writes=[wre[1]])
            P.op("V", lambda v: v.tensor_mul(rte[0][:, 2:3], wre[0][:, 1:2], wre[0][:, 3:4]),
                 reads=[wre[1]], writes=[rte[1]])
            P.op("V", lambda v: v.tensor_mul(rte[0][:, 3:4], wre[0][:, 2:3], wre[0][:, 3:4]),
                 reads=[wre[1]], writes=[rte[1]])
            dma_out("Y", f_rtl[0][tcol, :], f_rtl[1], rte[0][:, :], rte[1])
        for tb in range(int(os.environ.get("E1NB", NB))):
            e1_block(tb)

        P.barrier()
        es1.close()
        es3 = ExitStack()
        sb = lambda n, sh, dt: P.sb(tag + n, sh, dt, es=es3)
        if stopper("e1"):
            return
        P.op("P", lambda g: g.collective_compute(
            "AllGather", ALU.bypass, replica_groups=[list(range(NCORES))],
            ins=[f_vtok[0].opt()], outs=[f_hall[0].opt()]),
            reads=[f_vtok[1]], writes=[f_hall[1]], coll=True)
        P.op("P", lambda g: g.collective_compute(
            "AllGather", ALU.bypass, replica_groups=[list(range(NCORES))],
            ins=[f_rtl[0].opt()], outs=[f_rta[0].opt()]),
            reads=[f_rtl[1]], writes=[f_rta[1]], coll=True)
        if stopper("e2"):
            return
        rta = sb("rta", [128, NJ * 4], F32)
        dma_in("Y", rta[0][:, :].rearrange("p (j c) -> p j c", c=4), rta[1],
               f_rta[0].rearrange("(j p) c -> p j c", p=128), f_rta[1])
        rta3 = rta[0][:, :].rearrange("p (j c) -> p j c", c=4)
        eA1, eA2 = rta3[:, :, 0], rta3[:, :, 1]
        wA1, wA2 = rta3[:, :, 2], rta3[:, :, 3]
        wee = [sb("wee%d" % e, [128, 2 * NJ], F32) for e in range(NE)]
        msku = sb("msku", [128, 2 * NJ], U8)
        neg1 = sb("neg1", [128, NJ], F32)
        onesb = sb("onesb", [128, 2 * NJ], F32)
        P.op("V", lambda v: v.memset(neg1[0][:, :], -1.0), writes=[neg1[1]])
        P.op("V", lambda v: v.memset(onesb[0][:, :], 1.0), writes=[onesb[1]])
        for e in range(NE):
            weT, weB = wee[e]
            for ki, (eA, wA) in enumerate(((eA1, wA1), (eA2, wA2))):
                half = weT[:, ki * NJ:(ki + 1) * NJ]
                mh = msku[0][:, ki * NJ:(ki + 1) * NJ]
                P.op("V", lambda v, eA=eA, e=e, mh=mh: v.tensor_scalar(
                    mh, eA, float(e), 1.0, ALU.is_equal, ALU.mult),
                    reads=[rta[1]], writes=[msku[1]])
                P.op("V", lambda v, half=half, mh=mh, wA=wA: v.select(
                    half, mh, wA, neg1[0][:, :], add_drain=True),
                    reads=[msku[1], rta[1], neg1[1]], writes=[weB])
        if stopper("e3b"):
            return
        lohi = sb("lohi", [1, 24], F32)
        caprow = sb("caprow", [1, 8], F32)
        gmask = sb("gmask", [1, 8], U8)
        cc8 = sb("cc8", [128, 8], F32)
        inds = sb("inds", [128, 2 * NJ], F32)
        thbc = sb("thbc", [128, 8], F32)
        LO = lohi[0][0:1, 0:8]
        HI = lohi[0][0:1, 8:16]
        MID = lohi[0][0:1, 16:24]
        P.op("V", lambda v: v.memset(LO, -1.0), writes=[lohi[1]])
        P.op("V", lambda v: v.memset(HI, 1.0), writes=[lohi[1]])
        P.op("V", lambda v: v.memset(MID, 0.0), writes=[lohi[1]])
        P.op("V", lambda v: v.memset(caprow[0][:, :], float(CAP)), writes=[caprow[1]])
        for r in range(NR_BIS):
            bkM, bkMb = rot()
            mm_group(bkM[0:128, 0:8], bkMb, [(ones1[:, :], MID)], [b_ones1, lohi[1]])
            P.op("V", lambda v, bkM=bkM: v.tensor_copy(thbc[0][:, :], bkM[0:128, 0:8]),
                 reads=[bkMb], writes=[thbc[1]])
            for e in range(NE):
                weT, weB = wee[e]
                P.op("V", lambda v, weT=weT, e=e: v.tensor_scalar(
                    inds[0][:, :], weT[:, :], thbc[0][:, e:e + 1], 1.0, ALU.is_gt, ALU.mult),
                    reads=[weB, thbc[1]], writes=[inds[1]])
                P.op("V", lambda v, e=e: v.reduce_sum(cc8[0][:, e:e + 1], inds[0][:, :],
                                                      axis=AX.X),
                     reads=[inds[1]], writes=[cc8[1]])
            bkN, bkNb = rot()
            mm_group(bkN[0:1, 0:8], bkNb, [(onesc[:, :], cc8[0][:, :])], [b_onesc, cc8[1]])
            P.op("V", lambda v, bkN=bkN: v.tensor_tensor(gmask[0][:, :], bkN[0:1, 0:8],
                                                         caprow[0][:, :], ALU.is_gt),
                 reads=[bkNb, caprow[1]], writes=[gmask[1]])
            P.op("V", lambda v: v.select(LO, gmask[0][:, :], MID, LO, add_drain=True),
                 reads=[gmask[1], lohi[1]], writes=[lohi[1]])
            P.op("V", lambda v: v.select(HI, gmask[0][:, :], HI, MID, add_drain=True),
                 reads=[gmask[1], lohi[1]], writes=[lohi[1]])
            P.op("V", lambda v: v.tensor_add(MID, LO, HI), reads=[lohi[1]], writes=[lohi[1]])
            P.op("V", lambda v: v.tensor_scalar(MID, MID, 0.5, None, ALU.mult),
                 reads=[lohi[1]], writes=[lohi[1]])
        bkM, bkMb = rot()
        mm_group(bkM[0:128, 0:8], bkMb, [(ones1[:, :], HI)], [b_ones1, lohi[1]])
        P.op("V", lambda v, bkM=bkM: v.tensor_copy(thbc[0][:, :], bkM[0:128, 0:8]),
             reads=[bkMb], writes=[thbc[1]])
        tha = sb("tha", [128, 2 * NJ], F32)
        kw = sb("kw", [128, 2 * NJ], F32)
        tws = sb("tws", [128, NJ], F32)
        cadr = sb("cadr", [128, 2 * NJ], F32)
        sele = sb("sele", [128, 2 * NJ], F32)
        cume = sb("cume", [128, 2 * NJ], F32)
        scr1 = sb("scr1", [128, 2 * NJ], F32)
        rowc = sb("rowc", [1, 3 * 128], F32)
        colx = sb("colx", [128, 2], F32)
        P.op("V", lambda v: v.memset(tha[0][:, :], 0.0), writes=[tha[1]])
        for e in range(NE):
            for ki, eA in enumerate((eA1, eA2)):
                hh = tha[0][:, ki * NJ:(ki + 1) * NJ]
                P.op("V", lambda v, eA=eA, e=e: v.tensor_scalar(
                    scr1[0][:, 0:NJ], eA, float(e), 1.0, ALU.is_equal, ALU.mult),
                    reads=[rta[1]], writes=[scr1[1]])
                P.op("V", lambda v, hh=hh, e=e: v.scalar_tensor_tensor(
                    hh, scr1[0][:, 0:NJ], thbc[0][:, e:e + 1], hh, ALU.mult, ALU.add),
                    reads=[scr1[1], thbc[1]], writes=[tha[1]])
        for ki, wA in enumerate((wA1, wA2)):
            hh = kw[0][:, ki * NJ:(ki + 1) * NJ]
            th = tha[0][:, ki * NJ:(ki + 1) * NJ]
            P.op("V", lambda v, hh=hh, wA=wA, th=th: v.tensor_tensor(hh, wA, th, ALU.is_gt),
                 reads=[rta[1], tha[1]], writes=[kw[1]])
            P.op("V", lambda v, hh=hh, wA=wA: v.tensor_mul(hh, hh, wA),
                 reads=[rta[1]], writes=[kw[1]])
        P.op("V", lambda v: v.tensor_add(tws[0][:, :], kw[0][:, 0:NJ], kw[0][:, NJ:2 * NJ]),
             reads=[kw[1]], writes=[tws[1]])
        P.op("V", lambda v: v.tensor_scalar(tws[0][:, :], tws[0][:, :], MOE_EPS, None, ALU.max),
             reads=[tws[1]], writes=[tws[1]])
        P.op("V", lambda v: v.reciprocal(tws[0][:, :], tws[0][:, :]),
             reads=[tws[1]], writes=[tws[1]])
        for ki in range(2):
            hh = kw[0][:, ki * NJ:(ki + 1) * NJ]
            P.op("V", lambda v, hh=hh: v.tensor_mul(hh, hh, tws[0][:, :]),
                 reads=[tws[1]], writes=[kw[1]])
        P.op("V", lambda v: v.memset(cadr[0][:, :], float(NE * CAPP)), writes=[cadr[1]])
        for e in range(NE):
            weT, weB = wee[e]
            P.op("V", lambda v, weT=weT, e=e: v.tensor_scalar(
                sele[0][:, :], weT[:, :], thbc[0][:, e:e + 1], 1.0, ALU.is_gt, ALU.mult),
                reads=[weB, thbc[1]], writes=[sele[1]])
            P.op("V", lambda v: v.tensor_tensor_scan(
                cume[0][:, :], onesb[0][:, :], sele[0][:, :], 0.0, ALU.mult, ALU.add),
                reads=[onesb[1], sele[1]], writes=[cume[1]])
            bkX, bkXb = rot()
            P.op("T", lambda t, bkX=bkX: t.transpose(
                bkX[0:1, 0:128], cume[0][:, 2 * NJ - 1:2 * NJ], r_idn[:, :]),
                reads=[cume[1], b_idn], writes=[bkXb])
            R1 = rowc[0][0:1, 0:128]
            R2 = rowc[0][0:1, 128:256]
            P.op("V", lambda v, bkX=bkX: v.tensor_copy(R1, bkX[0:1, 0:128]),
                 reads=[bkXb], writes=[rowc[1]])
            P.op("V", lambda v: v.tensor_tensor_scan(R2, ones1[:, :], R1, 0.0,
                                                     ALU.mult, ALU.add),
                 reads=[rowc[1], b_ones1], writes=[rowc[1]])
            P.op("V", lambda v: v.tensor_sub(R2, R2, R1), reads=[rowc[1]], writes=[rowc[1]])
            bkY, bkYb = rot()
            P.op("T", lambda t, bkY=bkY: t.transpose(bkY[0:128, 0:1], R2, r_idn[0:1, 0:1]),
                 reads=[rowc[1], b_idn], writes=[bkYb])
            P.op("V", lambda v, bkY=bkY: v.tensor_copy(colx[0][:, 0:1], bkY[0:128, 0:1]),
                 reads=[bkYb], writes=[colx[1]])
            P.op("V", lambda v: v.tensor_sub(cume[0][:, :], cume[0][:, :], sele[0][:, :]),
                 reads=[sele[1]], writes=[cume[1]])
            P.op("V", lambda v: v.tensor_scalar(cume[0][:, :], cume[0][:, :],
                                                colx[0][:, 0:1], None, ALU.add),
                 reads=[colx[1]], writes=[cume[1]])
            P.op("V", lambda v, e=e: v.tensor_scalar(
                inds[0][:, :], cume[0][:, :], float(e * CAPP), None, ALU.add),
                reads=[cume[1]], writes=[inds[1]])
            P.op("V", lambda v: v.tensor_scalar(msku[0][:, :], sele[0][:, :], 0.5, None,
                                                ALU.is_gt), reads=[sele[1]], writes=[msku[1]])
            P.op("V", lambda v: v.select(cadr[0][:, :], msku[0][:, :], inds[0][:, :],
                                         cadr[0][:, :], add_drain=True),
                 reads=[msku[1], inds[1]], writes=[cadr[1]])
        NJB = (NJ + 127) // 128
        quants = [(cadr, 0, 0), (cadr, NJ, 1), (kw, 0, 2), (kw, NJ, 3)]
        ttr = sb("ttr", [128, 128], F32)
        for (tile, off, q) in quants:
            for jb in range(NJB):
                w = min(128, NJ - jb * 128)
                srcs = tile[0][:, off + jb * 128: off + jb * 128 + w]
                bkT_, bkTb_ = rot()
                P.op("T", lambda t, srcs=srcs, w=w, bkT_=bkT_: t.transpose(
                    bkT_[0:w, 0:128], srcs, r_idn[:, :]), reads=[tile[1], b_idn],
                    writes=[bkTb_])
                P.op("V", lambda v, w=w, bkT_=bkT_: v.tensor_copy(
                    ttr[0][0:w, :], bkT_[0:w, 0:128]), reads=[bkTb_], writes=[ttr[1]])
                dma_out("Y", f_t6[0][q * NJ + jb * 128: q * NJ + jb * 128 + w, :],
                        f_t6[1], ttr[0][0:w, :], ttr[1])
        P.op("P", lambda g: g.iota(iop[0][:, 0:1], [[0, 1]], channel_multiplier=1,
                                   allow_small_or_imprecise_dtypes=True), writes=[iop[1]])
        P.op("V", lambda v: v.tensor_copy(iop[0][:, 1:2], r_qiw[0][:, 0:1]),
             reads=[r_qiw[1]], writes=[iop[1]])
        P.op("V", lambda v: v.tensor_scalar(iop[0][:, 3:4], r_qiw[0][:, 1:2], float(NJ), None,
                                            ALU.mult), reads=[r_qiw[1]], writes=[iop[1]])
        P.op("V", lambda v: v.scalar_tensor_tensor(
            iop[0][:, 3:4], r_ecol[0][:, :], float(JW), iop[0][:, 3:4], ALU.mult, ALU.add),
            reads=[r_ecol[1], iop[1]], writes=[iop[1]])
        P.op("V", lambda v: v.tensor_add(iop[0][:, 3:4], iop[0][:, 3:4], iop[0][:, 1:2]),
             reads=[iop[1]], writes=[iop[1]])
        dma_in("P", idxs[0][:, 0:1], idxs[1], iop[0][:, 3:4], iop[1])
        P.op("P", lambda g: g.indirect_dma_start(
            out=owns[0][:, :], out_offset=None, in_=f_t6[0][:, :],
            in_offset=IndirectOffsetOnAxis(ap=idxs[0][:, 0:1], axis=0),
            bounds_check=rg_t6, oob_is_err=False),
            reads=[f_t6[1], idxs[1]], writes=[owns[1]], dma=True)
        bkT_, bkTb_ = rot()
        P.op("T", lambda t, bkT_=bkT_: t.transpose(
            bkT_[0:128, 0:128], owns[0][:, :], r_idn[:, :]),
            reads=[owns[1], b_idn], writes=[bkTb_])
        P.op("V", lambda v, bkT_=bkT_: v.tensor_copy(
            ownT[0][:, 0:128], bkT_[0:128, 0:128]),
            reads=[bkTb_], writes=[ownT[1]])
        dma_in("P", ownci[0][:, 0:2 * JW], ownci[1], ownT[0][:, 0:2 * JW], ownT[1])
        if stopper("e3"):
            return
        big9 = sb("big9", [128, G], F32)
        P.op("V", lambda v: v.memset(big9[0][:, :], float(NCORES * Tc)), writes=[big9[1]])
        for e in range(NE):
            dma_out("Y", f_lin[0][e * 128 * G:(e + 1) * 128 * G, :].rearrange(
                "(p g) one -> p (g one)", p=128), f_lin[1], big9[0][:, :], big9[1])
        P.op("P", lambda g: g.iota(tokv[0][:, :], [[128, JW]], channel_multiplier=1,
                                   allow_small_or_imprecise_dtypes=True), writes=[tokv[1]])
        P.op("V", lambda v: v.tensor_scalar(tokv[0][:, :], tokv[0][:, :],
                                            r_cbase[0][:, :], None, ALU.add),
             reads=[r_cbase[1]], writes=[tokv[1]])
        for ki in range(2):
            for i in range(JW):
                icol = ownci[0][:, ki * JW + i: ki * JW + i + 1]
                vcol = tokv[0][:, i:i + 1]
                P.op("P", lambda g, icol=icol, vcol=vcol: g.indirect_dma_start(
                    out=f_lin[0][:, :], out_offset=IndirectOffsetOnAxis(ap=icol, axis=0),
                    in_=vcol, in_offset=None,
                    bounds_check=rg_lin, oob_is_err=False),
                    reads=[ownci[1], tokv[1], f_lin[1]], writes=[f_lin[1]], dma=True)
        P.op("P", lambda g: g.collective_compute(
            "AllReduce", ALU.min, replica_groups=[list(range(NCORES))],
            ins=[f_lin[0].opt()], outs=[f_lout[0].opt()]),
            reads=[f_lin[1]], writes=[f_lout[1]], coll=True)
        if stopper("e4a"):
            return
        P.op("V", lambda v: v.tensor_scalar(iop[0][:, 3:4], iop[0][:, 0:1],
                                            float(G - 1), None, ALU.min),
             reads=[iop[1]], writes=[iop[1]])
        P.op("V", lambda v: v.scalar_tensor_tensor(
            iop[0][:, 3:4], r_ecol[0][:, :], float(G), iop[0][:, 3:4], ALU.mult, ALU.add),
            reads=[r_ecol[1], iop[1]], writes=[iop[1]])
        dma_in("P", idxs[0][:, 2:3], idxs[1], iop[0][:, 3:4], iop[1])
        P.op("P", lambda g: g.indirect_dma_start(
            out=lstrw[0][:, :], out_offset=None,
            in_=f_lout[0].rearrange("(r t) one -> r (t one)", t=128),
            in_offset=IndirectOffsetOnAxis(ap=idxs[0][:, 2:3], axis=0),
            bounds_check=rg_lrow, oob_is_err=False),
            reads=[f_lout[1], idxs[1]], writes=[lstrw[1]], dma=True)
        bkT_, bkTb_ = rot()
        P.op("T", lambda t, bkT_=bkT_: t.transpose(
            bkT_[0:128, 0:128], lstrw[0][:, :], r_idn[:, :]),
            reads=[lstrw[1], b_idn], writes=[bkTb_])
        P.op("V", lambda v, bkT_=bkT_: v.tensor_copy(lstf[0][:, :], bkT_[0:128, 0:G]),
             reads=[bkTb_], writes=[lstf[1]])
        dma_in("P", lsti[0][:, :], lsti[1], lstf[0][:, :], lstf[1])
        P.op("V", lambda v: v.memset(zb16[0][:, :], 0.0), writes=[zb16[1]])
        for g_ in range(G):
            dma_out("Y", f_ei[0][g_ * 128:(g_ + 1) * 128, :], f_ei[1],
                    zb16[0][:, :], zb16[1])
        P.op("V", lambda v: v.memset(gtile[0][0][:, :], 0.0), writes=[gtile[0][1]])
        P.op("V", lambda v: v.memset(gtile[1][0][:, :], 0.0), writes=[gtile[1][1]])
        for g_ in range(G):
            icol = lsti[0][:, g_:g_ + 1]
            gT, gB = gtile[g_ % 2]
            P.op("P", lambda g, icol=icol, gT=gT: g.indirect_dma_start(
                out=gT[:, :], out_offset=None,
                in_=f_hall[0][:, :],
                in_offset=IndirectOffsetOnAxis(ap=icol, axis=0),
                bounds_check=rg_hall, oob_is_err=False),
                reads=[f_hall[1], lsti[1]], writes=[gB], dma=True)
            dma_out("Y", f_ei[0][g_ * 128:(g_ + 1) * 128, :], f_ei[1], gT[:, :], gB)
        if stopper("e4"):
            return
        P.barrier()
        es3.close()
        es5 = ExitStack()
        sb = lambda n, sh, dt: P.sb(tag + n, sh, dt, es=es5)
        eif = [sb("eif%d" % i, [128, KD * TFC], BF16) for i in range(2)]
        s1o = sb("s1o", [128, KF * TFC], BF16)
        P.op("P", lambda g: g.dma_start(
            out=weif[:, :].rearrange("p (k m) -> p k m", k=KD), in_=dr3(wf1[0])),
            reads=[wf1[1]], writes=[b_weif], dma=True)
        for fc in range(NCF):
            ib = fc % 2
            eT, eB = eif[ib]
            for k in range(KD):
                P.op("Y", lambda e, eT=eT, k=k, fc=fc: e.dma_start_transpose(
                    eT[:, k * TFC:(k + 1) * TFC],
                    f_ei[0][fc * TFC:(fc + 1) * TFC, k * 128:(k + 1) * 128]),
                    reads=[f_ei[1]], writes=[eB], dma=True)
            dma_out("Y", dr3(f_eifm[0][:, fc * TFC:(fc + 1) * TFC]), f_eifm[1],
                    fm3(eT[:, :], KD), eB)
            for m in range(KF):
                bk, bkb = rot()
                mm_group(bk[0:128, 0:TFC], bkb,
                         [(weif[:, k * DFF + m * 128: k * DFF + (m + 1) * 128],
                           eT[:, k * TFC:(k + 1) * TFC]) for k in range(KD)],
                         [b_weif, eB])
                sm = s1o[0][:, m * TFC:(m + 1) * TFC]
                P.op("V", lambda v, sm=sm, bk=bk: v.tensor_copy(sm, bk[0:128, 0:TFC]),
                     reads=[bkb], writes=[s1o[1]])
            dma_out("Y", dr3(f_f1[0][:, fc * TFC:(fc + 1) * TFC]), f_f1[1],
                    fm3(s1o[0][:, :], KF), s1o[1])
        if stopper("e5a"):
            return
        P.op("P", lambda g: g.dma_start(
            out=weif[:, :].rearrange("p (k m) -> p k m", k=KD), in_=dr3(wfg[0])),
            reads=[wfg[1]], writes=[b_weif], dma=True)
        pre_t = sb("pre_t", [128, KF * TFC], BF16)
        for fc in range(NCF):
            ib = fc % 2
            eT, eB = eif[ib]
            dma_in("Y", fm3(eT[:, :], KD), eB, dr3(f_eifm[0][:, fc * TFC:(fc + 1) * TFC]),
                   f_eifm[1])
            dma_in("Y", fm3(pre_t[0][:, :], KF), pre_t[1],
                   dr3(f_f1[0][:, fc * TFC:(fc + 1) * TFC]), f_f1[1])
            for m in range(KF):
                bk, bkb = rot()
                mm_group(bk[0:128, 0:TFC], bkb,
                         [(weif[:, k * DFF + m * 128: k * DFF + (m + 1) * 128],
                           eT[:, k * TFC:(k + 1) * TFC]) for k in range(KD)],
                         [b_weif, eB])
                sm = s1o[0][:, m * TFC:(m + 1) * TFC]
                pm = pre_t[0][:, m * TFC:(m + 1) * TFC]
                P.op("S", lambda s, sm=sm, bk=bk: s.activation(sm, bk[0:128, 0:TFC], AF.Silu),
                     reads=[bkb], writes=[s1o[1]])
                P.op("V", lambda v, sm=sm, pm=pm: v.tensor_mul(sm, sm, pm),
                     reads=[pre_t[1]], writes=[s1o[1]])
            dma_out("Y", dr3(f_f1[0][:, fc * TFC:(fc + 1) * TFC]), f_f1[1],
                    fm3(s1o[0][:, :], KF), s1o[1])
        P.barrier()
        es5.close()
        es5c = ExitStack()
        sb = lambda n, sh, dt: P.sb(tag + n, sh, dt, es=es5c)
        P.op("P", lambda g: g.dma_start(
            out=weif[:, 0:KF * D].rearrange("p (k m) -> p k m", k=KF), in_=dr3(wf2[0])),
            reads=[wf2[1]], writes=[b_weif], dma=True)
        hp = [sb("hp%d" % i, [128, KF * TFC], BF16) for i in range(2)]
        eor = sb("eor", [128, KD * 128], BF16)
        eos = sb("eos", [128, TFC], F32)
        for fc in range(NCF):
            ib = fc % 2
            hT, hB = hp[ib]
            dma_in("Y", fm3(hT[:, :], KF), hB, dr3(f_f1[0][:, fc * TFC:(fc + 1) * TFC]),
                   f_f1[1])
            nblk = TFC // 128
            for m in range(KD):
                bk, bkb = rot()
                mm_group(bk[0:128, 0:TFC], bkb,
                         [(weif[:, k * D + m * 128: k * D + (m + 1) * 128],
                           hT[:, k * TFC:(k + 1) * TFC]) for k in range(KF)],
                         [b_weif, hB])
                P.op("V", lambda v, bk=bk: v.tensor_copy(eos[0][:, 0:TFC], bk[0:128, 0:TFC]),
                     reads=[bkb], writes=[eos[1]])
                for blk in range(nblk):
                    bkT_, bkTb_ = rot()
                    P.op("T", lambda t, blk=blk, bkT_=bkT_: t.transpose(
                        bkT_[0:128, 0:128], eos[0][:, blk * 128:(blk + 1) * 128],
                        r_idn[:, :]), reads=[eos[1], b_idn], writes=[bkTb_])
                    erk = eor[0][:, m * 128:(m + 1) * 128]
                    P.op("V", lambda v, erk=erk, bkT_=bkT_: v.tensor_copy(
                        erk, bkT_[0:128, 0:128]), reads=[bkTb_], writes=[eor[1]])
                    dma_out("Y", f_eo[0][fc * TFC + blk * 128: fc * TFC + (blk + 1) * 128,
                                         m * 128:(m + 1) * 128],
                            f_eo[1], erk, eor[1])
        if stopper("e5c"):
            return
        P.op("P", lambda g: g.collective_compute(
            "AllGather", ALU.bypass, replica_groups=[list(range(NCORES))],
            ins=[f_eo[0].opt()], outs=[f_eall[0].opt()]),
            reads=[f_eo[1]], writes=[f_eall[1]], coll=True)
        if stopper("e6"):
            return
        P.barrier()
        es5c.close()
        es7 = ExitStack()
        sb = lambda n, sh, dt: P.sb(tag + n, sh, dt, es=es7)
        eo1 = sb("eo1", [128, D], BF16)
        eo2 = sb("eo2", [128, D], BF16)
        dcmb = sb("dcmb", [128, D], F32)
        xcmb = [sb("xcmb%d" % i, [128, KD * 128], F32) for i in range(2)]
        P.op("V", lambda v: v.memset(eo1[0][:, :], 0.0), writes=[eo1[1]])
        P.op("V", lambda v: v.memset(eo2[0][:, :], 0.0), writes=[eo2[1]])
        for tb in range(NB):
            ib = tb % 2
            tcol = slice(tb * 128, (tb + 1) * 128)
            a1 = ownci[0][:, tb: tb + 1]
            a2 = ownci[0][:, JW + tb: JW + tb + 1]
            E7M = os.environ.get("E7_MODE", "full")
            if "noind" in E7M:
                dma_in("P", eo1[0][:, :], eo1[1], f_eall[0][0:128, :], f_eall[1])
                dma_in("P", eo2[0][:, :], eo2[1], f_eall[0][0:128, :], f_eall[1])
            else:
                P.op("P", lambda g, a1=a1: g.indirect_dma_start(
                    out=eo1[0][:, :], out_offset=None, in_=f_eall[0][:, :],
                    in_offset=IndirectOffsetOnAxis(ap=a1, axis=0),
                    bounds_check=rg_eall, oob_is_err=False),
                    reads=[f_eall[1], ownci[1], eo1[1]], writes=[eo1[1]], dma=True)
                P.op("P", lambda g, a2=a2: g.indirect_dma_start(
                    out=eo2[0][:, :], out_offset=None, in_=f_eall[0][:, :],
                    in_offset=IndirectOffsetOnAxis(ap=a2, axis=0),
                    bounds_check=rg_eall, oob_is_err=False),
                    reads=[f_eall[1], ownci[1], eo2[1]], writes=[eo2[1]], dma=True)
            k1 = ownT[0][:, 2 * JW + tb: 2 * JW + tb + 1]
            k2 = ownT[0][:, 3 * JW + tb: 3 * JW + tb + 1]
            if "nomix" in os.environ.get("E7_MODE", "full"):
                P.op("V", lambda v: v.tensor_copy(dcmb[0][:, :], eo1[0][:, :]),
                     reads=[eo1[1]], writes=[dcmb[1]])
            else:
                P.op("V", lambda v, k1=k1: v.tensor_scalar_mul(dcmb[0][:, :], eo1[0][:, :], k1),
                     reads=[eo1[1], ownT[1]], writes=[dcmb[1]])
                P.op("V", lambda v, k2=k2: v.scalar_tensor_tensor(
                    dcmb[0][:, :], eo2[0][:, :], k2, dcmb[0][:, :], ALU.mult, ALU.add),
                    reads=[eo2[1], ownT[1]], writes=[dcmb[1]])
            xT, xB = xcmb[ib]
            dma_in("Y", fm3(xT[:, :], KD), xB, dr3(x_in[0][:, tcol]), x_in[1])
            for k in range(KD):
                bkT_, bkTb_ = rot()
                P.op("T", lambda t, k=k, bkT_=bkT_: t.transpose(
                    bkT_[0:128, 0:128], dcmb[0][:, k * 128:(k + 1) * 128], r_idn[:, :]),
                    reads=[dcmb[1], b_idn], writes=[bkTb_])
                xk = xT[:, k * 128:(k + 1) * 128]
                P.op("V", lambda v, xk=xk, bkT_=bkT_: v.tensor_add(
                    xk, xk, bkT_[0:128, 0:128]), reads=[bkTb_], writes=[xB])
            dma_out("Y", dr3(x_out[0][:, tcol]), x_out[1], fm3(xT[:, :], KD), xB)
        P.barrier()
        if os.environ.get("FFN_DUMP", "") and tag == "f1_":
            _dodump(os.environ["FFN_DUMP"])
        es7.close()
        esE.close()

    def attnb_phase():
        esB = ExitStack()
        sb = lambda n, sh, dt: P.sb("b_" + n, sh, dt, es=esB)
        r_cwb = f32res("r_cwb", cwb, [128, KH2 * KB], es=esB)
        r_cbb = f32res("r_cbb", cbb, [128, KH2], es=esB)
        r_bdtb = f32res("r_bdtb", bdtb, [128, KH2], es=esB)
        r_dparb = f32res("r_dparb", dparb, [128, KH2], es=esB)
        r_even = f32res("r_even", evenm, [128, 1], es=esB)
        araw_s = f32res("araw_s", arawt, [128, KH2 * NB_], es=esB)
        acols = sb("acols", [128, KH2 * NB_], F32)
        P.op("S", lambda s: s.activation(acols[0][:, :], araw_s[0][:, :], AF.Exp),
             reads=[araw_s[1]], writes=[acols[1]])
        P.op("V", lambda v: v.tensor_scalar(acols[0][:, :], acols[0][:, :], -1.0, None,
                                            ALU.mult), reads=[acols[1]], writes=[acols[1]])
        NDBC = DTR + 2 * NB_
        esB1 = ExitStack()
        sb = lambda n, sh, dt: P.sb("b_" + n, sh, dt, es=esB1)
        xb0 = [sb("xb0_%d" % i, [128, KD * TT], F32) for i in range(2)]
        lnb0 = [sb("lnb0_%d" % i, [128, KD * TT], BF16) for i in range(2)]
        sqb = sb("sqb", [128, TT], F32)
        rsb = sb("rsb", [128, TT], F32)
        aab = sb("aab", [128, TT], F32)

        def b0_chunk(tc):
            ib = tc % 2
            xT, xB = xb0[ib]
            lT, lB = lnb0[ib]
            col = slice(tc * TT, (tc + 1) * TT)
            dma_in("Y", fm3(xT[:, :], KD), xB, dr3(s_x2[0][:, col]), s_x2[1])
            (bkS, bkSb), (bkQ, bkQb) = stats_pair(xT, xB, TT, sqb)
            mu = rows(slice(0, TT))
            msq = rows(slice(TT, 2 * TT))
            var = rows(slice(2 * TT, 3 * TT))
            rstd = rows(slice(3 * TT, 4 * TT))
            arow = rows(slice(4 * TT, 5 * TT))
            P.op("V", lambda v: v.tensor_scalar(mu, bkS[0:1, 0:TT], 1.0 / D, None, ALU.mult),
                 reads=[bkSb], writes=[ROWB])
            P.op("V", lambda v: v.tensor_scalar(msq, bkQ[0:1, 0:TT], 1.0 / D, None, ALU.mult),
                 reads=[bkQb], writes=[ROWB])
            P.op("S", lambda s: s.activation(var, mu, AF.Square), reads=[ROWB], writes=[ROWB])
            P.op("V", lambda v: v.tensor_sub(var, msq, var), reads=[ROWB], writes=[ROWB])
            P.op("S", lambda s: s.activation(rstd, var, AF.Sqrt, bias=c_eps[0:1, 0:1]),
                 reads=[ROWB, b_ceps], writes=[ROWB])
            P.op("V", lambda v: v.reciprocal(rstd, rstd), reads=[ROWB], writes=[ROWB])
            P.op("V", lambda v: v.tensor_mul(arow, mu, rstd), reads=[ROWB], writes=[ROWB])
            bcast(rot, rstd, ROWB, rsb[0][:, 0:TT], rsb[1], TT)
            bcast(rot, arow, ROWB, aab[0][:, 0:TT], aab[1], TT)
            for k in range(KD):
                xk = xT[:, k * TT:(k + 1) * TT]
                lk = lT[:, k * TT:(k + 1) * TT]
                P.op("V", [lambda v, xk=xk: v.tensor_mul(xk, xk, rsb[0][:, 0:TT]),
                           lambda v, xk=xk: v.tensor_sub(xk, xk, aab[0][:, 0:TT]),
                           lambda v, xk=xk, lk=lk: v.tensor_copy(lk, xk)],
                     reads=[xB, rsb[1], aab[1]], writes=[xB, lB])
            dma_out("Y", dr3(b_lnb[0][:, col]), b_lnb[1], fm3(lT[:, :], KD), lB)

        for tc in range(NCH):
            b0_chunk(tc)
        P.op("P", lambda g: g.collective_compute(
            "AllGather", ALU.bypass, replica_groups=[[0, 1], [2, 3], [4, 5], [6, 7]],
            ins=[b_lnb[0].opt()], outs=[b_lnall[0].opt()]),
            reads=[b_lnb[1]], writes=[b_lnall[1]], coll=True)
        P.barrier()
        esB1.close()
        esB2 = ExitStack()
        sb = lambda n, sh, dt: P.sb("b_" + n, sh, dt, es=esB2)
        winb, b_winb = sb("winb", [128, KD * 2 * DH], BF16)
        P.op("P", lambda e: e.dma_start(out=winb[:, :].rearrange("p (k m) -> p k m", k=KD),
                                        in_=dr3(w_inb[0])),
             reads=[w_inb[1]], writes=[b_winb], dma=True)
        wxb, b_wxb = sb("wxb", [128, KH2 * NDBC], BF16)
        P.op("P", lambda e: e.dma_start(
            out=wxb[:, :].rearrange("p (k m) -> p k m", k=KH2), in_=dr3(w_xb[0])),
            reads=[w_xb[1]], writes=[b_wxb], dma=True)
        wdtb, b_wdtb = sb("wdtb", [DTR, DH], BF16)
        P.op("P", lambda e: e.dma_start(out=wdtb[:, :], in_=w_dtb[0][:, :]),
             reads=[w_dtb[1]], writes=[b_wdtb], dma=True)
        wob, b_wob = sb("wob", [128, KH2 * D], BF16)
        P.op("P", lambda e: e.dma_start(out=wob[:, :].rearrange("p (k m) -> p k m", k=KH2),
                                        in_=dr3(w_ob[0])),
             reads=[w_ob[1]], writes=[b_wob], dma=True)
        xzb = [sb("xzb0", [128, TT], F32)]
        cvb = [sb("cvb0", [128, TT], F32)]
        hkb = sb("hkb", [128, KH2 * 3], F32)
        xmfb = sb("xmfb", [128, TT], F32)
        lnt = [sb("lnt%d" % i, [128, KD * TT], BF16) for i in range(2)]
        xcbt, b_xcb = sb("xcbt", [128, KH2 * TT], BF16)
        dbt = sb("dbt", [96, TT], F32)
        dtt = [sb("dtt0", [DTR, TT], BF16)]
        brow2 = [sb("brow2_0", [1, TT], F32)]
        bbct, _bb = sb("bbc", [128, NB_ * TT], BF16)
        cbct, _cb = sb("cbc", [128, NB_ * TT], BF16)
        bbcb = [Buf("b_bbc%d" % n) for n in range(NB_)]
        cbcb = [Buf("b_cbc%d" % n) for n in range(NB_)]
        dlt2 = [sb("dlt%d" % i, [128, TT], F32) for i in range(2)]
        dlt4 = [sb("dltr%d" % i, [128, TT], F32) for i in range(2)]
        spa1 = sb("spa", [128, TT], F32)
        spb1 = sb("spb", [128, TT], F32)
        wdx = sb("wdx", [128, TT], F32)
        xc4 = [sb("xc4_%d" % i, [128, TT], F32) for i in range(2)]
        zt51 = sb("zt5", [128, TT], F32)
        sg51 = sb("sg5", [128, TT], F32)
        NQ = 2
        argq = [sb("argq%d" % i, [128, NQ * TT], F32) for i in range(2)]
        uuq = [sb("uuq%d" % i, [128, NQ * TT], F32) for i in range(2)]
        hht, _hb = sb("hhq", [128, NQ * TT], F32)
        argbq = [[Buf("b_arg%d_%d" % (i, j)) for j in range(NQ)] for i in range(2)]
        uubq = [[Buf("b_uu%d_%d" % (i, j)) for j in range(NQ)] for i in range(2)]
        hhb = [Buf("b_hh%d" % j) for j in range(NQ)]
        yy = sb("yy", [128, TT], F32)
        car4 = sb("car4", [128, KH2 * NB_], F32)
        P.op("V", lambda v: v.memset(car4[0][:, :], 0.0), writes=[car4[1]])
        ygt = [sb("ygt%d" % i, [128, KH2 * TT], BF16) for i in range(2)]
        ob52 = [sb("ob5_%d" % i, [128, TT], BF16) for i in range(2)]
        xf6 = [sb("xf6_0", [128, KD * 128], F32)]
        f06 = [sb("f06_0", [128, 128], F32)]
        f16 = [sb("f16_0", [128, 128], F32)]
        NCB = TB2 // TT

        def stage1_pre(tc):
            ib = tc % 2
            lT, lB = lnt[ib]
            rblk = 0 if tc < NCH else 1
            loc = slice((tc % NCH) * TT, (tc % NCH + 1) * TT)
            dma_in("Y", fm3(lT[:, :], KD), lB,
                   dr3(b_lnall[0][rblk * D:(rblk + 1) * D, loc]), b_lnall[1])

        def stage1_m(tc, m):
            ib = tc % 2
            lT, lB = lnt[ib]
            col = slice(tc * TT, (tc + 1) * TT)
            if True:
                bk, bkb = rot()
                bko = bk[0:128, 0:TT]
                mm_group(bko, bkb,
                         [(winb[:, (k * 2 * DH) + m * 128:(k * 2 * DH) + (m + 1) * 128],
                           lT[:, k * TT:(k + 1) * TT]) for k in range(KD)],
                         [b_winb, lB])
                if m >= KH2:
                    mg = m - KH2
                    zT, zB = xzb[0]
                    zo = zT[:, 0:TT]
                    P.op("S", lambda s, zo=zo, bko=bko: s.activation(zo, bko, AF.Copy),
                         reads=[bkb], writes=[zB])
                    dma_out("Y", b_z[0][mg * 128:(mg + 1) * 128, col], b_z[1], zo, zB)
                    return
                if tc == 0:
                    P.op("V", lambda v, m=m: v.memset(hkb[0][:, 3 * m:3 * m + 3], 0.0),
                         writes=[hkb[1]])
                cT, cB = cvb[0]
                co = cT[:, 0:TT]
                hk = hkb[0][:, 3 * m:3 * m + 3]
                w3 = [r_cwb[0][:, KB * m + j: KB * m + j + 1] for j in range(KB)]
                P.op("V", lambda v, co=co, bko=bko, w=w3[3]: v.tensor_scalar_mul(co, bko, w),
                     reads=[bkb, r_cwb[1]], writes=[cB])
                for j, sh in ((2, 1), (1, 2), (0, 3)):
                    P.op("V", lambda v, co=co, bko=bko, w=w3[j], sh=sh: v.scalar_tensor_tensor(
                        co[:, sh:TT], bko[:, 0:TT - sh], w, co[:, sh:TT], ALU.mult, ALU.add),
                        reads=[bkb, r_cwb[1]], writes=[cB])
                    P.op("V", lambda v, co=co, hk=hk, w=w3[j], sh=sh: v.scalar_tensor_tensor(
                        co[:, 0:sh], hk[:, 3 - sh:3], w, co[:, 0:sh], ALU.mult, ALU.add),
                        reads=[hkb[1], r_cwb[1]], writes=[cB])
                P.op("V", lambda v, hk=hk, bko=bko: v.tensor_copy(hk, bko[:, TT - 3:TT]),
                     reads=[bkb], writes=[hkb[1]])
                cbk = r_cbb[0][:, m:m + 1]
                P.op("S", lambda s, co=co, cbk=cbk: s.activation(
                    xmfb[0][:, 0:TT], co, AF.Silu, bias=cbk), reads=[cB, r_cbb[1]],
                    writes=[xmfb[1]])
                xck = xcbt[:, m * TT:(m + 1) * TT]
                P.op("S", lambda s, xck=xck: s.activation(xck, xmfb[0][:, 0:TT], AF.Copy),
                     reads=[xmfb[1]], writes=[b_xcb])
                dma_out("Y", b_xc[0][m * 128:(m + 1) * 128, col], b_xc[1],
                        xmfb[0][:, 0:TT], xmfb[1])
        def stage1_tail(tc):
            ib = tc % 2
            lT, lB = lnt[ib]
            col = slice(tc * TT, (tc + 1) * TT)
            bkD, bkDb = rot()
            mm_group(bkD[0:96, 0:TT], bkDb,
                     [(wxb[:, k * NDBC:(k + 1) * NDBC], xcbt[:, k * TT:(k + 1) * TT])
                      for k in range(KH2)],
                     [b_wxb, b_xcb])
            P.op("S", lambda s: s.activation(dbt[0][:, :], bkD[0:96, 0:TT], AF.Copy),
                 reads=[bkDb], writes=[dbt[1]])
            dma_out("Y", b_dbcp[0][tc * NDBC:(tc + 1) * NDBC, :], b_dbcp[1],
                    dbt[0][:, :], dbt[1])
            rsl = slice(tc * NDBC, (tc + 1) * NDBC)
            P.op("P", lambda g, rsl=rsl: g.collective_compute(
                "AllReduce", ALU.add, replica_groups=[[0, 1], [2, 3], [4, 5], [6, 7]],
                ins=[b_dbcp[0][rsl, :].opt()], outs=[b_dbc[0][rsl, :].opt()]),
                reads=[b_dbcp[1]], writes=[b_dbc[1]], coll=True)
        def stage23_pre(tc):
            for (base, dstt, dstbs) in ((DTR, bbct, bbcb), (DTR + NB_, cbct, cbcb)):
                for n in range(NB_):
                    rT, rB = brow2[0]
                    rowsrc = b_dbc[0][tc * NDBC + base + n:tc * NDBC + base + n + 1, :]
                    P.op("Y", lambda e, rT=rT, rowsrc=rowsrc: e.dma_start(
                        out=rT[0:1, :], in_=rowsrc), reads=[b_dbc[1]], writes=[rB],
                        dma=True)
                    bcast(rot, rT[0:1, :], rB, dstt[:, n * TT:(n + 1) * TT], dstbs[n],
                          TT, eng="S")

        def stage23_dlt(tc, dk):
            col = slice(tc * TT, (tc + 1) * TT)
            if dk == 0:
                dT_, dB_ = dtt[0]
                dma_in("P", dT_[:, :], dB_, b_dbc[0][tc * NDBC:tc * NDBC + DTR, :],
                       b_dbc[1])
            dT_, dB_ = dtt[0]
            idb = dk % 2
            bk3, bk3b = rot()
            bko3 = bk3[0:128, 0:TT]
            mm_group(bko3, bk3b, [(wdtb[:, dk * 128:(dk + 1) * 128], dT_[:, :])],
                     [b_wdtb, dB_])
            bc = r_bdtb[0][:, dk:dk + 1]
            dT5, dB5 = dlt2[idb]
            spaT, spaB = spa1
            spbT, spbB = spb1
            P.op("S", lambda s, spbT=spbT, bko3=bko3, bc=bc: s.activation(
                spbT[:, :], bko3, AF.Abs, bias=bc), reads=[bk3b, r_bdtb[1]],
                writes=[spbB])
            P.op("S", lambda s, spbT=spbT: s.activation(spbT[:, :], spbT[:, :],
                                                        AF.Exp, scale=-1.0),
                 reads=[spbB], writes=[spbB])
            P.op("S", lambda s, spbT=spbT: s.activation(spbT[:, :], spbT[:, :],
                                                        AF.Ln, bias=1.0),
                 reads=[spbB], writes=[spbB])
            P.op("S", lambda s, spaT=spaT, bko3=bko3, bc=bc: s.activation(
                spaT[:, :], bko3, AF.Relu, bias=bc), reads=[bk3b, r_bdtb[1]],
                writes=[spaB])
            P.op("V", lambda v, dT5=dT5, spaT=spaT, spbT=spbT: v.tensor_add(
                dT5[:, :], spaT[:, :], spbT[:, :]), reads=[spaB, spbB], writes=[dB5])
            dma_out("Y", b_dlt[0][dk * 128:(dk + 1) * 128, col], b_dlt[1],
                    dT5[:, :], dB5)

        def stage23_scan(tc, dk):
            ic = tc % 2
            col = slice(tc * TT, (tc + 1) * TT)
            ygT, ygB = ygt[ic]
            if True:
                idb = dk % 2
                dT5, dB5 = dlt4[idb]
                dma_in("Y", dT5[:, :], dB5, b_dlt[0][dk * 128:(dk + 1) * 128, col],
                       b_dlt[1])
                xT4, xB4 = xc4[idb]
                dma_in("Y", xT4[:, :], xB4, b_xc[0][dk * 128:(dk + 1) * 128, col], b_xc[1])
                P.op("P", lambda g, dT5=dT5, xT4=xT4: g.tensor_mul(
                    wdx[0][:, :], dT5[:, :], xT4[:, :]), reads=[dB5, xB4], writes=[wdx[1]])
                zT5, zB5 = zt51
                sgT, sgB = sg51
                dma_in("Y", zT5[:, :], zB5, b_z[0][dk * 128:(dk + 1) * 128, col], b_z[1])
                P.op("S", lambda s, zT5=zT5, sgT=sgT: s.activation(
                    sgT[:, :], zT5[:, :], AF.Silu), reads=[zB5], writes=[sgB])
                for q in range(NB_ // NQ):
                    iq = q % 2
                    argt = argq[iq][0]
                    argb = argbq[iq]
                    b_argt = argq[iq][1]
                    uut = uuq[iq][0]
                    uub = uubq[iq]
                    for j in range(NQ):
                        n = q * NQ + j
                        asl = argt[:, j * TT:(j + 1) * TT]
                        acol = acols[0][:, dk * NB_ + n: dk * NB_ + n + 1]
                        P.op("S", lambda s, asl=asl, dT5=dT5, acol=acol: s.activation(
                            asl, dT5[:, :], AF.Exp, scale=acol),
                            reads=[dB5, acols[1]], writes=[argb[j]])
                        usl = uut[:, j * TT:(j + 1) * TT]
                        bsl = bbct[:, n * TT:(n + 1) * TT]
                        P.op("P", lambda g, usl=usl, bsl=bsl: g.tensor_mul(
                            usl, wdx[0][:, :], bsl), reads=[wdx[1], bbcb[n]],
                            writes=[uub[j]])
                        hsl = hht[:, j * TT:(j + 1) * TT]
                        ccar = car4[0][:, dk * NB_ + n: dk * NB_ + n + 1]
                        P.op("V", lambda v, hsl=hsl, asl=asl, usl=usl, ccar=ccar:
                             v.tensor_tensor_scan(hsl, asl, usl, ccar, ALU.mult, ALU.add),
                             reads=[argb[j], uub[j], car4[1]], writes=[hhb[j]])
                    P.op("S", lambda s, dk=dk, q=q: s.activation(
                        car4[0][:, dk * NB_ + q * NQ:dk * NB_ + (q + 1) * NQ],
                        hht[:, :].rearrange("p (n t) -> p n t", n=NQ)[:, :, TT - 1],
                        AF.Copy),
                        reads=list(hhb), writes=[car4[1]])
                    csl = cbct[:, (q * NQ) * TT:(q * NQ + NQ) * TT]
                    P.op("V", lambda v, csl=csl: v.tensor_mul(hht[:, :], hht[:, :], csl),
                         reads=list(hhb) + cbcb[q * NQ:(q + 1) * NQ], writes=list(hhb))
                    if q == 0:
                        P.op("V", lambda v: v.tensor_add(yy[0][:, :], hht[:, 0:TT],
                                                         hht[:, TT:2 * TT]),
                             reads=[hhb[0], hhb[1]], writes=[yy[1]])
                    else:
                        P.op("V", lambda v: v.tensor_add(hht[:, 0:TT], hht[:, 0:TT],
                                                         hht[:, TT:2 * TT]),
                             reads=[hhb[1]], writes=[hhb[0]])
                        P.op("V", lambda v: v.tensor_add(yy[0][:, :], yy[0][:, :],
                                                         hht[:, 0:TT]),
                             reads=[hhb[0]], writes=[yy[1]])
                dpk = r_dparb[0][:, dk:dk + 1]
                P.op("V", lambda v, dpk=dpk, xT4=xT4: v.scalar_tensor_tensor(
                    yy[0][:, :], xT4[:, :], dpk, yy[0][:, :], ALU.mult, ALU.add),
                    reads=[xB4, r_dparb[1]], writes=[yy[1]])
                yk = ygT[:, dk * TT:(dk + 1) * TT]
                P.op("V", lambda v, yk=yk, sgT=sgT: v.tensor_mul(yk, yy[0][:, :], sgT[:, :]),
                     reads=[yy[1], sgB], writes=[ygB])

        def stage4(tc):
            ic = tc % 2
            ygT, ygB = ygt[ic]
            for m in range(KD):
                bk, bkb = rot()
                bko = bk[0:128, 0:TT]
                mm_group(bko, bkb,
                         [(wob[:, k * D + m * 128: k * D + (m + 1) * 128],
                           ygT[:, k * TT:(k + 1) * TT]) for k in range(KH2)],
                         [b_wob, ygB])
                obT, obB = ob52[m % 2]
                P.op("S", lambda s, obT=obT, bko=bko: s.activation(obT[:, :], bko, AF.Copy),
                     reads=[bkb], writes=[obB])
                dma_out("Y", b_op[0][tc * D + m * 128:tc * D + (m + 1) * 128, :],
                        b_op[1], obT[:, :], obB)
            rsl = slice(tc * D, (tc + 1) * D)
            P.op("P", lambda g, rsl=rsl: g.collective_compute(
                "AllReduce", ALU.add, replica_groups=[[0, 1], [2, 3], [4, 5], [6, 7]],
                ins=[b_op[0][rsl, :].opt()], outs=[b_opr[0][rsl, :].opt()]),
                reads=[b_op[1]], writes=[b_opr[1]], coll=True)

        def b6_chunk(tc):
            for bb in range(TT // 128):
                xT, xB = xf6[0]
                colb = slice(tc * TT + bb * 128, tc * TT + (bb + 1) * 128)
                dma_in("Y", fm3(xT[:, :], KD), xB, dr3(s_x2[0][:, colb]), s_x2[1])
                for m in range(KD):
                    f0T, f0B = f06[0]
                    f1T, f1B = f16[0]
                    dma_in("P", f0T[:, :], f0B,
                           b_opr[0][tc * D + m * 128:tc * D + (m + 1) * 128,
                                    bb * 128:(bb + 1) * 128], b_opr[1])
                    dma_in("P", f1T[:, :], f1B,
                           b_opr[0][(NCH + tc) * D + m * 128:(NCH + tc) * D + (m + 1) * 128,
                                    bb * 128:(bb + 1) * 128], b_opr[1])
                    xk = xT[:, m * 128:(m + 1) * 128]
                    P.op("V", [
                        lambda v, xk=xk, f0T=f0T: v.scalar_tensor_tensor(
                            xk, f0T[:, :], r_even[0][:, :], xk, ALU.mult, ALU.add),
                        lambda v, xk=xk, f1T=f1T: v.scalar_tensor_tensor(
                            xk, f1T[:, :], r_odd[0][:, :], xk, ALU.mult, ALU.add)],
                        reads=[f0B, f1B, r_even[1], r_odd[1]], writes=[xB])
                dma_out("Y", dr3(s_x3[0][:, colb]), s_x3[1], fm3(xT[:, :], KD), xB)

        for tc in range(NCB):
            if tc >= 3:
                stage4(tc - 3)
            if tc >= 2:
                stage23_pre(tc - 2)
            stage1_pre(tc)
            for dk in range(KH2):
                if tc >= 2:
                    stage23_scan(tc - 2, dk)
                stage1_m(tc, 2 * dk)
                stage1_m(tc, 2 * dk + 1)
                if tc >= 1:
                    stage23_dlt(tc - 1, dk)
            stage1_tail(tc)
            if tc >= 11:
                b6_chunk(tc - 11)
        stage23_pre(NCB - 2)
        for dk in range(KH2):
            stage23_scan(NCB - 2, dk)
            stage23_dlt(NCB - 1, dk)
        stage4(NCB - 3)
        stage23_pre(NCB - 1)
        for dk in range(KH2):
            stage23_scan(NCB - 1, dk)
        stage4(NCB - 2)
        stage4(NCB - 1)
        for j in range(NCB - 11, NCH):
            b6_chunk(j)
        P.barrier()
        esB2.close()
        esB.close()


    sqt_e = P.sb("sqt_e", [128, 128], F32)
    vtr = P.sb("vtr", [128, KD * 128], BF16)
    ffn_phase(s_x1, s_x2, "f1_")
    attnb_phase()
    ffn_phase(s_x3, out_fm, "f2_")
    P.barrier()

    P.emit()
    return nc



def _chunk_cols(vec, nch):
    v = np.asarray(vec, np.float32).reshape(nch, 128).T
    return np.ascontiguousarray(v)


def _pad_col(vec, n=128):
    v = np.zeros((n, 1), np.float32)
    a = np.asarray(vec, np.float32).reshape(-1)
    v[:a.shape[0], 0] = a
    return v


def host_inputs(cfg, inputs, core):
    Tc, D, INNER, NA = cfg["Tc"], cfg["D"], cfg["INNER"], cfg["NA"]
    KD, KI = D // 128, INNER // 128
    b, h = core // 2, core % 2
    x = np.asarray(inputs["L0_attn_a_x"], np.float32)
    xs = x[b, h * Tc:(h + 1) * Tc, :]
    d = {}
    d["x_fm"] = np.ascontiguousarray(xs.T)
    if h == 1:
        d["halo_x"] = np.ascontiguousarray(x[b, h * Tc - 2:h * Tc, :].T)
        d["oddmask"] = np.ones((128, 1), np.float32)
        co = np.zeros((8, 1), np.float32)
        co[core - 1, 0] = 1.0
        d["coreoh"] = co
    else:
        d["halo_x"] = np.zeros((D, 2), np.float32)
        d["oddmask"] = np.zeros((128, 1), np.float32)
        d["coreoh"] = np.zeros((8, 1), np.float32)
    d["n1w"] = _chunk_cols(inputs["L0_attn_a_norm1_w"], KD)
    d["n2w"] = _chunk_cols(inputs["L0_attn_a_norm2_w"], KD)
    d["w_ip"] = np.ascontiguousarray(np.asarray(inputs["L0_attn_a_in_proj_w"], np.float32))
    d["b_ip"] = _chunk_cols(inputs["L0_attn_a_in_proj_b"], 2 * KI)
    cw = np.asarray(inputs["L0_attn_a_conv_w"], np.float32).reshape(INNER, 3)
    cwa = np.zeros((128, KI * 3), np.float32)
    for k in range(KI):
        for j in range(3):
            cwa[:, 3 * k + j] = cw[k * 128:(k + 1) * 128, j]
    d["cwa"] = cwa
    d["cba"] = _chunk_cols(inputs["L0_attn_a_conv_b"], KI)
    d["w_dt"] = np.ascontiguousarray(np.asarray(inputs["L0_attn_a_dt_w"], np.float32))
    d["w_bp"] = np.ascontiguousarray(np.asarray(inputs["L0_attn_a_bp_w"], np.float32))
    d["w_cp"] = np.ascontiguousarray(np.asarray(inputs["L0_attn_a_cp_w"], np.float32))
    bd = np.zeros((128, 3), np.float32)
    bd[:NA, 0] = np.asarray(inputs["L0_attn_a_dt_b"], np.float32)
    bd[:NA, 1] = np.asarray(inputs["L0_attn_a_bp_b"], np.float32)
    bd[:NA, 2] = np.asarray(inputs["L0_attn_a_cp_b"], np.float32)
    d["b_dtc"] = bd
    d["w_s2i"] = np.ascontiguousarray(np.asarray(inputs["L0_attn_a_s2i_w"], np.float32))
    d["b_s2i"] = _chunk_cols(inputs["L0_attn_a_s2i_b"], KI)
    d["dpar"] = _chunk_cols(inputs["L0_attn_a_D_param"], KI)
    d["w_oa"] = np.ascontiguousarray(np.asarray(inputs["L0_attn_a_out_w"], np.float32))
    d["b_oa"] = _chunk_cols(inputs["L0_attn_a_out_b"], KD)
    d["w_ga"] = np.ascontiguousarray(np.asarray(inputs["L0_attn_a_gate_w"], np.float32))
    gb = np.asarray(inputs["L0_attn_a_gate_b"], np.float32).reshape(1, 4)
    d["b_gab"] = np.ascontiguousarray(np.broadcast_to(gb, (128, 4)))
    ew1 = np.asarray(inputs["L0_attn_a_e_w1"], np.float32)
    d["e_w1"] = np.ascontiguousarray(ew1.reshape(4 * D, -1))
    HID = ew1.shape[2]
    KH = HID // 128
    eb1 = np.asarray(inputs["L0_attn_a_e_b1"], np.float32)
    m1 = np.zeros((128, 4 * KH), np.float32)
    for e in range(4):
        m1[:, e * KH:(e + 1) * KH] = eb1[e].reshape(KH, 128).T
    d["e_b1"] = m1
    ew2 = np.asarray(inputs["L0_attn_a_e_w2"], np.float32)
    d["e_w2"] = np.ascontiguousarray(ew2.reshape(4 * HID, D))
    d["e_b2"] = np.ascontiguousarray(np.asarray(inputs["L0_attn_a_e_b2"], np.float32))
    d["idn"] = np.eye(128, dtype=np.float32)
    s4 = np.zeros((4, 512), np.float32)
    for e in range(4):
        s4[e, e * 128:(e + 1) * 128] = 1.0
    d["sel4"] = s4
    d["w_rt"] = np.ascontiguousarray(np.asarray(inputs["L0_ffn_router_weight"], np.float32).T)
    d["wf1"] = np.ascontiguousarray(np.asarray(inputs["L0_ffn_ff_pre_act_weight"], np.float32)[core].T)
    d["wfg"] = np.ascontiguousarray(np.asarray(inputs["L0_ffn_gate_weight"], np.float32)[core].T)
    d["wf2"] = np.ascontiguousarray(np.asarray(inputs["L0_ffn_ff_post_act_weight"], np.float32)[core].T)
    d["ecol"] = np.full((128, 1), float(core), np.float32)
    d["cbase"] = np.full((128, 1), float(core * Tc), np.float32)
    JW = Tc // 128
    qi = np.zeros((128, 2), np.float32)
    qi[:, 0] = np.arange(128) % JW
    qi[:, 1] = np.arange(128) // JW
    d["qiw"] = qi
    dh = h
    DH = 1024
    win = np.asarray(inputs["L0_attn_b_W_in"], np.float32)
    xc_rows = win[dh * DH:(dh + 1) * DH, :]
    z_rows = win[2048 + dh * DH: 2048 + (dh + 1) * DH, :]
    d["w_inb"] = np.ascontiguousarray(np.concatenate([xc_rows.T, z_rows.T], axis=1))
    cwb_ = np.asarray(inputs["L0_attn_b_conv_w"], np.float32).reshape(2048, 4)[dh * DH:(dh + 1) * DH]
    m2 = np.zeros((128, 8 * 4), np.float32)
    for k in range(8):
        for j in range(4):
            m2[:, 4 * k + j] = cwb_[k * 128:(k + 1) * 128, j]
    d["cwb"] = m2
    d["cbb"] = _chunk_cols(np.asarray(inputs["L0_attn_b_conv_b"], np.float32)[dh * DH:(dh + 1) * DH], 8)
    wx = np.asarray(inputs["L0_attn_b_W_x"], np.float32)
    d["w_xb"] = np.ascontiguousarray(wx.T[dh * DH:(dh + 1) * DH, :])
    wdt2 = np.asarray(inputs["L0_attn_b_W_dt"], np.float32)
    d["w_dtb"] = np.ascontiguousarray(wdt2.T[:, dh * DH:(dh + 1) * DH])
    d["bdtb"] = _chunk_cols(np.asarray(inputs["L0_attn_b_b_dt"], np.float32)[dh * DH:(dh + 1) * DH], 8)
    ar = np.asarray(inputs["L0_attn_b_A_raw"], np.float32)
    arT = ar.T[dh * DH:(dh + 1) * DH, :]
    m3 = np.zeros((128, 8 * 16), np.float32)
    for k in range(8):
        m3[:, k * 16:(k + 1) * 16] = arT[k * 128:(k + 1) * 128, :]
    d["arawt"] = m3
    d["dparb"] = _chunk_cols(np.asarray(inputs["L0_attn_b_D"], np.float32)[dh * DH:(dh + 1) * DH], 8)
    wo = np.asarray(inputs["L0_attn_b_W_out"], np.float32)
    d["w_ob"] = np.ascontiguousarray(wo.T[dh * DH:(dh + 1) * DH, :])
    d["evenm"] = np.full((128, 1), 1.0 if h == 0 else 0.0, np.float32)
    return d


def kernel(**inputs):
    cfg = make_cfg(4096)
    nc = build_program(cfg)
    in_maps = [host_inputs(cfg, inputs, c) for c in range(NCORES)]
    res = run_bass_kernel_spmd(nc, in_maps, core_ids=list(range(NCORES)))
    Tc, D = cfg["Tc"], cfg["D"]
    out = np.zeros((4, 2 * Tc, D), np.float32)
    for c in range(NCORES):
        b, h = c // 2, c % 2
        out[b, h * Tc:(h + 1) * Tc, :] = np.asarray(res.results[c]["out_fm"]).T
    return out


# version 11
# speedup vs baseline: 1.0321x; 1.0321x over previous
import math
import numpy as np
from contextlib import ExitStack

import concourse.bass as bass
import concourse.mybir as mybir
from concourse.bass_utils import run_bass_kernel_spmd
from concourse.bass_core_rust import IndirectOffsetOnAxis

F32 = mybir.dt.float32
BF16 = mybir.dt.bfloat16
I32 = mybir.dt.int32
U8 = mybir.dt.uint8
AF = mybir.ActivationFunctionType
ALU = mybir.AluOpType
AX = mybir.AxisListType

NCORES = 8
LN_EPS = 1e-5
RMS_EPS = 1e-6
MOE_EPS = 1.1920929e-07


def make_cfg(Tc=4096):
    cap = int(math.ceil(1.25 * (NCORES * Tc * 2) / 8.0))
    G = (cap + 127) // 128
    return dict(Tc=Tc, D=1024, INNER=2048, NA=64,
                EA=4, HID=4096, NE=8, DFF=4096, CAP=cap, G=G,
                DIB=2048, NB=16, KB=4, DTR=64,
                TT=min(512, Tc))



class Buf:
    __slots__ = ("name", "lw", "readers")

    def __init__(self, name):
        self.name = name
        self.lw = None
        self.readers = []


class Prog:
    ENGS = ("T", "V", "S", "P", "Y")
    LANES = ("Y0", "Y1", "Y2", "Y3", "P0", "P1", "P2", "P3", "C0")

    def __init__(self, nc):
        self.nc = nc
        self.es = ExitStack()
        self.stream = {e: [] for e in self.ENGS}
        allch = self.ENGS + self.LANES
        self.sems = {e: self.es.enter_context(nc.semaphore("ch" + e)) for e in allch}
        self.cnt = {e: 0 for e in allch}
        self.satisfied = {e: {} for e in self.ENGS}
        self.rr = {"Y": 0, "P": 0}
        self.nlane = 4

    def sb(self, name, shape, dtype, es=None):
        t = (es or self.es).enter_context(self.nc.sbuf_tensor(name, shape, dtype))
        return t, Buf(name)

    def dram(self, name, shape, dtype=F32):
        t = self.nc.dram_tensor(name, shape, dtype)
        return t.ap(), Buf(name)

    def psum(self, name):
        t = self.es.enter_context(self.nc.psum_tensor(name, [128, 512], F32))
        return t, Buf(name)

    def op(self, eng, fns, reads=(), writes=(), dma=False, coll=False, extra=(), quiet=False):
        if not isinstance(fns, (list, tuple)):
            fns = [fns]
        if eng != "T" and len(fns) > 1 and not (dma or coll or quiet):
            tok = None
            for f in fns:
                tok = self.op(eng, [f], reads=reads, writes=writes, extra=extra)
            return tok
        deps = {}

        def merge(d):
            if d is None:
                return
            ch, c = d
            if deps.get(ch, -1) < c:
                deps[ch] = c

        for b in reads:
            merge(b.lw)
        for b in writes:
            merge(b.lw)
            for r in b.readers:
                merge(r)
        for d in extra:
            merge(d)
        if coll:
            ch_self = "C0"
            inc = 1
            deps.setdefault(ch_self, self.cnt[ch_self])
            if deps[ch_self] < self.cnt[ch_self]:
                deps[ch_self] = self.cnt[ch_self]
        elif dma:
            ch_self = eng + str(self.rr[eng])
            self.rr[eng] = (self.rr[eng] + 1) % self.nlane
            inc = 16
            need = max(0, self.cnt[ch_self] - 32)
            if deps.get(ch_self, -1) < need:
                deps[ch_self] = need
        else:
            ch_self = eng
            inc = 1
        n1 = self.cnt[ch_self] + inc
        waits = []
        sat = self.satisfied[eng]
        for ch in sorted(deps):
            c = deps[ch]
            if eng == "T" and ch == "T":
                continue
            if sat.get(ch, -1) >= c:
                continue
            waits.append((ch, c))
            sat[ch] = c
        if quiet:
            inc = 0
            n1 = self.cnt[ch_self]
        self.stream[eng].append((fns, waits, inc, ch_self,
                                 [b.name for b in reads], [b.name for b in writes]))
        self.cnt[ch_self] = n1
        tok = (ch_self, n1)
        for b in reads:
            b.readers.append(tok)
        for b in writes:
            b.lw = tok
            b.readers = []
        return tok

    def barrier(self):
        allch = self.ENGS + self.LANES
        for e in self.ENGS:
            sat = self.satisfied[e]
            for ch in allch:
                c = self.cnt[ch]
                if c > 0 and sat.get(ch, -1) < c:
                    self.stream[e].append((None, [(ch, c)], 0, None, [], []))
                    sat[ch] = c

    def emit(self):
        engmap = {"T": "tensor", "V": "vector", "S": "scalar", "P": "gpsimd", "Y": "sync"}
        with self.nc.Block() as block:
            for e in self.ENGS:
                items = self.stream[e]
                if not items:
                    continue

                def body(engobj, items=items, sems=self.sems):
                    for fns, waits, inc, ch_self, _rn, _wn in items:
                        for ch, c in waits:
                            engobj.wait_ge(sems[ch], c)
                        if fns is None:
                            continue
                        inst = None
                        for f in fns:
                            inst = f(engobj)
                        if inc:
                            inst.then_inc(sems[ch_self], inc)
                getattr(block, engmap[e])(body)



def build_program(cfg):
    nc = bass.Bass("TRN2", target_bir_lowering=False)
    Tc, D, INNER, NA = cfg["Tc"], cfg["D"], cfg["INNER"], cfg["NA"]
    TT = cfg["TT"]
    NCH = Tc // TT
    KD = D // 128
    KI = INNER // 128

    P = Prog(nc)

    def inp(name, shape, dtype=F32):
        return nc.dram_tensor(name, shape, dtype, kind="ExternalInput").ap(), Buf(name)

    def outp(name, shape, dtype=F32):
        return nc.dram_tensor(name, shape, dtype, kind="ExternalOutput").ap(), Buf(name)

    x_fm = inp("x_fm", [D, Tc])
    halo_x = inp("halo_x", [D, 2])
    oddmask = inp("oddmask", [128, 1])
    coreoh = inp("coreoh", [8, 1])
    n1w = inp("n1w", [128, KD])
    n2w = inp("n2w", [128, KD])
    w_ip = inp("w_ip", [D, 2 * INNER])
    b_ip = inp("b_ip", [128, 2 * KI])
    cwa = inp("cwa", [128, KI * 3])
    cba = inp("cba", [128, KI])
    w_dt = inp("w_dt", [INNER, NA])
    w_bp = inp("w_bp", [INNER, NA])
    w_cp = inp("w_cp", [INNER, NA])
    b_dtc = inp("b_dtc", [128, 3])
    w_s2i = inp("w_s2i", [NA, INNER])
    b_s2i = inp("b_s2i", [128, KI])
    dpar = inp("dpar", [128, KI])
    w_oa = inp("w_oa", [INNER, D])
    b_oa = inp("b_oa", [128, KD])
    w_ga = inp("w_ga", [D, 4])
    b_gab = inp("b_gab", [128, 4])
    e_w1 = inp("e_w1", [4 * D, cfg["HID"]])
    e_b1 = inp("e_b1", [128, 4 * cfg["HID"] // 128])
    e_w2 = inp("e_w2", [4 * cfg["HID"], D])
    e_b2 = inp("e_b2", [4, D])
    idn = inp("idn", [128, 128])
    sel4 = inp("sel4", [4, 512])
    w_rt = inp("w_rt", [D, cfg["NE"]])
    wf1 = inp("wf1", [D, cfg["DFF"]])
    wfg = inp("wfg", [D, cfg["DFF"]])
    wf2 = inp("wf2", [cfg["DFF"], D])
    ecol = inp("ecol", [128, 1])
    cbase = inp("cbase", [128, 1])
    qiw = inp("qiw", [128, 2])
    DIB, NB_, KB, DTR = cfg["DIB"], cfg["NB"], cfg["KB"], cfg["DTR"]
    DH = DIB // 2
    KH2 = DH // 128
    w_inb = inp("w_inb", [D, 2 * DH])
    cwb = inp("cwb", [128, KH2 * KB])
    cbb = inp("cbb", [128, KH2])
    w_xb = inp("w_xb", [DH, DTR + 2 * NB_])
    w_dtb = inp("w_dtb", [DTR, DH])
    bdtb = inp("bdtb", [128, KH2])
    arawt = inp("arawt", [128, KH2 * NB_])
    dparb = inp("dparb", [128, KH2])
    w_ob = inp("w_ob", [DH, D])
    evenm = inp("evenm", [128, 1])

    out_fm = outp("out_fm", [D, Tc])

    s_u = P.dram("s_u", [D, Tc])
    s_gate = P.dram("s_gate", [INNER, Tc])
    s_xm = P.dram("s_xm", [INNER, Tc])
    s_dtb = P.dram("s_dtb", [NA, 3 * Tc])
    s_u1 = P.dram("s_u1", [D, Tc])
    s_x1 = P.dram("s_x1", [D, Tc])
    s_r2b = P.dram("s_r2b", [D, Tc], BF16)
    s_hid = P.dram("s_hid", [cfg["HID"], Tc], BF16)
    NE, DFF, CAP, G = cfg["NE"], cfg["DFF"], cfg["CAP"], cfg["G"]
    CAPP = G * 128
    NJ = NCORES * Tc // 128
    JW = Tc // 128
    s_x2 = P.dram("s_x2", [D, Tc])
    f_vtok = P.dram("f_vtok", [Tc, D], BF16)
    f_hall = P.dram("f_hall", [NCORES * Tc, D], BF16)
    f_rtl = P.dram("f_rtl", [Tc, 4])
    f_rta = P.dram("f_rta", [NCORES * Tc, 4])
    f_t6 = P.dram("f_t6", [4 * NJ, 128])
    f_lin = P.dram("f_lin", [NE * 128 * G, 1])
    f_lout = P.dram("f_lout", [NE * 128 * G, 1])
    f_ei = P.dram("f_ei", [CAPP, D], BF16)
    f_eifm = P.dram("f_eifm", [D, CAPP], BF16)
    f_f1 = P.dram("f_f1", [DFF, CAPP], BF16)
    f_eo = P.dram("f_eo", [CAPP, D], BF16)
    f_eall = P.dram("f_eall", [NE * CAPP, D], BF16)
    TB2 = 2 * Tc
    s_x3 = P.dram("s_x3", [D, Tc])
    s_x4 = P.dram("s_x4", [D, Tc])
    b_lnb = P.dram("b_lnb", [D, Tc], BF16)
    b_lnall = P.dram("b_lnall", [2 * D, Tc], BF16)
    b_xc = P.dram("b_xc", [DH, TB2])
    b_z = P.dram("b_z", [DH, TB2])
    b_dbcp = P.dram("b_dbcp", [(DTR + 2 * NB_) * (TB2 // TT), TT])
    b_dbc = P.dram("b_dbc", [(DTR + 2 * NB_) * (TB2 // TT), TT])
    b_dlt = P.dram("b_dlt", [DH, TB2])
    b_y = P.dram("b_y", [DH, TB2])
    b_op = P.dram("b_op", [D * (TB2 // TT), TT], BF16)
    b_opr = P.dram("b_opr", [D * (TB2 // TT), TT], BF16)
    agh_in = P.dram("agh_in", [1, NA])
    agh_out = P.dram("agh_out", [8, NA])
    sin_rt = P.dram("sin_rt", [NA, 1])

    ones1, b_ones1 = P.sb("ones1", [1, 128], F32)
    onesc, b_onesc = P.sb("onesc", [128, 1], F32)
    P.op("V", lambda v: v.memset(ones1[:, :], 1.0), writes=[b_ones1])
    P.op("V", lambda v: v.memset(onesc[:, :], 1.0), writes=[b_onesc])
    c_eps, b_ceps = P.sb("c_eps", [128, 2], F32)
    P.op("V", lambda v: v.memset(c_eps[:, 0:1], LN_EPS), writes=[b_ceps])
    P.op("V", lambda v: v.memset(c_eps[:, 1:2], RMS_EPS), writes=[b_ceps])
    PSB = [P.psum("pb%d" % i) for i in range(8)]

    class Rot:
        def __init__(self, banks):
            self.banks = banks
            self.i = 0

        def __call__(self):
            b = self.banks[self.i % len(self.banks)]
            self.i += 1
            return b

    def dma_in(q, tile, tb, src_ap, src_buf, cast=False):
        eng = "P" if cast else q
        return P.op(eng, lambda e, t=tile, a=src_ap: e.dma_start(out=t, in_=a),
                    reads=[src_buf], writes=[tb], dma=True)

    def dma_out(q, dst_ap, dst_buf, tile, tb, cast=False):
        eng = "P" if cast else q
        return P.op(eng, lambda e, t=tile, a=dst_ap: e.dma_start(out=a, in_=t),
                    reads=[tb], writes=[dst_buf], dma=True)

    def mm_group(bank, bankbuf, mms, reads):
        fns = []
        n = len(mms)
        for i, (l, r) in enumerate(mms):
            fns.append(lambda t, l=l, r=r, s=(i == 0), e=(i == n - 1):
                       t.matmul(bank, l, r, start=s, stop=e))
        return P.op("T", fns, reads=list(reads), writes=[bankbuf])

    def fm3(tile_ap, kk):
        return tile_ap.rearrange("p (k t) -> p k t", k=kk)

    def dr3(dram_slice):
        return dram_slice.rearrange("(k p) t -> p k t", p=128)

    def bcast(rot_, row_ap, row_buf, dst_tile_ap, dst_buf, W, eng="V"):
        bk, bkb = rot_()
        mm_group(bk[0:128, 0:W], bkb, [(ones1[:, :], row_ap)], [b_ones1, row_buf])
        if eng == "S":
            return P.op("S", lambda s, bk=bk: s.activation(dst_tile_ap, bk[0:128, 0:W],
                                                           AF.Copy),
                        reads=[bkb], writes=[dst_buf])
        return P.op(eng, lambda v, bk=bk: v.tensor_copy(dst_tile_ap, bk[0:128, 0:W]),
                    reads=[bkb], writes=[dst_buf])

    rot = Rot(PSB)

    def f32res(name, src, shape, es=None):
        t, b = P.sb(name, shape, F32, es=es)
        P.op("Y", lambda e, t=t, s=src: e.dma_start(out=t[:, :], in_=s[0][:, :]),
             reads=[src[1]], writes=[b], dma=True)
        return t, b

    r_n1w = f32res("r_n1w", n1w, [128, KD])
    r_n2w = f32res("r_n2w", n2w, [128, KD])
    r_bip = f32res("r_bip", b_ip, [128, 2 * KI])
    r_cwa = f32res("r_cwa", cwa, [128, KI * 3])
    r_cba = f32res("r_cba", cba, [128, KI])
    r_bdtc = f32res("r_bdtc", b_dtc, [128, 3])
    r_bs2i = f32res("r_bs2i", b_s2i, [128, KI])
    r_dpar = f32res("r_dpar", dpar, [128, KI])
    r_boa = f32res("r_boa", b_oa, [128, KD])
    r_odd = f32res("r_odd", oddmask, [128, 1])
    r_coh, b_coh = P.sb("r_coh", [8, 1], F32)
    P.op("Y", lambda e: e.dma_start(out=r_coh[:, :], in_=coreoh[0][:, :]),
         reads=[coreoh[1]], writes=[b_coh], dma=True)
    rowt = P.sb("rowt", [1, 6 * TT], F32)
    r_idn, b_idn = P.sb("r_idn", [128, 128], F32)
    P.op("Y", lambda e: e.dma_start(out=r_idn[:, :], in_=idn[0][:, :]),
         reads=[idn[1]], writes=[b_idn], dma=True)
    ROW = rowt[0]
    ROWB = rowt[1]

    def rows(sl):
        return ROW[0:1, sl]

    esA = ExitStack()
    wip, b_wip = P.sb("wip", [128, KD * 2 * INNER], BF16, es=esA)
    P.op("P", lambda e: e.dma_start(out=wip[:, :].rearrange("p (k m) -> p k m", k=KD),
                                    in_=dr3(w_ip[0])),
         reads=[w_ip[1]], writes=[b_wip], dma=True)
    wdt, b_wdt = P.sb("wdt", [128, KI * NA], BF16, es=esA)
    wbp, b_wbp = P.sb("wbp", [128, KI * NA], BF16, es=esA)
    wcp, b_wcp = P.sb("wcp", [128, KI * NA], BF16, es=esA)
    for (t_, tb_, src_) in ((wdt, b_wdt, w_dt), (wbp, b_wbp, w_bp), (wcp, b_wcp, w_cp)):
        P.op("P", lambda e, t=t_, s=src_, _ki=KI: e.dma_start(
            out=t[:, :].rearrange("p (k m) -> p k m", k=_ki), in_=dr3(s[0])),
            reads=[src_[1]], writes=[tb_], dma=True)

    xt = [P.sb("xt%d" % i, [128, KD * TT], F32, es=esA) for i in range(2)]
    r1t = [P.sb("r1t%d" % i, [128, KD * TT], BF16, es=esA) for i in range(2)]
    sqt = P.sb("sqt", [128, TT], F32, es=esA)
    rstdb = P.sb("rstdb", [128, TT], F32, es=esA)
    ab = P.sb("ab", [128, TT], F32, es=esA)
    nrb = P.sb("nrb", [128, TT], F32, es=esA)
    xzt = [P.sb("xzt%d" % i, [128, TT], F32, es=esA) for i in range(3)]
    xmbt = [P.sb("xmbt%d" % i, [128, KI * TT], BF16, es=esA) for i in range(2)]
    cvt = [P.sb("cvt%d" % i, [128, TT], F32, es=esA) for i in range(2)]
    halo_keep = P.sb("halo_keep", [128, KI * 2], F32, es=esA)
    xmf = P.sb("xmf", [128, TT], F32, es=esA)

    def stats_pair(x_tile, xb, width, sq_scr, do_sum=True, nk=KD):
        ret_s = None
        if do_sum:
            bkS, bkSb = rot()
            mm_group(bkS[0:1, 0:width], bkSb,
                     [(onesc[:, :], x_tile[:, k * width:(k + 1) * width]) for k in range(nk)],
                     [b_onesc, xb])
            ret_s = (bkS, bkSb)
        bkQ, bkQb = rot()
        for k in range(nk):
            xk = x_tile[:, k * width:(k + 1) * width]
            P.op("S", lambda s, xk=xk, w=width: s.activation(sq_scr[0][:, 0:w], xk, AF.Square),
                 reads=[xb], writes=[sq_scr[1]])
            P.op("T", [lambda t, kk=k, bkQ=bkQ, w=width: t.matmul(
                bkQ[0:1, 0:w], onesc[:, :], sq_scr[0][:, 0:w],
                start=(kk == 0), stop=(kk == nk - 1))],
                reads=[b_onesc, sq_scr[1]], writes=[bkQb])
        return ret_s, (bkQ, bkQb)

    def chunkA(tc):
        W = 2 if tc == -1 else TT
        ib = tc % 2
        xT, xB = xt[ib]
        rT, rB = r1t[ib]
        if tc == -1:
            dma_in("Y", fm3(xT[:, 0:KD * 2], KD), xB, dr3(halo_x[0]), halo_x[1])
        else:
            dma_in("Y", fm3(xT[:, :], KD), xB,
                   dr3(x_fm[0][:, tc * TT:(tc + 1) * TT]), x_fm[1])
        (bkS, bkSb), (bkQ, bkQb) = stats_pair(xT, xB, W, sqt)
        mu = rows(slice(0, W))
        msq = rows(slice(TT, TT + W))
        var = rows(slice(2 * TT, 2 * TT + W))
        rstd = rows(slice(3 * TT, 3 * TT + W))
        arow = rows(slice(4 * TT, 4 * TT + W))
        P.op("V", lambda v: v.tensor_scalar(mu, bkS[0:1, 0:W], 1.0 / D, None, ALU.mult),
             reads=[bkSb], writes=[ROWB])
        P.op("V", lambda v: v.tensor_scalar(msq, bkQ[0:1, 0:W], 1.0 / D, None, ALU.mult),
             reads=[bkQb], writes=[ROWB])
        P.op("S", lambda s: s.activation(var, mu, AF.Square), reads=[ROWB], writes=[ROWB])
        P.op("V", lambda v: v.tensor_sub(var, msq, var), reads=[ROWB], writes=[ROWB])
        P.op("S", lambda s: s.activation(rstd, var, AF.Sqrt, bias=c_eps[0:1, 0:1]),
             reads=[ROWB, b_ceps], writes=[ROWB])
        P.op("V", lambda v: v.reciprocal(rstd, rstd), reads=[ROWB], writes=[ROWB])
        P.op("V", lambda v: v.tensor_mul(arow, mu, rstd), reads=[ROWB], writes=[ROWB])
        bcast(rot, rstd, ROWB, rstdb[0][:, 0:W], rstdb[1], W)
        bcast(rot, arow, ROWB, ab[0][:, 0:W], ab[1], W)
        for k in range(KD):
            xk = xT[:, k * W:(k + 1) * W]
            P.op("V", [lambda v, xk=xk: v.tensor_mul(xk, xk, rstdb[0][:, 0:W]),
                       lambda v, xk=xk: v.tensor_sub(xk, xk, ab[0][:, 0:W])],
                 reads=[xB, rstdb[1], ab[1]], writes=[xB])
        if tc >= 0:
            dma_out("Y", dr3(s_u[0][:, tc * TT:(tc + 1) * TT]),
                    s_u[1], fm3(xT[:, :], KD), xB)
        _, (bkR, bkRb) = stats_pair(xT, xB, W, sqt, do_sum=False)
        nr = rows(slice(5 * TT, 5 * TT + W))
        P.op("V", lambda v: v.tensor_scalar(nr, bkR[0:1, 0:W], 1.0 / D, None, ALU.mult),
             reads=[bkRb], writes=[ROWB])
        P.op("S", lambda s: s.activation(nr, nr, AF.Sqrt, bias=c_eps[0:1, 1:2]),
             reads=[ROWB, b_ceps], writes=[ROWB])
        P.op("V", lambda v: v.reciprocal(nr, nr), reads=[ROWB], writes=[ROWB])
        bcast(rot, nr, ROWB, nrb[0][:, 0:W], nrb[1], W)
        for k in range(KD):
            xk = xT[:, k * W:(k + 1) * W]
            rk = rT[:, k * W:(k + 1) * W]
            n1k = r_n1w[0][:, k:k + 1]
            P.op("V", [lambda v, xk=xk: v.tensor_mul(sqt[0][:, 0:W], xk, nrb[0][:, 0:W]),
                       lambda v, rk=rk, n1k=n1k: v.tensor_scalar_mul(rk, sqt[0][:, 0:W], n1k)],
                 reads=[xB, nrb[1], r_n1w[1]], writes=[rB, sqt[1]])
        xmbT, xmbB = xmbt[ib]
        MLIM = KI if tc == -1 else 2 * KI
        for m in range(MLIM):
            bk, bkb = rot()
            bko = bk[0:128, 0:W]
            mm_group(bko, bkb,
                     [(wip[:, (k * 2 * INNER) + m * 128:(k * 2 * INNER) + (m + 1) * 128],
                       rT[:, k * W:(k + 1) * W]) for k in range(KD)],
                     [b_wip, rB])
            zT, zB = xzt[m % 3]
            zo = zT[:, 0:W]
            bipc = r_bip[0][:, m:m + 1]
            P.op("V", lambda v, zo=zo, bko=bko, bipc=bipc: v.tensor_scalar(
                zo, bko, bipc, None, ALU.add), reads=[bkb, r_bip[1]], writes=[zB])
            if tc == -1:
                P.op("V", lambda v, zo=zo: v.tensor_scalar_mul(zo, zo, r_odd[0][:, :]),
                     reads=[zB, r_odd[1]], writes=[zB])
                hk = halo_keep[0][:, 2 * m:2 * m + 2]
                P.op("V", lambda v, hk=hk, zo=zo: v.tensor_copy(hk, zo),
                     reads=[zB], writes=[halo_keep[1]])
                continue
            if m >= KI:
                mg = m - KI
                dma_out("Y", s_gate[0][mg * 128:(mg + 1) * 128, tc * TT:(tc + 1) * TT],
                        s_gate[1], zo, zB)
                continue
            cT, cB = cvt[m % 2]
            co = cT[:, 0:W]
            hk = halo_keep[0][:, 2 * m:2 * m + 2]
            w0 = r_cwa[0][:, 3 * m:3 * m + 1]
            w1 = r_cwa[0][:, 3 * m + 1:3 * m + 2]
            w2 = r_cwa[0][:, 3 * m + 2:3 * m + 3]
            P.op("V", lambda v, co=co, zo=zo, w2=w2: v.tensor_scalar_mul(co, zo, w2),
                 reads=[zB, r_cwa[1]], writes=[cB])
            P.op("V", lambda v, co=co, zo=zo, w1=w1: v.scalar_tensor_tensor(
                co[:, 1:W], zo[:, 0:W - 1], w1, co[:, 1:W], ALU.mult, ALU.add),
                reads=[zB, r_cwa[1]], writes=[cB])
            P.op("V", lambda v, co=co, zo=zo, w0=w0: v.scalar_tensor_tensor(
                co[:, 2:W], zo[:, 0:W - 2], w0, co[:, 2:W], ALU.mult, ALU.add),
                reads=[zB, r_cwa[1]], writes=[cB])
            P.op("V", lambda v, co=co, hk=hk, w1=w1: v.scalar_tensor_tensor(
                co[:, 0:1], hk[:, 1:2], w1, co[:, 0:1], ALU.mult, ALU.add),
                reads=[halo_keep[1], r_cwa[1]], writes=[cB])
            P.op("V", lambda v, co=co, hk=hk, w0=w0: v.scalar_tensor_tensor(
                co[:, 0:2], hk[:, 0:2], w0, co[:, 0:2], ALU.mult, ALU.add),
                reads=[halo_keep[1], r_cwa[1]], writes=[cB])
            P.op("V", lambda v, hk=hk, zo=zo: v.tensor_copy(hk, zo[:, W - 2:W]),
                 reads=[zB], writes=[halo_keep[1]])
            cbk = r_cba[0][:, m:m + 1]
            P.op("S", lambda s, co=co, cbk=cbk: s.activation(
                xmf[0][:, 0:W], co, AF.Silu, bias=cbk), reads=[cB, r_cba[1]], writes=[xmf[1]])
            xmbk = xmbT[:, m * W:(m + 1) * W]
            P.op("V", lambda v, xmbk=xmbk: v.tensor_copy(xmbk, xmf[0][:, 0:W]),
                 reads=[xmf[1]], writes=[xmbB])
            dma_out("Y", s_xm[0][m * 128:(m + 1) * 128, tc * TT:(tc + 1) * TT],
                    s_xm[1], xmf[0][:, 0:W], xmf[1])
        if tc == -1:
            return
        for j, (wtile, wbuf, sig) in enumerate((
                (wdt, b_wdt, True), (wbp, b_wbp, False), (wcp, b_wcp, False))):
            bk, bkb = rot()
            bko = bk[0:64, 0:TT]
            mm_group(bko, bkb,
                     [(wtile[:, k * NA:(k + 1) * NA], xmbT[:, k * TT:(k + 1) * TT])
                      for k in range(KI)],
                     [wbuf, xmbB])
            zT, zB = xzt[j % 3]
            dst = zT[0:64, 0:TT]
            bc = r_bdtc[0][0:64, j:j + 1]
            if sig:
                P.op("S", lambda s, dst=dst, bko=bko, bc=bc: s.activation(
                    dst, bko, AF.Sigmoid, bias=bc), reads=[bkb, r_bdtc[1]], writes=[zB])
            else:
                P.op("V", lambda v, dst=dst, bko=bko, bc=bc: v.tensor_scalar(
                    dst, bko, bc, None, ALU.add), reads=[bkb, r_bdtc[1]], writes=[zB])
            dma_out("Y", s_dtb[0][0:NA, j * Tc + tc * TT: j * Tc + (tc + 1) * TT],
                    s_dtb[1], dst, zB)

    for tc in range(-1, NCH):
        chunkA(tc)

    P.barrier()
    esA.close()

    esC = ExitStack()
    hlo, b_hlo = P.sb("hlo", [64, Tc], F32, es=esC)
    ppr, b_ppr = P.sb("ppr", [64, Tc], F32, es=esC)
    sdt = [P.sb("sdt%d" % i, [64, 2 * TT], F32, es=esC) for i in range(2)]
    zrt = P.sb("zrt", [64, TT], F32, es=esC)
    sin, b_sin = P.sb("sin", [64, 1], F32, es=esC)
    P.op("V", lambda v: v.memset(zrt[0][:, :], 0.0), writes=[zrt[1]])

    for tc in range(NCH):
        ib = tc % 2
        dT, dB = sdt[ib]
        dma_in("Y", dT[:, 0:TT], dB, s_dtb[0][0:NA, tc * TT:(tc + 1) * TT], s_dtb[1])
        dma_in("Y", dT[:, TT:2 * TT], dB, s_dtb[0][0:NA, Tc + tc * TT: Tc + (tc + 1) * TT],
               s_dtb[1])
        P.op("V", lambda v, dT=dT: v.tensor_mul(dT[:, TT:2 * TT], dT[:, 0:TT], dT[:, TT:2 * TT]),
             reads=[dB], writes=[dB])
        P.op("V", lambda v, dT=dT: v.tensor_scalar(dT[:, 0:TT], dT[:, 0:TT], 1.0, -1.0,
                                                   ALU.subtract, ALU.mult),
             reads=[dB], writes=[dB])
        hcol = hlo[:, tc * TT:(tc + 1) * TT]
        pcol = ppr[:, tc * TT:(tc + 1) * TT]
        hprev = 0.0 if tc == 0 else hlo[:, tc * TT - 1: tc * TT]
        pprev = 1.0 if tc == 0 else ppr[:, tc * TT - 1: tc * TT]
        P.op("V", lambda v, dT=dT, hcol=hcol, hprev=hprev: v.tensor_tensor_scan(
            hcol, dT[:, 0:TT], dT[:, TT:2 * TT], hprev, ALU.mult, ALU.add),
            reads=[dB, b_hlo], writes=[b_hlo])
        P.op("V", lambda v, dT=dT, pcol=pcol, pprev=pprev: v.tensor_tensor_scan(
            pcol, dT[:, 0:TT], zrt[0][:, :], pprev, ALU.mult, ALU.add),
            reads=[dB, zrt[1], b_ppr], writes=[b_ppr])

    dma_out("P", agh_in[0][:, :], agh_in[1], hlo[:, Tc - 1:Tc], b_hlo)
    P.op("P", lambda g: g.collective_compute(
        "AllGather", ALU.bypass, replica_groups=[list(range(NCORES))],
        ins=[agh_in[0].opt()], outs=[agh_out[0].opt()]),
        reads=[agh_in[1]], writes=[agh_out[1]], coll=True)
    aght, b_aght = P.sb("aght", [8, NA], F32, es=esC)
    dma_in("Y", aght[:, :], b_aght, agh_out[0][:, :], agh_out[1])
    bkC, bkCb = rot()
    mm_group(bkC[0:1, 0:NA], bkCb, [(r_coh[:, :], aght[:, :])], [b_coh, b_aght])
    sinrow, b_sinrow = P.sb("sinrow", [1, NA], F32, es=esC)
    P.op("V", lambda v: v.tensor_copy(sinrow[:, :], bkC[0:1, 0:NA]), reads=[bkCb],
         writes=[b_sinrow])
    dma_out("P", sin_rt[0][:, :], sin_rt[1], sinrow[:, :], b_sinrow)
    dma_in("Y", sin[:, :], b_sin, sin_rt[0][:, :], sin_rt[1])
    P.op("V", lambda v: v.scalar_tensor_tensor(hlo[:, :], ppr[:, :], sin[:, :], hlo[:, :],
                                               ALU.mult, ALU.add),
         reads=[b_ppr, b_sin, b_hlo], writes=[b_hlo])

    woa, b_woa = P.sb("woa", [128, KI * D], BF16, es=esC)
    P.op("P", lambda e: e.dma_start(out=woa[:, :].rearrange("p (k m) -> p k m", k=KI),
                                    in_=dr3(w_oa[0])),
         reads=[w_oa[1]], writes=[b_woa], dma=True)
    ws2i, b_ws2i = P.sb("ws2i", [64, INNER], BF16, es=esC)
    P.op("P", lambda e: e.dma_start(out=ws2i[:, :], in_=w_s2i[0][:, :]),
         reads=[w_s2i[1]], writes=[b_ws2i], dma=True)

    sq2 = P.sb("sq2", [128, TT], F32, es=esC)
    ynb = P.sb("ynb", [64, TT], BF16, es=esC)
    rst2 = P.sb("rst2", [64, TT], F32, es=esC)
    ab2 = P.sb("ab2", [64, TT], F32, es=esC)
    yst = [P.sb("yst%d" % i, [64, TT], F32, es=esC) for i in range(2)]
    xm1 = [P.sb("xm1_%d" % i, [128, TT], F32, es=esC) for i in range(2)]
    gt1 = [P.sb("gt1_%d" % i, [128, TT], F32, es=esC) for i in range(2)]
    ud1 = [P.sb("ud1_%d" % i, [128, TT], F32, es=esC) for i in range(2)]
    gab = [P.sb("gab%d" % i, [128, KI * TT], BF16, es=esC) for i in range(2)]
    u1t = [P.sb("u1t%d" % i, [128, KD * TT], F32, es=esC) for i in range(2)]
    sg2 = P.sb("sg2", [128, TT], F32, es=esC)

    def chunkC(tc):
        ib = tc % 2
        ycol = slice(tc * TT, (tc + 1) * TT)
        yT, yB = yst[ib]
        dT, dB = sdt[ib]
        dma_in("Y", dT[:, 0:TT], dB, s_dtb[0][0:NA, 2 * Tc + tc * TT: 2 * Tc + (tc + 1) * TT],
               s_dtb[1])
        P.op("V", lambda v, yT=yT, dT=dT: v.tensor_mul(yT[:, :], dT[:, 0:TT], hlo[:, ycol]),
             reads=[dB, b_hlo], writes=[yB])
        bkS, bkSb = rot()
        bkQ, bkQb = rot()
        mm_group(bkS[0:1, 0:TT], bkSb, [(onesc[0:64, :], yT[:, :])], [b_onesc, yB])
        P.op("S", lambda s, yT=yT: s.activation(sq2[0][0:64, 0:TT], yT[:, :], AF.Square),
             reads=[yB], writes=[sq2[1]])
        mm_group(bkQ[0:1, 0:TT], bkQb, [(onesc[0:64, :], sq2[0][0:64, 0:TT])],
                 [b_onesc, sq2[1]])
        mu = rows(slice(0, TT))
        msq = rows(slice(TT, 2 * TT))
        var = rows(slice(2 * TT, 3 * TT))
        rstd = rows(slice(3 * TT, 4 * TT))
        arow = rows(slice(4 * TT, 5 * TT))
        P.op("V", lambda v: v.tensor_scalar(mu, bkS[0:1, 0:TT], 1.0 / NA, None, ALU.mult),
             reads=[bkSb], writes=[ROWB])
        P.op("V", lambda v: v.tensor_scalar(msq, bkQ[0:1, 0:TT], 1.0 / NA, None, ALU.mult),
             reads=[bkQb], writes=[ROWB])
        P.op("S", lambda s: s.activation(var, mu, AF.Square), reads=[ROWB], writes=[ROWB])
        P.op("V", lambda v: v.tensor_sub(var, msq, var), reads=[ROWB], writes=[ROWB])
        P.op("S", lambda s: s.activation(rstd, var, AF.Sqrt, bias=c_eps[0:1, 0:1]),
             reads=[ROWB, b_ceps], writes=[ROWB])
        P.op("V", lambda v: v.reciprocal(rstd, rstd), reads=[ROWB], writes=[ROWB])
        P.op("V", lambda v: v.tensor_mul(arow, mu, rstd), reads=[ROWB], writes=[ROWB])
        bkA, bkAb = rot()
        mm_group(bkA[0:128, 0:TT], bkAb, [(ones1[:, :], rstd)], [b_ones1, ROWB])
        P.op("V", lambda v, bkA=bkA: v.tensor_copy(rst2[0][:, :], bkA[0:64, 0:TT]),
             reads=[bkAb], writes=[rst2[1]])
        bkB, bkBb = rot()
        mm_group(bkB[0:128, 0:TT], bkBb, [(ones1[:, :], arow)], [b_ones1, ROWB])
        P.op("V", lambda v, bkB=bkB: v.tensor_copy(ab2[0][:, :], bkB[0:64, 0:TT]),
             reads=[bkBb], writes=[ab2[1]])
        P.op("V", [lambda v, yT=yT: v.tensor_mul(sq2[0][0:64, 0:TT], yT[:, :], rst2[0][:, :]),
                   lambda v: v.tensor_sub(ynb[0][:, :], sq2[0][0:64, 0:TT], ab2[0][:, :])],
             reads=[yB, rst2[1], ab2[1]], writes=[ynb[1], sq2[1]])
        gaT, gaB = gab[ib]
        for m in range(KI):
            bk, bkb = rot()
            bko = bk[0:128, 0:TT]
            mm_group(bko, bkb, [(ws2i[:, m * 128:(m + 1) * 128], ynb[0][:, :])],
                     [b_ws2i, ynb[1]])
            xmT, xmB = xm1[m % 2]
            gtT, gtB = gt1[m % 2]
            dma_in("Y", xmT[:, :], xmB,
                   s_xm[0][m * 128:(m + 1) * 128, ycol], s_xm[1])
            dma_in("Y", gtT[:, :], gtB,
                   s_gate[0][m * 128:(m + 1) * 128, ycol], s_gate[1])
            gak = gaT[:, m * TT:(m + 1) * TT]
            bsk = r_bs2i[0][:, m:m + 1]
            dpk = r_dpar[0][:, m:m + 1]
            P.op("S", lambda s, gtT=gtT: s.activation(sg2[0][:, 0:TT], gtT[:, :], AF.Sigmoid),
                 reads=[gtB], writes=[sg2[1]])
            P.op("V", [
                lambda v, bko=bko, bsk=bsk: v.tensor_scalar(
                    sq2[0][:, 0:TT], bko, bsk, None, ALU.add),
                lambda v, xmT=xmT, dpk=dpk: v.scalar_tensor_tensor(
                    sq2[0][:, 0:TT], xmT[:, :], dpk, sq2[0][:, 0:TT], ALU.mult, ALU.add),
                lambda v, gak=gak: v.tensor_mul(gak, sq2[0][:, 0:TT], sg2[0][:, 0:TT])],
                reads=[bkb, r_bs2i[1], r_dpar[1], xmB, sg2[1]], writes=[gaB, sq2[1]])
        u1T, u1B = u1t[ib]
        for m in range(KD):
            bk, bkb = rot()
            bko = bk[0:128, 0:TT]
            mm_group(bko, bkb,
                     [(woa[:, k * D + m * 128: k * D + (m + 1) * 128],
                       gaT[:, k * TT:(k + 1) * TT]) for k in range(KI)],
                     [b_woa, gaB])
            uT, uB = ud1[m % 2]
            dma_in("Y", uT[:, :], uB, s_u[0][m * 128:(m + 1) * 128, ycol], s_u[1])
            u1k = u1T[:, m * TT:(m + 1) * TT]
            bok = r_boa[0][:, m:m + 1]
            P.op("V", [
                lambda v, u1k=u1k, bko=bko, bok=bok: v.tensor_scalar(
                    u1k, bko, bok, None, ALU.add),
                lambda v, u1k=u1k, uT=uT: v.tensor_add(u1k, u1k, uT[:, :])],
                reads=[bkb, r_boa[1], uB], writes=[u1B])
        dma_out("Y", dr3(s_u1[0][:, ycol]), s_u1[1], fm3(u1T[:, :], KD), u1B)

    for tc in range(NCH):
        chunkC(tc)

    P.barrier()
    esC.close()

    HID = cfg["HID"]
    KH = HID // 128
    TD = max(64, TT // 2)
    NCD = Tc // TD
    esDW = ExitStack()
    r_eb1, b_eb1 = P.sb("r_eb1", [128, 4 * KH], F32, es=esDW)
    P.op("Y", lambda e: e.dma_start(out=r_eb1[:, :], in_=e_b1[0][:, :]),
         reads=[e_b1[1]], writes=[b_eb1], dma=True)
    r_sel4, b_sel4 = P.sb("r_sel4", [4, 512], F32, es=esDW)
    P.op("Y", lambda e: e.dma_start(out=r_sel4[:, :], in_=sel4[0][:, :]),
         reads=[sel4[1]], writes=[b_sel4], dma=True)
    wfm, b_wfm = P.sb("wfm", [4, Tc], F32, es=esDW)
    esD = ExitStack()
    wga, b_wga = P.sb("wga", [128, KD * 4], F32, es=esD)
    P.op("Y", lambda e: e.dma_start(out=wga[:, :].rearrange("p (k m) -> p k m", k=KD),
                                    in_=dr3(w_ga[0])),
         reads=[w_ga[1]], writes=[b_wga], dma=True)
    r_gab, b_gab_r = P.sb("r_gab", [128, 4], F32, es=esD)
    P.op("Y", lambda e: e.dma_start(out=r_gab[:, :], in_=b_gab[0][:, :]),
         reads=[b_gab[1]], writes=[b_gab_r], dma=True)
    r_eb2, b_eb2 = P.sb("r_eb2", [4, D], F32, es=esD)
    P.op("Y", lambda e: e.dma_start(out=r_eb2[:, :], in_=e_b2[0][:, :]),
         reads=[e_b2[1]], writes=[b_eb2], dma=True)

    u1d = [P.sb("u1d%d" % i, [128, KD * TD], F32, es=esD) for i in range(2)]
    xd = [P.sb("xd%d" % i, [128, KD * TD], F32, es=esD) for i in range(2)]
    r2d = [P.sb("r2d%d" % i, [128, KD * TD], BF16, es=esD) for i in range(2)]
    r2f = [P.sb("r2f%d" % i, [128, KD * TD], F32, es=esD) for i in range(2)]
    sqd = P.sb("sqd", [128, TD], F32, es=esD)
    nrbd = P.sb("nrbd", [128, TD], F32, es=esD)
    lgt = P.sb("lgt", [128, 8], F32, es=esD)
    m8t = P.sb("m8t", [128, 8], F32, es=esD)
    ohd = P.sb("ohd", [128, 4], F32, es=esD)
    wfd = P.sb("wfd", [128, 4], F32, es=esD)
    wrow = P.sb("wrow", [128, 2], F32, es=esD)

    P.op("V", lambda v: v.memset(lgt[0][:, 4:8], -1e30), writes=[lgt[1]])

    def stats_sq_only(x_tile, xb, width, nk):
        bkQ, bkQb = rot()
        for k in range(nk):
            xk = x_tile[:, k * width:(k + 1) * width]
            P.op("S", lambda s, xk=xk: s.activation(sqd[0][:, 0:width], xk, AF.Square),
                 reads=[xb], writes=[sqd[1]])
            P.op("T", [lambda t, kk=k: t.matmul(
                bkQ[0:1, 0:width], onesc[:, :], sqd[0][:, 0:width],
                start=(kk == 0), stop=(kk == nk - 1))],
                reads=[b_onesc, sqd[1]], writes=[bkQb])
        return bkQ, bkQb

    def chunkD0(tc):
        ib = tc % 2
        col = slice(tc * TD, (tc + 1) * TD)
        u1T, u1B = u1d[ib]
        xT, xB = xd[ib]
        r2T, r2B = r2d[ib]
        rfT, rfB = r2f[ib]
        dma_in("Y", fm3(u1T[:, :], KD), u1B, dr3(s_u1[0][:, col]), s_u1[1])
        dma_in("Y", fm3(xT[:, :], KD), xB, dr3(x_fm[0][:, col]), x_fm[1])
        bkR, bkRb = stats_sq_only(u1T, u1B, TD, KD)
        nr = rows(slice(0, TD))
        P.op("V", lambda v: v.tensor_scalar(nr, bkR[0:1, 0:TD], 1.0 / D, None, ALU.mult),
             reads=[bkRb], writes=[ROWB])
        P.op("S", lambda s: s.activation(nr, nr, AF.Sqrt, bias=c_eps[0:1, 1:2]),
             reads=[ROWB, b_ceps], writes=[ROWB])
        P.op("V", lambda v: v.reciprocal(nr, nr), reads=[ROWB], writes=[ROWB])
        bcast(rot, nr, ROWB, nrbd[0][:, 0:TD], nrbd[1], TD)
        for k in range(KD):
            uk = u1T[:, k * TD:(k + 1) * TD]
            rk = r2T[:, k * TD:(k + 1) * TD]
            rfk = rfT[:, k * TD:(k + 1) * TD]
            n2k = r_n2w[0][:, k:k + 1]
            P.op("V", [lambda v, uk=uk: v.tensor_mul(sqd[0][:, 0:TD], uk, nrbd[0][:, 0:TD]),
                       lambda v, rfk=rfk, n2k=n2k: v.tensor_scalar_mul(rfk, sqd[0][:, 0:TD], n2k),
                       lambda v, rk=rk, rfk=rfk: v.tensor_copy(rk, rfk)],
                 reads=[u1B, nrbd[1], r_n2w[1]], writes=[r2B, rfB, sqd[1]])
        dma_out("Y", dr3(s_r2b[0][:, col]), s_r2b[1], fm3(r2T[:, :], KD), r2B)
        TB = min(128, TD)
        for tb in range(TD // TB):
            bkL, bkLb = rot()
            mm_group(bkL[0:TB, 0:4], bkLb,
                     [(rfT[:, k * TD + tb * TB: k * TD + (tb + 1) * TB],
                       wga[:, k * 4:(k + 1) * 4]) for k in range(KD)],
                     [rfB, b_wga])
            P.op("V", lambda v, bkL=bkL: v.tensor_tensor(lgt[0][0:TB, 0:4], bkL[0:TB, 0:4],
                                                r_gab[0:TB, :], ALU.add),
                 reads=[bkLb, b_gab_r], writes=[lgt[1]])
            P.op("V", lambda v: v.max(out=m8t[0][0:TB, :], in_=lgt[0][0:TB, :]),
                 reads=[lgt[1]], writes=[m8t[1]])
            P.op("V", lambda v: v.tensor_sub(wrow[0][0:TB, 0:1], m8t[0][0:TB, 0:1], m8t[0][0:TB, 1:2]),
                 reads=[m8t[1]], writes=[wrow[1]])
            P.op("S", lambda s: s.activation(wrow[0][0:TB, 0:1], wrow[0][0:TB, 0:1], AF.Sigmoid),
                 reads=[wrow[1]], writes=[wrow[1]])
            P.op("V", lambda v: v.tensor_scalar(wrow[0][0:TB, 1:2], wrow[0][0:TB, 0:1], -1.0, 1.0,
                                                ALU.mult, ALU.add),
                 reads=[wrow[1]], writes=[wrow[1]])
            P.op("V", lambda v: v.tensor_scalar(ohd[0][0:TB, :], lgt[0][0:TB, 0:4],
                                                m8t[0][0:TB, 0:1], 1.0, ALU.is_equal, ALU.mult),
                 reads=[lgt[1], m8t[1]], writes=[ohd[1]])
            P.op("V", lambda v: v.tensor_scalar_mul(wfd[0][0:TB, :], ohd[0][0:TB, :], wrow[0][0:TB, 0:1]),
                 reads=[ohd[1], wrow[1]], writes=[wfd[1]])
            P.op("V", lambda v: v.tensor_scalar(ohd[0][0:TB, :], lgt[0][0:TB, 0:4],
                                                m8t[0][0:TB, 1:2], 1.0, ALU.is_equal, ALU.mult),
                 reads=[lgt[1], m8t[1]], writes=[ohd[1]])
            P.op("V", lambda v: v.scalar_tensor_tensor(wfd[0][0:TB, :], ohd[0][0:TB, :],
                                                       wrow[0][0:TB, 1:2], wfd[0][0:TB, :],
                                                       ALU.mult, ALU.add),
                 reads=[ohd[1], wrow[1]], writes=[wfd[1]])
            bkT, bkTb = rot()
            P.op("T", lambda t, bkT=bkT: t.transpose(bkT[0:4, 0:TB], wfd[0][0:TB, 0:4], r_idn[0:TB, 0:TB]),
                 reads=[wfd[1], b_idn], writes=[bkTb])
            wcolm = wfm[0:4, tc * TD + tb * TB: tc * TD + (tb + 1) * TB]
            P.op("V", lambda v, wcolm=wcolm, bkT=bkT: v.tensor_copy(wcolm, bkT[0:4, 0:TB]),
                 reads=[bkTb], writes=[b_wfm])
        for m in range(KD):
            bkA, bkAb = rot()
            mm_group(bkA[0:128, 0:TD], bkAb,
                     [(r_eb2[:, m * 128:(m + 1) * 128], wfm[0:4, col])],
                     [b_eb2, b_wfm])
            um = u1T[:, m * TD:(m + 1) * TD]
            xm_ = xT[:, m * TD:(m + 1) * TD]
            P.op("V", [lambda v, xm_=xm_, um=um: v.tensor_add(xm_, xm_, um),
                       lambda v, xm_=xm_, bkA=bkA: v.tensor_add(xm_, xm_, bkA[0:128, 0:TD])],
                 reads=[bkAb, u1B], writes=[xB])
        dma_out("Y", dr3(s_x1[0][:, col]), s_x1[1], fm3(xT[:, :], KD), xB)

    for tc in range(NCD):
        chunkD0(tc)

    P.barrier()
    esD.close()
    TDE = TT
    NCE = Tc // TDE
    esD2 = ExitStack()
    wei, b_wei = P.sb("wei", [128, KD * HID], BF16, es=esD2)
    hidt = [P.sb("hidt%d" % i, [128, KH * TDE], BF16, es=esD2) for i in range(2)]
    wbc = P.sb("wbc", [128, TDE], BF16, es=esD2)
    r2e = [P.sb("r2e%d" % i, [128, KD * TDE], BF16, es=esD2) for i in range(2)]
    acce1 = P.sb("acce", [128, KD * TDE], F32, es=esD2)

    def expertD(e):
        P.op("P", lambda g, e=e: g.dma_start(
            out=wei[:, :].rearrange("p (k m) -> p k m", k=KD),
            in_=dr3(e_w1[0][e * D:(e + 1) * D, :])),
            reads=[e_w1[1]], writes=[b_wei], dma=True)
        for tc in range(NCE):
            ib = tc % 2
            col = slice(tc * TDE, (tc + 1) * TDE)
            r2T, r2B = r2e[ib]
            hT, hB = hidt[ib]
            dma_in("Y", fm3(r2T[:, :], KD), r2B, dr3(s_r2b[0][:, col]), s_r2b[1])
            bkW, bkWb = rot()
            mm_group(bkW[0:128, 0:TDE], bkWb,
                     [(r_sel4[0:4, e * 128:(e + 1) * 128], wfm[0:4, col])],
                     [b_sel4, b_wfm])
            P.op("V", lambda v, bkW=bkW: v.tensor_copy(wbc[0][:, 0:TDE], bkW[0:128, 0:TDE]),
                 reads=[bkWb], writes=[wbc[1]])
            for m in range(KH):
                bk, bkb = rot()
                mm_group(bk[0:128, 0:TDE], bkb,
                         [(wei[:, k * HID + m * 128: k * HID + (m + 1) * 128],
                           r2T[:, k * TDE:(k + 1) * TDE]) for k in range(KD)],
                         [b_wei, r2B])
                hm = hT[:, m * TDE:(m + 1) * TDE]
                eb1c = r_eb1[:, e * KH + m: e * KH + m + 1]
                P.op("S", lambda s, hm=hm, eb1c=eb1c, bk=bk: s.activation(
                    hm, bk[0:128, 0:TDE], AF.Gelu, bias=eb1c),
                    reads=[bkb, b_eb1], writes=[hB])
                P.op("V", lambda v, hm=hm: v.tensor_mul(hm, hm, wbc[0][:, 0:TDE]),
                     reads=[wbc[1]], writes=[hB])
            dma_out("Y", dr3(s_hid[0][:, col]), s_hid[1], fm3(hT[:, :], KH), hB)
        P.op("P", lambda g, e=e: g.dma_start(
            out=wei[:, 0:KH * D].rearrange("p (k m) -> p k m", k=KH),
            in_=dr3(e_w2[0][e * HID:(e + 1) * HID, :])),
            reads=[e_w2[1]], writes=[b_wei], dma=True)
        for tc in range(NCE):
            ib = tc % 2
            col = slice(tc * TDE, (tc + 1) * TDE)
            hT, hB = hidt[ib]
            aT, aB = acce1
            dma_in("Y", fm3(hT[:, :], KH), hB, dr3(s_hid[0][:, col]), s_hid[1])
            dma_in("Y", fm3(aT[:, :], KD), aB, dr3(s_x1[0][:, col]), s_x1[1])
            for m in range(KD):
                bk, bkb = rot()
                mm_group(bk[0:128, 0:TDE], bkb,
                         [(wei[:, k * D + m * 128: k * D + (m + 1) * 128],
                           hT[:, k * TDE:(k + 1) * TDE]) for k in range(KH)],
                         [b_wei, hB])
                am = aT[:, m * TDE:(m + 1) * TDE]
                P.op("V", lambda v, am=am, bk=bk: v.tensor_add(am, am, bk[0:128, 0:TDE]),
                     reads=[bkb], writes=[aB])
            dma_out("Y", dr3(s_x1[0][:, col]), s_x1[1], fm3(aT[:, :], KD), aB)

    for e in range(4):
        expertD(e)

    P.barrier()
    esD2.close()
    esDW.close()

    r_ecol = f32res("r_ecol", ecol, [128, 1])
    r_cbase = f32res("r_cbase", cbase, [128, 1])
    r_qiw = f32res("r_qiw", qiw, [128, 2])
    NR_BIS = 28
    rg_lin = nc.gpsimd.alloc_register()
    rg_hall = nc.gpsimd.alloc_register()
    rg_lrow = nc.gpsimd.alloc_register()
    rg_eall = nc.gpsimd.alloc_register()
    rg_t6 = nc.gpsimd.alloc_register()
    P.op("P", [lambda g: g.reg_mov(rg_lin, NE * 128 * G - 1),
               lambda g: g.reg_mov(rg_hall, NCORES * Tc - 1),
               lambda g: g.reg_mov(rg_lrow, NE * G - 1),
               lambda g: g.reg_mov(rg_eall, NE * CAPP - 1),
               lambda g: g.reg_mov(rg_t6, 4 * NJ - 1)], quiet=True)
    TFC = 128 * max(d for d in (4, 2, 1) if G % d == 0)
    NCF = CAPP // TFC
    assert NCF * TFC == CAPP
    KF = DFF // 128

    def ffn_phase(x_in, x_out, tag):
        import os
        STOP = os.environ.get("FFN_STOP", "")

        def _dodump(dump):
            dmap = {"vtok": (f_vtok, BF16, Tc), "hall": (f_hall, BF16, NCORES * Tc),
                    "ei": (f_ei, BF16, CAPP), "eo": (f_eo, BF16, CAPP),
                    "eall": (f_eall, BF16, NE * CAPP)}
            src_t, sdt, nrows = dmap[dump]
            dbg = outp("dbg", [4096, D])
            esdbg = ExitStack()
            cpd = P.sb("cpdbg", [128, D], F32, es=esdbg)
            for i in range(32):
                dma_in("P", cpd[0][:, :], cpd[1],
                       src_t[0][(i * 128) % nrows:(i * 128) % nrows + 128, :], src_t[1])
                dma_out("Y", dbg[0][i * 128:(i + 1) * 128, :], dbg[1], cpd[0][:, :], cpd[1])
            P.barrier()
            esdbg.close()

        def stopper(point):
            if STOP == point:
                if os.environ.get("FFN_DUMP", "") and tag == "f1_":
                    _dodump(os.environ["FFN_DUMP"])
                cpt = P.sb(tag + "cpt" + point, [128, KD * 128], F32, es=esE)
                for tb2 in range(Tc // 128):
                    tc2 = slice(tb2 * 128, (tb2 + 1) * 128)
                    dma_in("Y", fm3(cpt[0][:, :], KD), cpt[1], dr3(x_in[0][:, tc2]), x_in[1])
                    dma_out("Y", dr3(x_out[0][:, tc2]), x_out[1], fm3(cpt[0][:, :], KD), cpt[1])
                P.barrier()
                return True
            return False
        esE = ExitStack()
        sbE = lambda n, sh, dt: P.sb(tag + n, sh, dt, es=esE)
        iop = sbE("iop", [128, 4], F32)
        idxs = sbE("idxs", [128, 4], I32)
        owns = sbE("owns", [128, 128], F32)
        ownc = sbE("ownc", [128, 128], F32)
        ownT = sbE("ownT", [128, 256], F32)
        ownci = sbE("ownci", [128, 128], I32)
        tokv = sbE("tokv", [128, JW], F32)
        lstf = sbE("lstf", [128, G], F32)
        lsti = sbE("lsti", [128, G], I32)
        lstrw = sbE("lstrw", [128, 128], F32)
        zb16 = sbE("zb16", [128, D], BF16)
        weif, b_weif = sbE("weif", [128, KD * DFF], BF16)
        gtile = [P.sb(tag + "gt%d" % i, [128, D], BF16, es=esE) for i in range(2)]
        es1 = ExitStack()
        sb = lambda n, sh, dt: P.sb(tag + n, sh, dt, es=es1)
        wrt, b_wrt = sb("wrt", [128, KD * NE], F32)
        P.op("Y", lambda e: e.dma_start(out=wrt[:, :].rearrange("p (k m) -> p k m", k=KD),
                                        in_=dr3(w_rt[0])),
             reads=[w_rt[1]], writes=[b_wrt], dma=True)
        xv = [sb("xv%d" % i, [128, KD * 128], F32) for i in range(2)]
        vtk = [sb("vtk%d" % i, [128, KD * 128], BF16) for i in range(2)]
        sqe = sb("sqe", [128, 128], F32)
        rse = sb("rse", [128, 128], F32)
        aae = sb("aae", [128, 128], F32)
        lgte = sb("lgte", [128, NE], F32)
        m8e = sb("m8e", [128, 8], F32)
        ohe = sb("ohe", [128, NE], F32)
        wre = sb("wre", [128, 8], F32)
        rte = sb("rte", [128, 4], F32)
        iota8 = sb("iota8", [128, 8], F32)
        P.op("P", lambda g: g.iota(iota8[0][:, :], [[1, 8]], channel_multiplier=0,
                                   allow_small_or_imprecise_dtypes=True), writes=[iota8[1]])
        NB = Tc // 128

        def e1_block(tb):
            ib = tb % 2
            xT, xB = xv[ib]
            vT, vB = vtk[ib]
            tcol = slice(tb * 128, (tb + 1) * 128)
            dma_in("Y", fm3(xT[:, :], KD), xB, dr3(x_in[0][:, tcol]), x_in[1])
            if os.environ.get("E1DBG", "") == "xraw":
                dma_out("Y", f_vtok[0][tcol, :], f_vtok[1], fm3(xT[:, :], KD), xB)
            (bkS, bkSb), (bkQ, bkQb) = stats_pair(xT, xB, 128, sqt_e)
            mu = rows(slice(0, 128))
            msq = rows(slice(TT, TT + 128))
            var = rows(slice(2 * TT, 2 * TT + 128))
            rstd = rows(slice(3 * TT, 3 * TT + 128))
            arow = rows(slice(4 * TT, 4 * TT + 128))
            P.op("V", lambda v: v.tensor_scalar(mu, bkS[0:1, 0:128], 1.0 / D, None, ALU.mult),
                 reads=[bkSb], writes=[ROWB])
            P.op("V", lambda v: v.tensor_scalar(msq, bkQ[0:1, 0:128], 1.0 / D, None, ALU.mult),
                 reads=[bkQb], writes=[ROWB])
            P.op("S", lambda s: s.activation(var, mu, AF.Square), reads=[ROWB], writes=[ROWB])
            P.op("V", lambda v: v.tensor_sub(var, msq, var), reads=[ROWB], writes=[ROWB])
            P.op("S", lambda s: s.activation(rstd, var, AF.Sqrt, bias=c_eps[0:1, 0:1]),
                 reads=[ROWB, b_ceps], writes=[ROWB])
            P.op("V", lambda v: v.reciprocal(rstd, rstd), reads=[ROWB], writes=[ROWB])
            P.op("V", lambda v: v.tensor_mul(arow, mu, rstd), reads=[ROWB], writes=[ROWB])
            if os.environ.get("E1DBG3") == "rows":
                for seg in range(6):
                    dma_out("Y", f_t6[0][tb * 8 + seg: tb * 8 + seg + 1, 0:128],
                            f_t6[1], ROW[0:1, seg * TT: seg * TT + 128], ROWB)
            bcast(rot, rstd, ROWB, rse[0][:, :], rse[1], 128)
            bcast(rot, arow, ROWB, aae[0][:, :], aae[1], 128)
            for k in range(KD):
                xk = xT[:, k * 128:(k + 1) * 128]
                vk = vT[:, k * 128:(k + 1) * 128]
                P.op("V", [lambda v, xk=xk: v.tensor_mul(xk, xk, rse[0][:, :]),
                           lambda v, xk=xk: v.tensor_sub(xk, xk, aae[0][:, :]),
                           lambda v, xk=xk, vk=vk: v.tensor_copy(vk, xk)],
                     reads=[xB, rse[1], aae[1]], writes=[xB, vB])
            for k in ([] if os.environ.get("E1DBG") == "notr" else range(KD)):
                xk = xT[:, k * 128:(k + 1) * 128]
                bkT_, bkTb_ = rot()
                P.op("T", lambda t, xk=xk, bkT_=bkT_: t.transpose(
                    bkT_[0:128, 0:128], xk, r_idn[:, :]), reads=[xB, b_idn], writes=[bkTb_])
                trk = vtr[0][:, k * 128:(k + 1) * 128]
                P.op("V", lambda v, trk=trk, bkT_=bkT_: v.tensor_copy(trk, bkT_[0:128, 0:128]),
                     reads=[bkTb_], writes=[vtr[1]])
            if os.environ.get("E1DBG", "") in ("lnfm", "notr"):
                dma_out("Y", f_vtok[0][tcol, :], f_vtok[1],
                        fm3(vT[:, :], KD), vB)
            else:
                dma_out("Y", f_vtok[0][tcol, :], f_vtok[1],
                        vtr[0][:, :].rearrange("p (k t) -> p k t", k=KD), vtr[1])
            bkL, bkLb = rot()
            mm_group(bkL[0:128, 0:NE], bkLb,
                     [(xT[:, k * 128:(k + 1) * 128], wrt[:, k * NE:(k + 1) * NE])
                      for k in range(KD)],
                     [xB, b_wrt])
            P.op("V", lambda v: v.tensor_copy(lgte[0][:, :], bkL[0:128, 0:NE]),
                 reads=[bkLb], writes=[lgte[1]])
            P.op("V", lambda v: v.max(out=m8e[0][:, :], in_=lgte[0][:, :]),
                 reads=[lgte[1]], writes=[m8e[1]])
            P.op("V", lambda v: v.tensor_scalar(ohe[0][:, :], lgte[0][:, :],
                                                m8e[0][:, 0:1], 1.0, ALU.is_equal, ALU.mult),
                 reads=[lgte[1], m8e[1]], writes=[ohe[1]])
            P.op("V", lambda v: v.tensor_mul(ohe[0][:, :], ohe[0][:, :], iota8[0][:, :]),
                 reads=[ohe[1], iota8[1]], writes=[ohe[1]])
            P.op("V", lambda v: v.reduce_sum(rte[0][:, 0:1], ohe[0][:, :], axis=AX.X),
                 reads=[ohe[1]], writes=[rte[1]])
            P.op("V", lambda v: v.tensor_scalar(ohe[0][:, :], lgte[0][:, :],
                                                m8e[0][:, 1:2], 1.0, ALU.is_equal, ALU.mult),
                 reads=[lgte[1], m8e[1]], writes=[ohe[1]])
            P.op("V", lambda v: v.tensor_mul(ohe[0][:, :], ohe[0][:, :], iota8[0][:, :]),
                 reads=[ohe[1], iota8[1]], writes=[ohe[1]])
            P.op("V", lambda v: v.reduce_sum(rte[0][:, 1:2], ohe[0][:, :], axis=AX.X),
                 reads=[ohe[1]], writes=[rte[1]])
            P.op("V", lambda v: v.tensor_sub(wre[0][:, 0:1], m8e[0][:, 0:1], m8e[0][:, 1:2]),
                 reads=[m8e[1]], writes=[wre[1]])
            P.op("S", lambda s: s.activation(wre[0][:, 1:2], wre[0][:, 0:1], AF.Sigmoid),
                 reads=[wre[1]], writes=[wre[1]])
            P.op("S", lambda s: s.activation(wre[0][:, 2:3], wre[0][:, 0:1], AF.Sigmoid,
                                             scale=-1.0),
                 reads=[wre[1]], writes=[wre[1]])
            P.op("V", lambda v: v.tensor_add(wre[0][:, 3:4], wre[0][:, 1:2], wre[0][:, 2:3]),
                 reads=[wre[1]], writes=[wre[1]])
            P.op("V", lambda v: v.tensor_scalar(wre[0][:, 3:4], wre[0][:, 3:4], MOE_EPS, None,
                                                ALU.max), reads=[wre[1]], writes=[wre[1]])
            P.op("V", lambda v: v.reciprocal(wre[0][:, 3:4], wre[0][:, 3:4]),
                 reads=[wre[1]], writes=[wre[1]])
            P.op("V", lambda v: v.tensor_mul(rte[0][:, 2:3], wre[0][:, 1:2], wre[0][:, 3:4]),
                 reads=[wre[1]], writes=[rte[1]])
            P.op("V", lambda v: v.tensor_mul(rte[0][:, 3:4], wre[0][:, 2:3], wre[0][:, 3:4]),
                 reads=[wre[1]], writes=[rte[1]])
            dma_out("Y", f_rtl[0][tcol, :], f_rtl[1], rte[0][:, :], rte[1])
        for tb in range(int(os.environ.get("E1NB", NB))):
            e1_block(tb)

        P.barrier()
        es1.close()
        es3 = ExitStack()
        sb = lambda n, sh, dt: P.sb(tag + n, sh, dt, es=es3)
        if stopper("e1"):
            return
        P.op("P", lambda g: g.collective_compute(
            "AllGather", ALU.bypass, replica_groups=[list(range(NCORES))],
            ins=[f_vtok[0].opt()], outs=[f_hall[0].opt()]),
            reads=[f_vtok[1]], writes=[f_hall[1]], coll=True)
        P.op("P", lambda g: g.collective_compute(
            "AllGather", ALU.bypass, replica_groups=[list(range(NCORES))],
            ins=[f_rtl[0].opt()], outs=[f_rta[0].opt()]),
            reads=[f_rtl[1]], writes=[f_rta[1]], coll=True)
        if stopper("e2"):
            return
        rta = sb("rta", [128, NJ * 4], F32)
        dma_in("Y", rta[0][:, :].rearrange("p (j c) -> p j c", c=4), rta[1],
               f_rta[0].rearrange("(j p) c -> p j c", p=128), f_rta[1])
        rta3 = rta[0][:, :].rearrange("p (j c) -> p j c", c=4)
        eA1, eA2 = rta3[:, :, 0], rta3[:, :, 1]
        wA1, wA2 = rta3[:, :, 2], rta3[:, :, 3]
        wee = [sb("wee%d" % e, [128, 2 * NJ], F32) for e in range(NE)]
        msku = sb("msku", [128, 2 * NJ], U8)
        neg1 = sb("neg1", [128, NJ], F32)
        onesb = sb("onesb", [128, 2 * NJ], F32)
        P.op("V", lambda v: v.memset(neg1[0][:, :], -1.0), writes=[neg1[1]])
        P.op("V", lambda v: v.memset(onesb[0][:, :], 1.0), writes=[onesb[1]])
        for e in range(NE):
            weT, weB = wee[e]
            for ki, (eA, wA) in enumerate(((eA1, wA1), (eA2, wA2))):
                half = weT[:, ki * NJ:(ki + 1) * NJ]
                mh = msku[0][:, ki * NJ:(ki + 1) * NJ]
                P.op("V", lambda v, eA=eA, e=e, mh=mh: v.tensor_scalar(
                    mh, eA, float(e), 1.0, ALU.is_equal, ALU.mult),
                    reads=[rta[1]], writes=[msku[1]])
                P.op("V", lambda v, half=half, mh=mh, wA=wA: v.select(
                    half, mh, wA, neg1[0][:, :], add_drain=True),
                    reads=[msku[1], rta[1], neg1[1]], writes=[weB])
        if stopper("e3b"):
            return
        lohi = sb("lohi", [1, 24], F32)
        caprow = sb("caprow", [1, 8], F32)
        gmask = sb("gmask", [1, 8], U8)
        cc8 = sb("cc8", [128, 8], F32)
        inds = sb("inds", [128, 2 * NJ], F32)
        thbc = sb("thbc", [128, 8], F32)
        LO = lohi[0][0:1, 0:8]
        HI = lohi[0][0:1, 8:16]
        MID = lohi[0][0:1, 16:24]
        P.op("V", lambda v: v.memset(LO, -1.0), writes=[lohi[1]])
        P.op("V", lambda v: v.memset(HI, 1.0), writes=[lohi[1]])
        P.op("V", lambda v: v.memset(MID, 0.0), writes=[lohi[1]])
        P.op("V", lambda v: v.memset(caprow[0][:, :], float(CAP)), writes=[caprow[1]])
        for r in range(NR_BIS):
            bkM, bkMb = rot()
            mm_group(bkM[0:128, 0:8], bkMb, [(ones1[:, :], MID)], [b_ones1, lohi[1]])
            P.op("V", lambda v, bkM=bkM: v.tensor_copy(thbc[0][:, :], bkM[0:128, 0:8]),
                 reads=[bkMb], writes=[thbc[1]])
            for e in range(NE):
                weT, weB = wee[e]
                P.op("V", lambda v, weT=weT, e=e: v.tensor_scalar(
                    inds[0][:, :], weT[:, :], thbc[0][:, e:e + 1], 1.0, ALU.is_gt, ALU.mult),
                    reads=[weB, thbc[1]], writes=[inds[1]])
                P.op("V", lambda v, e=e: v.reduce_sum(cc8[0][:, e:e + 1], inds[0][:, :],
                                                      axis=AX.X),
                     reads=[inds[1]], writes=[cc8[1]])
            bkN, bkNb = rot()
            mm_group(bkN[0:1, 0:8], bkNb, [(onesc[:, :], cc8[0][:, :])], [b_onesc, cc8[1]])
            P.op("V", lambda v, bkN=bkN: v.tensor_tensor(gmask[0][:, :], bkN[0:1, 0:8],
                                                         caprow[0][:, :], ALU.is_gt),
                 reads=[bkNb, caprow[1]], writes=[gmask[1]])
            P.op("V", lambda v: v.select(LO, gmask[0][:, :], MID, LO, add_drain=True),
                 reads=[gmask[1], lohi[1]], writes=[lohi[1]])
            P.op("V", lambda v: v.select(HI, gmask[0][:, :], HI, MID, add_drain=True),
                 reads=[gmask[1], lohi[1]], writes=[lohi[1]])
            P.op("V", lambda v: v.tensor_add(MID, LO, HI), reads=[lohi[1]], writes=[lohi[1]])
            P.op("V", lambda v: v.tensor_scalar(MID, MID, 0.5, None, ALU.mult),
                 reads=[lohi[1]], writes=[lohi[1]])
        bkM, bkMb = rot()
        mm_group(bkM[0:128, 0:8], bkMb, [(ones1[:, :], HI)], [b_ones1, lohi[1]])
        P.op("V", lambda v, bkM=bkM: v.tensor_copy(thbc[0][:, :], bkM[0:128, 0:8]),
             reads=[bkMb], writes=[thbc[1]])
        tha = sb("tha", [128, 2 * NJ], F32)
        kw = sb("kw", [128, 2 * NJ], F32)
        tws = sb("tws", [128, NJ], F32)
        cadr = sb("cadr", [128, 2 * NJ], F32)
        sele = sb("sele", [128, 2 * NJ], F32)
        cume = sb("cume", [128, 2 * NJ], F32)
        scr1 = sb("scr1", [128, 2 * NJ], F32)
        rowc = sb("rowc", [1, 3 * 128], F32)
        colx = sb("colx", [128, 2], F32)
        P.op("V", lambda v: v.memset(tha[0][:, :], 0.0), writes=[tha[1]])
        for e in range(NE):
            for ki, eA in enumerate((eA1, eA2)):
                hh = tha[0][:, ki * NJ:(ki + 1) * NJ]
                P.op("V", lambda v, eA=eA, e=e: v.tensor_scalar(
                    scr1[0][:, 0:NJ], eA, float(e), 1.0, ALU.is_equal, ALU.mult),
                    reads=[rta[1]], writes=[scr1[1]])
                P.op("V", lambda v, hh=hh, e=e: v.scalar_tensor_tensor(
                    hh, scr1[0][:, 0:NJ], thbc[0][:, e:e + 1], hh, ALU.mult, ALU.add),
                    reads=[scr1[1], thbc[1]], writes=[tha[1]])
        for ki, wA in enumerate((wA1, wA2)):
            hh = kw[0][:, ki * NJ:(ki + 1) * NJ]
            th = tha[0][:, ki * NJ:(ki + 1) * NJ]
            P.op("V", lambda v, hh=hh, wA=wA, th=th: v.tensor_tensor(hh, wA, th, ALU.is_gt),
                 reads=[rta[1], tha[1]], writes=[kw[1]])
            P.op("V", lambda v, hh=hh, wA=wA: v.tensor_mul(hh, hh, wA),
                 reads=[rta[1]], writes=[kw[1]])
        P.op("V", lambda v: v.tensor_add(tws[0][:, :], kw[0][:, 0:NJ], kw[0][:, NJ:2 * NJ]),
             reads=[kw[1]], writes=[tws[1]])
        P.op("V", lambda v: v.tensor_scalar(tws[0][:, :], tws[0][:, :], MOE_EPS, None, ALU.max),
             reads=[tws[1]], writes=[tws[1]])
        P.op("V", lambda v: v.reciprocal(tws[0][:, :], tws[0][:, :]),
             reads=[tws[1]], writes=[tws[1]])
        for ki in range(2):
            hh = kw[0][:, ki * NJ:(ki + 1) * NJ]
            P.op("V", lambda v, hh=hh: v.tensor_mul(hh, hh, tws[0][:, :]),
                 reads=[tws[1]], writes=[kw[1]])
        P.op("V", lambda v: v.memset(cadr[0][:, :], float(NE * CAPP)), writes=[cadr[1]])
        for e in range(NE):
            weT, weB = wee[e]
            P.op("V", lambda v, weT=weT, e=e: v.tensor_scalar(
                sele[0][:, :], weT[:, :], thbc[0][:, e:e + 1], 1.0, ALU.is_gt, ALU.mult),
                reads=[weB, thbc[1]], writes=[sele[1]])
            P.op("V", lambda v: v.tensor_tensor_scan(
                cume[0][:, :], onesb[0][:, :], sele[0][:, :], 0.0, ALU.mult, ALU.add),
                reads=[onesb[1], sele[1]], writes=[cume[1]])
            bkX, bkXb = rot()
            P.op("T", lambda t, bkX=bkX: t.transpose(
                bkX[0:1, 0:128], cume[0][:, 2 * NJ - 1:2 * NJ], r_idn[:, :]),
                reads=[cume[1], b_idn], writes=[bkXb])
            R1 = rowc[0][0:1, 0:128]
            R2 = rowc[0][0:1, 128:256]
            P.op("V", lambda v, bkX=bkX: v.tensor_copy(R1, bkX[0:1, 0:128]),
                 reads=[bkXb], writes=[rowc[1]])
            P.op("V", lambda v: v.tensor_tensor_scan(R2, ones1[:, :], R1, 0.0,
                                                     ALU.mult, ALU.add),
                 reads=[rowc[1], b_ones1], writes=[rowc[1]])
            P.op("V", lambda v: v.tensor_sub(R2, R2, R1), reads=[rowc[1]], writes=[rowc[1]])
            bkY, bkYb = rot()
            P.op("T", lambda t, bkY=bkY: t.transpose(bkY[0:128, 0:1], R2, r_idn[0:1, 0:1]),
                 reads=[rowc[1], b_idn], writes=[bkYb])
            P.op("V", lambda v, bkY=bkY: v.tensor_copy(colx[0][:, 0:1], bkY[0:128, 0:1]),
                 reads=[bkYb], writes=[colx[1]])
            P.op("V", lambda v: v.tensor_sub(cume[0][:, :], cume[0][:, :], sele[0][:, :]),
                 reads=[sele[1]], writes=[cume[1]])
            P.op("V", lambda v: v.tensor_scalar(cume[0][:, :], cume[0][:, :],
                                                colx[0][:, 0:1], None, ALU.add),
                 reads=[colx[1]], writes=[cume[1]])
            P.op("V", lambda v, e=e: v.tensor_scalar(
                inds[0][:, :], cume[0][:, :], float(e * CAPP), None, ALU.add),
                reads=[cume[1]], writes=[inds[1]])
            P.op("V", lambda v: v.tensor_scalar(msku[0][:, :], sele[0][:, :], 0.5, None,
                                                ALU.is_gt), reads=[sele[1]], writes=[msku[1]])
            P.op("V", lambda v: v.select(cadr[0][:, :], msku[0][:, :], inds[0][:, :],
                                         cadr[0][:, :], add_drain=True),
                 reads=[msku[1], inds[1]], writes=[cadr[1]])
        NJB = (NJ + 127) // 128
        quants = [(cadr, 0, 0), (cadr, NJ, 1), (kw, 0, 2), (kw, NJ, 3)]
        ttr = sb("ttr", [128, 128], F32)
        for (tile, off, q) in quants:
            for jb in range(NJB):
                w = min(128, NJ - jb * 128)
                srcs = tile[0][:, off + jb * 128: off + jb * 128 + w]
                bkT_, bkTb_ = rot()
                P.op("T", lambda t, srcs=srcs, w=w, bkT_=bkT_: t.transpose(
                    bkT_[0:w, 0:128], srcs, r_idn[:, :]), reads=[tile[1], b_idn],
                    writes=[bkTb_])
                P.op("V", lambda v, w=w, bkT_=bkT_: v.tensor_copy(
                    ttr[0][0:w, :], bkT_[0:w, 0:128]), reads=[bkTb_], writes=[ttr[1]])
                dma_out("Y", f_t6[0][q * NJ + jb * 128: q * NJ + jb * 128 + w, :],
                        f_t6[1], ttr[0][0:w, :], ttr[1])
        P.op("P", lambda g: g.iota(iop[0][:, 0:1], [[0, 1]], channel_multiplier=1,
                                   allow_small_or_imprecise_dtypes=True), writes=[iop[1]])
        P.op("V", lambda v: v.tensor_copy(iop[0][:, 1:2], r_qiw[0][:, 0:1]),
             reads=[r_qiw[1]], writes=[iop[1]])
        P.op("V", lambda v: v.tensor_scalar(iop[0][:, 3:4], r_qiw[0][:, 1:2], float(NJ), None,
                                            ALU.mult), reads=[r_qiw[1]], writes=[iop[1]])
        P.op("V", lambda v: v.scalar_tensor_tensor(
            iop[0][:, 3:4], r_ecol[0][:, :], float(JW), iop[0][:, 3:4], ALU.mult, ALU.add),
            reads=[r_ecol[1], iop[1]], writes=[iop[1]])
        P.op("V", lambda v: v.tensor_add(iop[0][:, 3:4], iop[0][:, 3:4], iop[0][:, 1:2]),
             reads=[iop[1]], writes=[iop[1]])
        dma_in("P", idxs[0][:, 0:1], idxs[1], iop[0][:, 3:4], iop[1])
        P.op("P", lambda g: g.indirect_dma_start(
            out=owns[0][:, :], out_offset=None, in_=f_t6[0][:, :],
            in_offset=IndirectOffsetOnAxis(ap=idxs[0][:, 0:1], axis=0),
            bounds_check=rg_t6, oob_is_err=False),
            reads=[f_t6[1], idxs[1]], writes=[owns[1]], dma=True)
        bkT_, bkTb_ = rot()
        P.op("T", lambda t, bkT_=bkT_: t.transpose(
            bkT_[0:128, 0:128], owns[0][:, :], r_idn[:, :]),
            reads=[owns[1], b_idn], writes=[bkTb_])
        P.op("V", lambda v, bkT_=bkT_: v.tensor_copy(
            ownT[0][:, 0:128], bkT_[0:128, 0:128]),
            reads=[bkTb_], writes=[ownT[1]])
        dma_in("P", ownci[0][:, 0:2 * JW], ownci[1], ownT[0][:, 0:2 * JW], ownT[1])
        if stopper("e3"):
            return
        big9 = sb("big9", [128, G], F32)
        P.op("V", lambda v: v.memset(big9[0][:, :], float(NCORES * Tc)), writes=[big9[1]])
        for e in range(NE):
            dma_out("Y", f_lin[0][e * 128 * G:(e + 1) * 128 * G, :].rearrange(
                "(p g) one -> p (g one)", p=128), f_lin[1], big9[0][:, :], big9[1])
        P.op("P", lambda g: g.iota(tokv[0][:, :], [[128, JW]], channel_multiplier=1,
                                   allow_small_or_imprecise_dtypes=True), writes=[tokv[1]])
        P.op("V", lambda v: v.tensor_scalar(tokv[0][:, :], tokv[0][:, :],
                                            r_cbase[0][:, :], None, ALU.add),
             reads=[r_cbase[1]], writes=[tokv[1]])
        for ki in range(2):
            for i in range(JW):
                icol = ownci[0][:, ki * JW + i: ki * JW + i + 1]
                vcol = tokv[0][:, i:i + 1]
                P.op("P", lambda g, icol=icol, vcol=vcol: g.indirect_dma_start(
                    out=f_lin[0][:, :], out_offset=IndirectOffsetOnAxis(ap=icol, axis=0),
                    in_=vcol, in_offset=None,
                    bounds_check=rg_lin, oob_is_err=False),
                    reads=[ownci[1], tokv[1], f_lin[1]], writes=[f_lin[1]], dma=True)
        P.op("P", lambda g: g.collective_compute(
            "AllReduce", ALU.min, replica_groups=[list(range(NCORES))],
            ins=[f_lin[0].opt()], outs=[f_lout[0].opt()]),
            reads=[f_lin[1]], writes=[f_lout[1]], coll=True)
        if stopper("e4a"):
            return
        P.op("V", lambda v: v.tensor_scalar(iop[0][:, 3:4], iop[0][:, 0:1],
                                            float(G - 1), None, ALU.min),
             reads=[iop[1]], writes=[iop[1]])
        P.op("V", lambda v: v.scalar_tensor_tensor(
            iop[0][:, 3:4], r_ecol[0][:, :], float(G), iop[0][:, 3:4], ALU.mult, ALU.add),
            reads=[r_ecol[1], iop[1]], writes=[iop[1]])
        dma_in("P", idxs[0][:, 2:3], idxs[1], iop[0][:, 3:4], iop[1])
        P.op("P", lambda g: g.indirect_dma_start(
            out=lstrw[0][:, :], out_offset=None,
            in_=f_lout[0].rearrange("(r t) one -> r (t one)", t=128),
            in_offset=IndirectOffsetOnAxis(ap=idxs[0][:, 2:3], axis=0),
            bounds_check=rg_lrow, oob_is_err=False),
            reads=[f_lout[1], idxs[1]], writes=[lstrw[1]], dma=True)
        bkT_, bkTb_ = rot()
        P.op("T", lambda t, bkT_=bkT_: t.transpose(
            bkT_[0:128, 0:128], lstrw[0][:, :], r_idn[:, :]),
            reads=[lstrw[1], b_idn], writes=[bkTb_])
        P.op("V", lambda v, bkT_=bkT_: v.tensor_copy(lstf[0][:, :], bkT_[0:128, 0:G]),
             reads=[bkTb_], writes=[lstf[1]])
        dma_in("P", lsti[0][:, :], lsti[1], lstf[0][:, :], lstf[1])
        P.op("V", lambda v: v.memset(zb16[0][:, :], 0.0), writes=[zb16[1]])
        for g_ in range(G):
            dma_out("Y", f_ei[0][g_ * 128:(g_ + 1) * 128, :], f_ei[1],
                    zb16[0][:, :], zb16[1])
        P.op("V", lambda v: v.memset(gtile[0][0][:, :], 0.0), writes=[gtile[0][1]])
        P.op("V", lambda v: v.memset(gtile[1][0][:, :], 0.0), writes=[gtile[1][1]])
        for g_ in range(G):
            icol = lsti[0][:, g_:g_ + 1]
            gT, gB = gtile[g_ % 2]
            P.op("P", lambda g, icol=icol, gT=gT: g.indirect_dma_start(
                out=gT[:, :], out_offset=None,
                in_=f_hall[0][:, :],
                in_offset=IndirectOffsetOnAxis(ap=icol, axis=0),
                bounds_check=rg_hall, oob_is_err=False),
                reads=[f_hall[1], lsti[1]], writes=[gB], dma=True)
            dma_out("Y", f_ei[0][g_ * 128:(g_ + 1) * 128, :], f_ei[1], gT[:, :], gB)
        if stopper("e4"):
            return
        P.barrier()
        es3.close()
        es5 = ExitStack()
        sb = lambda n, sh, dt: P.sb(tag + n, sh, dt, es=es5)
        eif = [sb("eif%d" % i, [128, KD * TFC], BF16) for i in range(2)]
        s1o = sb("s1o", [128, KF * TFC], BF16)
        P.op("P", lambda g: g.dma_start(
            out=weif[:, :].rearrange("p (k m) -> p k m", k=KD), in_=dr3(wf1[0])),
            reads=[wf1[1]], writes=[b_weif], dma=True)
        for fc in range(NCF):
            ib = fc % 2
            eT, eB = eif[ib]
            for k in range(KD):
                P.op("Y", lambda e, eT=eT, k=k, fc=fc: e.dma_start_transpose(
                    eT[:, k * TFC:(k + 1) * TFC],
                    f_ei[0][fc * TFC:(fc + 1) * TFC, k * 128:(k + 1) * 128]),
                    reads=[f_ei[1]], writes=[eB], dma=True)
            dma_out("Y", dr3(f_eifm[0][:, fc * TFC:(fc + 1) * TFC]), f_eifm[1],
                    fm3(eT[:, :], KD), eB)
            for m in range(KF):
                bk, bkb = rot()
                mm_group(bk[0:128, 0:TFC], bkb,
                         [(weif[:, k * DFF + m * 128: k * DFF + (m + 1) * 128],
                           eT[:, k * TFC:(k + 1) * TFC]) for k in range(KD)],
                         [b_weif, eB])
                sm = s1o[0][:, m * TFC:(m + 1) * TFC]
                P.op("V", lambda v, sm=sm, bk=bk: v.tensor_copy(sm, bk[0:128, 0:TFC]),
                     reads=[bkb], writes=[s1o[1]])
            dma_out("Y", dr3(f_f1[0][:, fc * TFC:(fc + 1) * TFC]), f_f1[1],
                    fm3(s1o[0][:, :], KF), s1o[1])
        if stopper("e5a"):
            return
        P.op("P", lambda g: g.dma_start(
            out=weif[:, :].rearrange("p (k m) -> p k m", k=KD), in_=dr3(wfg[0])),
            reads=[wfg[1]], writes=[b_weif], dma=True)
        pre_t = sb("pre_t", [128, KF * TFC], BF16)
        for fc in range(NCF):
            ib = fc % 2
            eT, eB = eif[ib]
            dma_in("Y", fm3(eT[:, :], KD), eB, dr3(f_eifm[0][:, fc * TFC:(fc + 1) * TFC]),
                   f_eifm[1])
            dma_in("Y", fm3(pre_t[0][:, :], KF), pre_t[1],
                   dr3(f_f1[0][:, fc * TFC:(fc + 1) * TFC]), f_f1[1])
            for m in range(KF):
                bk, bkb = rot()
                mm_group(bk[0:128, 0:TFC], bkb,
                         [(weif[:, k * DFF + m * 128: k * DFF + (m + 1) * 128],
                           eT[:, k * TFC:(k + 1) * TFC]) for k in range(KD)],
                         [b_weif, eB])
                sm = s1o[0][:, m * TFC:(m + 1) * TFC]
                pm = pre_t[0][:, m * TFC:(m + 1) * TFC]
                P.op("S", lambda s, sm=sm, bk=bk: s.activation(sm, bk[0:128, 0:TFC], AF.Silu),
                     reads=[bkb], writes=[s1o[1]])
                P.op("V", lambda v, sm=sm, pm=pm: v.tensor_mul(sm, sm, pm),
                     reads=[pre_t[1]], writes=[s1o[1]])
            dma_out("Y", dr3(f_f1[0][:, fc * TFC:(fc + 1) * TFC]), f_f1[1],
                    fm3(s1o[0][:, :], KF), s1o[1])
        P.barrier()
        es5.close()
        es5c = ExitStack()
        sb = lambda n, sh, dt: P.sb(tag + n, sh, dt, es=es5c)
        P.op("P", lambda g: g.dma_start(
            out=weif[:, 0:KF * D].rearrange("p (k m) -> p k m", k=KF), in_=dr3(wf2[0])),
            reads=[wf2[1]], writes=[b_weif], dma=True)
        hp = [sb("hp%d" % i, [128, KF * TFC], BF16) for i in range(2)]
        eor = sb("eor", [128, KD * 128], BF16)
        eos = sb("eos", [128, TFC], F32)
        for fc in range(NCF):
            ib = fc % 2
            hT, hB = hp[ib]
            dma_in("Y", fm3(hT[:, :], KF), hB, dr3(f_f1[0][:, fc * TFC:(fc + 1) * TFC]),
                   f_f1[1])
            nblk = TFC // 128
            for m in range(KD):
                bk, bkb = rot()
                mm_group(bk[0:128, 0:TFC], bkb,
                         [(weif[:, k * D + m * 128: k * D + (m + 1) * 128],
                           hT[:, k * TFC:(k + 1) * TFC]) for k in range(KF)],
                         [b_weif, hB])
                P.op("V", lambda v, bk=bk: v.tensor_copy(eos[0][:, 0:TFC], bk[0:128, 0:TFC]),
                     reads=[bkb], writes=[eos[1]])
                for blk in range(nblk):
                    bkT_, bkTb_ = rot()
                    P.op("T", lambda t, blk=blk, bkT_=bkT_: t.transpose(
                        bkT_[0:128, 0:128], eos[0][:, blk * 128:(blk + 1) * 128],
                        r_idn[:, :]), reads=[eos[1], b_idn], writes=[bkTb_])
                    erk = eor[0][:, m * 128:(m + 1) * 128]
                    P.op("V", lambda v, erk=erk, bkT_=bkT_: v.tensor_copy(
                        erk, bkT_[0:128, 0:128]), reads=[bkTb_], writes=[eor[1]])
                    dma_out("Y", f_eo[0][fc * TFC + blk * 128: fc * TFC + (blk + 1) * 128,
                                         m * 128:(m + 1) * 128],
                            f_eo[1], erk, eor[1])
        if stopper("e5c"):
            return
        P.op("P", lambda g: g.collective_compute(
            "AllGather", ALU.bypass, replica_groups=[list(range(NCORES))],
            ins=[f_eo[0].opt()], outs=[f_eall[0].opt()]),
            reads=[f_eo[1]], writes=[f_eall[1]], coll=True)
        if stopper("e6"):
            return
        P.barrier()
        es5c.close()
        es7 = ExitStack()
        sb = lambda n, sh, dt: P.sb(tag + n, sh, dt, es=es7)
        eo1 = sb("eo1", [128, D], BF16)
        eo2 = sb("eo2", [128, D], BF16)
        dcmb = sb("dcmb", [128, D], F32)
        xcmb = [sb("xcmb%d" % i, [128, KD * 128], F32) for i in range(2)]
        P.op("V", lambda v: v.memset(eo1[0][:, :], 0.0), writes=[eo1[1]])
        P.op("V", lambda v: v.memset(eo2[0][:, :], 0.0), writes=[eo2[1]])
        for tb in range(NB):
            ib = tb % 2
            tcol = slice(tb * 128, (tb + 1) * 128)
            a1 = ownci[0][:, tb: tb + 1]
            a2 = ownci[0][:, JW + tb: JW + tb + 1]
            E7M = os.environ.get("E7_MODE", "full")
            if "noind" in E7M:
                dma_in("P", eo1[0][:, :], eo1[1], f_eall[0][0:128, :], f_eall[1])
                dma_in("P", eo2[0][:, :], eo2[1], f_eall[0][0:128, :], f_eall[1])
            else:
                P.op("P", lambda g, a1=a1: g.indirect_dma_start(
                    out=eo1[0][:, :], out_offset=None, in_=f_eall[0][:, :],
                    in_offset=IndirectOffsetOnAxis(ap=a1, axis=0),
                    bounds_check=rg_eall, oob_is_err=False),
                    reads=[f_eall[1], ownci[1], eo1[1]], writes=[eo1[1]], dma=True)
                P.op("P", lambda g, a2=a2: g.indirect_dma_start(
                    out=eo2[0][:, :], out_offset=None, in_=f_eall[0][:, :],
                    in_offset=IndirectOffsetOnAxis(ap=a2, axis=0),
                    bounds_check=rg_eall, oob_is_err=False),
                    reads=[f_eall[1], ownci[1], eo2[1]], writes=[eo2[1]], dma=True)
            k1 = ownT[0][:, 2 * JW + tb: 2 * JW + tb + 1]
            k2 = ownT[0][:, 3 * JW + tb: 3 * JW + tb + 1]
            if "nomix" in os.environ.get("E7_MODE", "full"):
                P.op("V", lambda v: v.tensor_copy(dcmb[0][:, :], eo1[0][:, :]),
                     reads=[eo1[1]], writes=[dcmb[1]])
            else:
                P.op("V", lambda v, k1=k1: v.tensor_scalar_mul(dcmb[0][:, :], eo1[0][:, :], k1),
                     reads=[eo1[1], ownT[1]], writes=[dcmb[1]])
                P.op("V", lambda v, k2=k2: v.scalar_tensor_tensor(
                    dcmb[0][:, :], eo2[0][:, :], k2, dcmb[0][:, :], ALU.mult, ALU.add),
                    reads=[eo2[1], ownT[1]], writes=[dcmb[1]])
            xT, xB = xcmb[ib]
            dma_in("Y", fm3(xT[:, :], KD), xB, dr3(x_in[0][:, tcol]), x_in[1])
            for k in range(KD):
                bkT_, bkTb_ = rot()
                P.op("T", lambda t, k=k, bkT_=bkT_: t.transpose(
                    bkT_[0:128, 0:128], dcmb[0][:, k * 128:(k + 1) * 128], r_idn[:, :]),
                    reads=[dcmb[1], b_idn], writes=[bkTb_])
                xk = xT[:, k * 128:(k + 1) * 128]
                P.op("V", lambda v, xk=xk, bkT_=bkT_: v.tensor_add(
                    xk, xk, bkT_[0:128, 0:128]), reads=[bkTb_], writes=[xB])
            dma_out("Y", dr3(x_out[0][:, tcol]), x_out[1], fm3(xT[:, :], KD), xB)
        P.barrier()
        if os.environ.get("FFN_DUMP", "") and tag == "f1_":
            _dodump(os.environ["FFN_DUMP"])
        es7.close()
        esE.close()

    def attnb_phase():
        esB = ExitStack()
        sb = lambda n, sh, dt: P.sb("b_" + n, sh, dt, es=esB)
        r_cwb = f32res("r_cwb", cwb, [128, KH2 * KB], es=esB)
        r_cbb = f32res("r_cbb", cbb, [128, KH2], es=esB)
        r_bdtb = f32res("r_bdtb", bdtb, [128, KH2], es=esB)
        r_dparb = f32res("r_dparb", dparb, [128, KH2], es=esB)
        r_even = f32res("r_even", evenm, [128, 1], es=esB)
        araw_s = f32res("araw_s", arawt, [128, KH2 * NB_], es=esB)
        acols = sb("acols", [128, KH2 * NB_], F32)
        P.op("S", lambda s: s.activation(acols[0][:, :], araw_s[0][:, :], AF.Exp),
             reads=[araw_s[1]], writes=[acols[1]])
        P.op("V", lambda v: v.tensor_scalar(acols[0][:, :], acols[0][:, :], -1.0, None,
                                            ALU.mult), reads=[acols[1]], writes=[acols[1]])
        NDBC = DTR + 2 * NB_
        esB1 = ExitStack()
        sb = lambda n, sh, dt: P.sb("b_" + n, sh, dt, es=esB1)
        xb0 = [sb("xb0_%d" % i, [128, KD * TT], F32) for i in range(2)]
        lnb0 = [sb("lnb0_%d" % i, [128, KD * TT], BF16) for i in range(2)]
        sqb = sb("sqb", [128, TT], F32)
        rsb = sb("rsb", [128, TT], F32)
        aab = sb("aab", [128, TT], F32)

        def b0_chunk(tc):
            ib = tc % 2
            xT, xB = xb0[ib]
            lT, lB = lnb0[ib]
            col = slice(tc * TT, (tc + 1) * TT)
            dma_in("Y", fm3(xT[:, :], KD), xB, dr3(s_x2[0][:, col]), s_x2[1])
            (bkS, bkSb), (bkQ, bkQb) = stats_pair(xT, xB, TT, sqb)
            mu = rows(slice(0, TT))
            msq = rows(slice(TT, 2 * TT))
            var = rows(slice(2 * TT, 3 * TT))
            rstd = rows(slice(3 * TT, 4 * TT))
            arow = rows(slice(4 * TT, 5 * TT))
            P.op("V", lambda v: v.tensor_scalar(mu, bkS[0:1, 0:TT], 1.0 / D, None, ALU.mult),
                 reads=[bkSb], writes=[ROWB])
            P.op("V", lambda v: v.tensor_scalar(msq, bkQ[0:1, 0:TT], 1.0 / D, None, ALU.mult),
                 reads=[bkQb], writes=[ROWB])
            P.op("S", lambda s: s.activation(var, mu, AF.Square), reads=[ROWB], writes=[ROWB])
            P.op("V", lambda v: v.tensor_sub(var, msq, var), reads=[ROWB], writes=[ROWB])
            P.op("S", lambda s: s.activation(rstd, var, AF.Sqrt, bias=c_eps[0:1, 0:1]),
                 reads=[ROWB, b_ceps], writes=[ROWB])
            P.op("V", lambda v: v.reciprocal(rstd, rstd), reads=[ROWB], writes=[ROWB])
            P.op("V", lambda v: v.tensor_mul(arow, mu, rstd), reads=[ROWB], writes=[ROWB])
            bcast(rot, rstd, ROWB, rsb[0][:, 0:TT], rsb[1], TT)
            bcast(rot, arow, ROWB, aab[0][:, 0:TT], aab[1], TT)
            for k in range(KD):
                xk = xT[:, k * TT:(k + 1) * TT]
                lk = lT[:, k * TT:(k + 1) * TT]
                P.op("V", [lambda v, xk=xk: v.tensor_mul(xk, xk, rsb[0][:, 0:TT]),
                           lambda v, xk=xk: v.tensor_sub(xk, xk, aab[0][:, 0:TT]),
                           lambda v, xk=xk, lk=lk: v.tensor_copy(lk, xk)],
                     reads=[xB, rsb[1], aab[1]], writes=[xB, lB])
            dma_out("Y", dr3(b_lnb[0][:, col]), b_lnb[1], fm3(lT[:, :], KD), lB)

        for tc in range(NCH):
            b0_chunk(tc)
        P.op("P", lambda g: g.collective_compute(
            "AllGather", ALU.bypass, replica_groups=[[0, 1], [2, 3], [4, 5], [6, 7]],
            ins=[b_lnb[0].opt()], outs=[b_lnall[0].opt()]),
            reads=[b_lnb[1]], writes=[b_lnall[1]], coll=True)
        P.barrier()
        esB1.close()
        esB2 = ExitStack()
        sb = lambda n, sh, dt: P.sb("b_" + n, sh, dt, es=esB2)
        winb, b_winb = sb("winb", [128, KD * 2 * DH], BF16)
        P.op("P", lambda e: e.dma_start(out=winb[:, :].rearrange("p (k m) -> p k m", k=KD),
                                        in_=dr3(w_inb[0])),
             reads=[w_inb[1]], writes=[b_winb], dma=True)
        wxb, b_wxb = sb("wxb", [128, KH2 * NDBC], BF16)
        P.op("P", lambda e: e.dma_start(
            out=wxb[:, :].rearrange("p (k m) -> p k m", k=KH2), in_=dr3(w_xb[0])),
            reads=[w_xb[1]], writes=[b_wxb], dma=True)
        wdtb, b_wdtb = sb("wdtb", [DTR, DH], BF16)
        P.op("P", lambda e: e.dma_start(out=wdtb[:, :], in_=w_dtb[0][:, :]),
             reads=[w_dtb[1]], writes=[b_wdtb], dma=True)
        wob, b_wob = sb("wob", [128, KH2 * D], BF16)
        P.op("P", lambda e: e.dma_start(out=wob[:, :].rearrange("p (k m) -> p k m", k=KH2),
                                        in_=dr3(w_ob[0])),
             reads=[w_ob[1]], writes=[b_wob], dma=True)
        xzb = [sb("xzb%d" % i, [128, TT], F32) for i in range(2)]
        cvb = [sb("cvb%d" % i, [128, TT], F32) for i in range(2)]
        hkb = sb("hkb", [128, KH2 * 3], F32)
        xmfb = sb("xmfb", [128, TT], F32)
        lnt = [sb("lnt%d" % i, [128, KD * TT], BF16) for i in range(2)]
        xcbt, b_xcb = sb("xcbt", [128, KH2 * TT], BF16)
        dbt = sb("dbt", [96, TT], F32)
        dtt = [sb("dtt0", [DTR, TT], BF16)]
        brow2 = [sb("brow2_0", [1, TT], F32)]
        bbct, _bb = sb("bbc", [128, NB_ * TT], BF16)
        cbct, _cb = sb("cbc", [128, NB_ * TT], BF16)
        bbcb = [Buf("b_bbc%d" % n) for n in range(NB_)]
        cbcb = [Buf("b_cbc%d" % n) for n in range(NB_)]
        dlt2 = [sb("dlt%d" % i, [128, TT], F32) for i in range(2)]
        spa1 = sb("spa", [128, TT], F32)
        spb1 = sb("spb", [128, TT], F32)
        wdx = sb("wdx", [128, TT], F32)
        xc4 = [sb("xc4_%d" % i, [128, TT], F32) for i in range(2)]
        zt51 = sb("zt5", [128, TT], F32)
        sg51 = sb("sg5", [128, TT], F32)
        NQ = 2
        argq = [sb("argq%d" % i, [128, NQ * TT], F32) for i in range(2)]
        uuq = [sb("uuq%d" % i, [128, NQ * TT], F32) for i in range(2)]
        hht, _hb = sb("hhq", [128, NQ * TT], F32)
        argbq = [[Buf("b_arg%d_%d" % (i, j)) for j in range(NQ)] for i in range(2)]
        uubq = [[Buf("b_uu%d_%d" % (i, j)) for j in range(NQ)] for i in range(2)]
        hhb = [Buf("b_hh%d" % j) for j in range(NQ)]
        yy = sb("yy", [128, TT], F32)
        car4 = sb("car4", [128, KH2 * NB_], F32)
        P.op("V", lambda v: v.memset(car4[0][:, :], 0.0), writes=[car4[1]])
        ygt = [sb("ygt%d" % i, [128, KH2 * TT], BF16) for i in range(2)]
        ob52 = [sb("ob5_%d" % i, [128, TT], BF16) for i in range(2)]
        xf6 = [sb("xf6_0", [128, KD * 128], F32)]
        f06 = [sb("f06_0", [128, 128], F32)]
        f16 = [sb("f16_0", [128, 128], F32)]
        NCB = TB2 // TT

        def stage1(tc):
            ib = tc % 2
            lT, lB = lnt[ib]
            col = slice(tc * TT, (tc + 1) * TT)
            rblk = 0 if tc < NCH else 1
            loc = slice((tc % NCH) * TT, (tc % NCH + 1) * TT)
            dma_in("Y", fm3(lT[:, :], KD), lB,
                   dr3(b_lnall[0][rblk * D:(rblk + 1) * D, loc]), b_lnall[1])
            for m in range(2 * KH2):
                bk, bkb = rot()
                bko = bk[0:128, 0:TT]
                mm_group(bko, bkb,
                         [(winb[:, (k * 2 * DH) + m * 128:(k * 2 * DH) + (m + 1) * 128],
                           lT[:, k * TT:(k + 1) * TT]) for k in range(KD)],
                         [b_winb, lB])
                if m >= KH2:
                    mg = m - KH2
                    zT, zB = xzb[m % 2]
                    zo = zT[:, 0:TT]
                    P.op("S", lambda s, zo=zo, bko=bko: s.activation(zo, bko, AF.Copy),
                         reads=[bkb], writes=[zB])
                    dma_out("Y", b_z[0][mg * 128:(mg + 1) * 128, col], b_z[1], zo, zB)
                    continue
                if tc == 0:
                    P.op("V", lambda v, m=m: v.memset(hkb[0][:, 3 * m:3 * m + 3], 0.0),
                         writes=[hkb[1]])
                cT, cB = cvb[m % 2]
                co = cT[:, 0:TT]
                hk = hkb[0][:, 3 * m:3 * m + 3]
                w3 = [r_cwb[0][:, KB * m + j: KB * m + j + 1] for j in range(KB)]
                P.op("V", lambda v, co=co, bko=bko, w=w3[3]: v.tensor_scalar_mul(co, bko, w),
                     reads=[bkb, r_cwb[1]], writes=[cB])
                for j, sh in ((2, 1), (1, 2), (0, 3)):
                    P.op("V", lambda v, co=co, bko=bko, w=w3[j], sh=sh: v.scalar_tensor_tensor(
                        co[:, sh:TT], bko[:, 0:TT - sh], w, co[:, sh:TT], ALU.mult, ALU.add),
                        reads=[bkb, r_cwb[1]], writes=[cB])
                    P.op("V", lambda v, co=co, hk=hk, w=w3[j], sh=sh: v.scalar_tensor_tensor(
                        co[:, 0:sh], hk[:, 3 - sh:3], w, co[:, 0:sh], ALU.mult, ALU.add),
                        reads=[hkb[1], r_cwb[1]], writes=[cB])
                P.op("V", lambda v, hk=hk, bko=bko: v.tensor_copy(hk, bko[:, TT - 3:TT]),
                     reads=[bkb], writes=[hkb[1]])
                cbk = r_cbb[0][:, m:m + 1]
                P.op("S", lambda s, co=co, cbk=cbk: s.activation(
                    xmfb[0][:, 0:TT], co, AF.Silu, bias=cbk), reads=[cB, r_cbb[1]],
                    writes=[xmfb[1]])
                xck = xcbt[:, m * TT:(m + 1) * TT]
                P.op("S", lambda s, xck=xck: s.activation(xck, xmfb[0][:, 0:TT], AF.Copy),
                     reads=[xmfb[1]], writes=[b_xcb])
                dma_out("Y", b_xc[0][m * 128:(m + 1) * 128, col], b_xc[1],
                        xmfb[0][:, 0:TT], xmfb[1])
            bkD, bkDb = rot()
            mm_group(bkD[0:96, 0:TT], bkDb,
                     [(wxb[:, k * NDBC:(k + 1) * NDBC], xcbt[:, k * TT:(k + 1) * TT])
                      for k in range(KH2)],
                     [b_wxb, b_xcb])
            P.op("S", lambda s: s.activation(dbt[0][:, :], bkD[0:96, 0:TT], AF.Copy),
                 reads=[bkDb], writes=[dbt[1]])
            dma_out("Y", b_dbcp[0][tc * NDBC:(tc + 1) * NDBC, :], b_dbcp[1],
                    dbt[0][:, :], dbt[1])
            rsl = slice(tc * NDBC, (tc + 1) * NDBC)
            P.op("P", lambda g, rsl=rsl: g.collective_compute(
                "AllReduce", ALU.add, replica_groups=[[0, 1], [2, 3], [4, 5], [6, 7]],
                ins=[b_dbcp[0][rsl, :].opt()], outs=[b_dbc[0][rsl, :].opt()]),
                reads=[b_dbcp[1]], writes=[b_dbc[1]], coll=True)
        def stage23(tc):
            ic = tc % 2
            col = slice(tc * TT, (tc + 1) * TT)
            for (base, dstt, dstbs) in ((DTR, bbct, bbcb), (DTR + NB_, cbct, cbcb)):
                for n in range(NB_):
                    rT, rB = brow2[0]
                    rowsrc = b_dbc[0][tc * NDBC + base + n:tc * NDBC + base + n + 1, :]
                    P.op("Y", lambda e, rT=rT, rowsrc=rowsrc: e.dma_start(
                        out=rT[0:1, :], in_=rowsrc), reads=[b_dbc[1]], writes=[rB],
                        dma=True)
                    bcast(rot, rT[0:1, :], rB, dstt[:, n * TT:(n + 1) * TT], dstbs[n],
                          TT, eng="S")
            dT_, dB_ = dtt[0]
            dma_in("P", dT_[:, :], dB_, b_dbc[0][tc * NDBC:tc * NDBC + DTR, :],
                   b_dbc[1])
            ygT, ygB = ygt[ic]
            for dk in range(KH2):
                idb = dk % 2
                bk3, bk3b = rot()
                bko3 = bk3[0:128, 0:TT]
                mm_group(bko3, bk3b, [(wdtb[:, dk * 128:(dk + 1) * 128], dT_[:, :])],
                         [b_wdtb, dB_])
                bc = r_bdtb[0][:, dk:dk + 1]
                dT5, dB5 = dlt2[idb]
                spaT, spaB = spa1
                spbT, spbB = spb1
                P.op("S", lambda s, spbT=spbT, bko3=bko3, bc=bc: s.activation(
                    spbT[:, :], bko3, AF.Abs, bias=bc), reads=[bk3b, r_bdtb[1]],
                    writes=[spbB])
                P.op("S", lambda s, spbT=spbT: s.activation(spbT[:, :], spbT[:, :],
                                                            AF.Exp, scale=-1.0),
                     reads=[spbB], writes=[spbB])
                P.op("S", lambda s, spbT=spbT: s.activation(spbT[:, :], spbT[:, :],
                                                            AF.Ln, bias=1.0),
                     reads=[spbB], writes=[spbB])
                P.op("S", lambda s, spaT=spaT, bko3=bko3, bc=bc: s.activation(
                    spaT[:, :], bko3, AF.Relu, bias=bc), reads=[bk3b, r_bdtb[1]],
                    writes=[spaB])
                P.op("V", lambda v, dT5=dT5, spaT=spaT, spbT=spbT: v.tensor_add(
                    dT5[:, :], spaT[:, :], spbT[:, :]), reads=[spaB, spbB], writes=[dB5])
                xT4, xB4 = xc4[idb]
                dma_in("Y", xT4[:, :], xB4, b_xc[0][dk * 128:(dk + 1) * 128, col], b_xc[1])
                P.op("P", lambda g, dT5=dT5, xT4=xT4: g.tensor_mul(
                    wdx[0][:, :], dT5[:, :], xT4[:, :]), reads=[dB5, xB4], writes=[wdx[1]])
                zT5, zB5 = zt51
                sgT, sgB = sg51
                dma_in("Y", zT5[:, :], zB5, b_z[0][dk * 128:(dk + 1) * 128, col], b_z[1])
                P.op("S", lambda s, zT5=zT5, sgT=sgT: s.activation(
                    sgT[:, :], zT5[:, :], AF.Silu), reads=[zB5], writes=[sgB])
                for q in range(NB_ // NQ):
                    iq = q % 2
                    argt = argq[iq][0]
                    argb = argbq[iq]
                    b_argt = argq[iq][1]
                    uut = uuq[iq][0]
                    uub = uubq[iq]
                    for j in range(NQ):
                        n = q * NQ + j
                        asl = argt[:, j * TT:(j + 1) * TT]
                        acol = acols[0][:, dk * NB_ + n: dk * NB_ + n + 1]
                        P.op("S", lambda s, asl=asl, dT5=dT5, acol=acol: s.activation(
                            asl, dT5[:, :], AF.Exp, scale=acol),
                            reads=[dB5, acols[1]], writes=[argb[j]])
                        usl = uut[:, j * TT:(j + 1) * TT]
                        bsl = bbct[:, n * TT:(n + 1) * TT]
                        P.op("P", lambda g, usl=usl, bsl=bsl: g.tensor_mul(
                            usl, wdx[0][:, :], bsl), reads=[wdx[1], bbcb[n]],
                            writes=[uub[j]])
                        hsl = hht[:, j * TT:(j + 1) * TT]
                        ccar = car4[0][:, dk * NB_ + n: dk * NB_ + n + 1]
                        P.op("V", lambda v, hsl=hsl, asl=asl, usl=usl, ccar=ccar:
                             v.tensor_tensor_scan(hsl, asl, usl, ccar, ALU.mult, ALU.add),
                             reads=[argb[j], uub[j], car4[1]], writes=[hhb[j]])
                    P.op("S", lambda s, dk=dk, q=q: s.activation(
                        car4[0][:, dk * NB_ + q * NQ:dk * NB_ + (q + 1) * NQ],
                        hht[:, :].rearrange("p (n t) -> p n t", n=NQ)[:, :, TT - 1],
                        AF.Copy),
                        reads=list(hhb), writes=[car4[1]])
                    csl = cbct[:, (q * NQ) * TT:(q * NQ + NQ) * TT]
                    P.op("V", lambda v, csl=csl: v.tensor_mul(hht[:, :], hht[:, :], csl),
                         reads=list(hhb) + cbcb[q * NQ:(q + 1) * NQ], writes=list(hhb))
                    if q == 0:
                        P.op("V", lambda v: v.tensor_add(yy[0][:, :], hht[:, 0:TT],
                                                         hht[:, TT:2 * TT]),
                             reads=[hhb[0], hhb[1]], writes=[yy[1]])
                    else:
                        P.op("V", lambda v: v.tensor_add(hht[:, 0:TT], hht[:, 0:TT],
                                                         hht[:, TT:2 * TT]),
                             reads=[hhb[1]], writes=[hhb[0]])
                        P.op("V", lambda v: v.tensor_add(yy[0][:, :], yy[0][:, :],
                                                         hht[:, 0:TT]),
                             reads=[hhb[0]], writes=[yy[1]])
                dpk = r_dparb[0][:, dk:dk + 1]
                P.op("V", lambda v, dpk=dpk, xT4=xT4: v.scalar_tensor_tensor(
                    yy[0][:, :], xT4[:, :], dpk, yy[0][:, :], ALU.mult, ALU.add),
                    reads=[xB4, r_dparb[1]], writes=[yy[1]])
                yk = ygT[:, dk * TT:(dk + 1) * TT]
                P.op("V", lambda v, yk=yk, sgT=sgT: v.tensor_mul(yk, yy[0][:, :], sgT[:, :]),
                     reads=[yy[1], sgB], writes=[ygB])

        def stage4(tc):
            ic = tc % 2
            ygT, ygB = ygt[ic]
            for m in range(KD):
                bk, bkb = rot()
                bko = bk[0:128, 0:TT]
                mm_group(bko, bkb,
                         [(wob[:, k * D + m * 128: k * D + (m + 1) * 128],
                           ygT[:, k * TT:(k + 1) * TT]) for k in range(KH2)],
                         [b_wob, ygB])
                obT, obB = ob52[m % 2]
                P.op("S", lambda s, obT=obT, bko=bko: s.activation(obT[:, :], bko, AF.Copy),
                     reads=[bkb], writes=[obB])
                dma_out("Y", b_op[0][tc * D + m * 128:tc * D + (m + 1) * 128, :],
                        b_op[1], obT[:, :], obB)
            rsl = slice(tc * D, (tc + 1) * D)
            P.op("P", lambda g, rsl=rsl: g.collective_compute(
                "AllReduce", ALU.add, replica_groups=[[0, 1], [2, 3], [4, 5], [6, 7]],
                ins=[b_op[0][rsl, :].opt()], outs=[b_opr[0][rsl, :].opt()]),
                reads=[b_op[1]], writes=[b_opr[1]], coll=True)

        def b6_chunk(tc):
            for bb in range(TT // 128):
                xT, xB = xf6[0]
                colb = slice(tc * TT + bb * 128, tc * TT + (bb + 1) * 128)
                dma_in("Y", fm3(xT[:, :], KD), xB, dr3(s_x2[0][:, colb]), s_x2[1])
                for m in range(KD):
                    f0T, f0B = f06[0]
                    f1T, f1B = f16[0]
                    dma_in("P", f0T[:, :], f0B,
                           b_opr[0][tc * D + m * 128:tc * D + (m + 1) * 128,
                                    bb * 128:(bb + 1) * 128], b_opr[1])
                    dma_in("P", f1T[:, :], f1B,
                           b_opr[0][(NCH + tc) * D + m * 128:(NCH + tc) * D + (m + 1) * 128,
                                    bb * 128:(bb + 1) * 128], b_opr[1])
                    xk = xT[:, m * 128:(m + 1) * 128]
                    P.op("V", [
                        lambda v, xk=xk, f0T=f0T: v.scalar_tensor_tensor(
                            xk, f0T[:, :], r_even[0][:, :], xk, ALU.mult, ALU.add),
                        lambda v, xk=xk, f1T=f1T: v.scalar_tensor_tensor(
                            xk, f1T[:, :], r_odd[0][:, :], xk, ALU.mult, ALU.add)],
                        reads=[f0B, f1B, r_even[1], r_odd[1]], writes=[xB])
                dma_out("Y", dr3(s_x3[0][:, colb]), s_x3[1], fm3(xT[:, :], KD), xB)

        for tc in range(NCB):
            if tc >= 3:
                stage4(tc - 3)
            if tc >= 2:
                stage23(tc - 2)
            stage1(tc)
            if tc >= 11:
                b6_chunk(tc - 11)
        stage23(NCB - 2)
        stage4(NCB - 3)
        stage23(NCB - 1)
        stage4(NCB - 2)
        stage4(NCB - 1)
        for j in range(NCB - 11, NCH):
            b6_chunk(j)
        P.barrier()
        esB2.close()
        esB.close()


    sqt_e = P.sb("sqt_e", [128, 128], F32)
    vtr = P.sb("vtr", [128, KD * 128], BF16)
    ffn_phase(s_x1, s_x2, "f1_")
    attnb_phase()
    ffn_phase(s_x3, out_fm, "f2_")
    P.barrier()

    P.emit()
    return nc



def _chunk_cols(vec, nch):
    v = np.asarray(vec, np.float32).reshape(nch, 128).T
    return np.ascontiguousarray(v)


def _pad_col(vec, n=128):
    v = np.zeros((n, 1), np.float32)
    a = np.asarray(vec, np.float32).reshape(-1)
    v[:a.shape[0], 0] = a
    return v


def host_inputs(cfg, inputs, core):
    Tc, D, INNER, NA = cfg["Tc"], cfg["D"], cfg["INNER"], cfg["NA"]
    KD, KI = D // 128, INNER // 128
    b, h = core // 2, core % 2
    x = np.asarray(inputs["L0_attn_a_x"], np.float32)
    xs = x[b, h * Tc:(h + 1) * Tc, :]
    d = {}
    d["x_fm"] = np.ascontiguousarray(xs.T)
    if h == 1:
        d["halo_x"] = np.ascontiguousarray(x[b, h * Tc - 2:h * Tc, :].T)
        d["oddmask"] = np.ones((128, 1), np.float32)
        co = np.zeros((8, 1), np.float32)
        co[core - 1, 0] = 1.0
        d["coreoh"] = co
    else:
        d["halo_x"] = np.zeros((D, 2), np.float32)
        d["oddmask"] = np.zeros((128, 1), np.float32)
        d["coreoh"] = np.zeros((8, 1), np.float32)
    d["n1w"] = _chunk_cols(inputs["L0_attn_a_norm1_w"], KD)
    d["n2w"] = _chunk_cols(inputs["L0_attn_a_norm2_w"], KD)
    d["w_ip"] = np.ascontiguousarray(np.asarray(inputs["L0_attn_a_in_proj_w"], np.float32))
    d["b_ip"] = _chunk_cols(inputs["L0_attn_a_in_proj_b"], 2 * KI)
    cw = np.asarray(inputs["L0_attn_a_conv_w"], np.float32).reshape(INNER, 3)
    cwa = np.zeros((128, KI * 3), np.float32)
    for k in range(KI):
        for j in range(3):
            cwa[:, 3 * k + j] = cw[k * 128:(k + 1) * 128, j]
    d["cwa"] = cwa
    d["cba"] = _chunk_cols(inputs["L0_attn_a_conv_b"], KI)
    d["w_dt"] = np.ascontiguousarray(np.asarray(inputs["L0_attn_a_dt_w"], np.float32))
    d["w_bp"] = np.ascontiguousarray(np.asarray(inputs["L0_attn_a_bp_w"], np.float32))
    d["w_cp"] = np.ascontiguousarray(np.asarray(inputs["L0_attn_a_cp_w"], np.float32))
    bd = np.zeros((128, 3), np.float32)
    bd[:NA, 0] = np.asarray(inputs["L0_attn_a_dt_b"], np.float32)
    bd[:NA, 1] = np.asarray(inputs["L0_attn_a_bp_b"], np.float32)
    bd[:NA, 2] = np.asarray(inputs["L0_attn_a_cp_b"], np.float32)
    d["b_dtc"] = bd
    d["w_s2i"] = np.ascontiguousarray(np.asarray(inputs["L0_attn_a_s2i_w"], np.float32))
    d["b_s2i"] = _chunk_cols(inputs["L0_attn_a_s2i_b"], KI)
    d["dpar"] = _chunk_cols(inputs["L0_attn_a_D_param"], KI)
    d["w_oa"] = np.ascontiguousarray(np.asarray(inputs["L0_attn_a_out_w"], np.float32))
    d["b_oa"] = _chunk_cols(inputs["L0_attn_a_out_b"], KD)
    d["w_ga"] = np.ascontiguousarray(np.asarray(inputs["L0_attn_a_gate_w"], np.float32))
    gb = np.asarray(inputs["L0_attn_a_gate_b"], np.float32).reshape(1, 4)
    d["b_gab"] = np.ascontiguousarray(np.broadcast_to(gb, (128, 4)))
    ew1 = np.asarray(inputs["L0_attn_a_e_w1"], np.float32)
    d["e_w1"] = np.ascontiguousarray(ew1.reshape(4 * D, -1))
    HID = ew1.shape[2]
    KH = HID // 128
    eb1 = np.asarray(inputs["L0_attn_a_e_b1"], np.float32)
    m1 = np.zeros((128, 4 * KH), np.float32)
    for e in range(4):
        m1[:, e * KH:(e + 1) * KH] = eb1[e].reshape(KH, 128).T
    d["e_b1"] = m1
    ew2 = np.asarray(inputs["L0_attn_a_e_w2"], np.float32)
    d["e_w2"] = np.ascontiguousarray(ew2.reshape(4 * HID, D))
    d["e_b2"] = np.ascontiguousarray(np.asarray(inputs["L0_attn_a_e_b2"], np.float32))
    d["idn"] = np.eye(128, dtype=np.float32)
    s4 = np.zeros((4, 512), np.float32)
    for e in range(4):
        s4[e, e * 128:(e + 1) * 128] = 1.0
    d["sel4"] = s4
    d["w_rt"] = np.ascontiguousarray(np.asarray(inputs["L0_ffn_router_weight"], np.float32).T)
    d["wf1"] = np.ascontiguousarray(np.asarray(inputs["L0_ffn_ff_pre_act_weight"], np.float32)[core].T)
    d["wfg"] = np.ascontiguousarray(np.asarray(inputs["L0_ffn_gate_weight"], np.float32)[core].T)
    d["wf2"] = np.ascontiguousarray(np.asarray(inputs["L0_ffn_ff_post_act_weight"], np.float32)[core].T)
    d["ecol"] = np.full((128, 1), float(core), np.float32)
    d["cbase"] = np.full((128, 1), float(core * Tc), np.float32)
    JW = Tc // 128
    qi = np.zeros((128, 2), np.float32)
    qi[:, 0] = np.arange(128) % JW
    qi[:, 1] = np.arange(128) // JW
    d["qiw"] = qi
    dh = h
    DH = 1024
    win = np.asarray(inputs["L0_attn_b_W_in"], np.float32)
    xc_rows = win[dh * DH:(dh + 1) * DH, :]
    z_rows = win[2048 + dh * DH: 2048 + (dh + 1) * DH, :]
    d["w_inb"] = np.ascontiguousarray(np.concatenate([xc_rows.T, z_rows.T], axis=1))
    cwb_ = np.asarray(inputs["L0_attn_b_conv_w"], np.float32).reshape(2048, 4)[dh * DH:(dh + 1) * DH]
    m2 = np.zeros((128, 8 * 4), np.float32)
    for k in range(8):
        for j in range(4):
            m2[:, 4 * k + j] = cwb_[k * 128:(k + 1) * 128, j]
    d["cwb"] = m2
    d["cbb"] = _chunk_cols(np.asarray(inputs["L0_attn_b_conv_b"], np.float32)[dh * DH:(dh + 1) * DH], 8)
    wx = np.asarray(inputs["L0_attn_b_W_x"], np.float32)
    d["w_xb"] = np.ascontiguousarray(wx.T[dh * DH:(dh + 1) * DH, :])
    wdt2 = np.asarray(inputs["L0_attn_b_W_dt"], np.float32)
    d["w_dtb"] = np.ascontiguousarray(wdt2.T[:, dh * DH:(dh + 1) * DH])
    d["bdtb"] = _chunk_cols(np.asarray(inputs["L0_attn_b_b_dt"], np.float32)[dh * DH:(dh + 1) * DH], 8)
    ar = np.asarray(inputs["L0_attn_b_A_raw"], np.float32)
    arT = ar.T[dh * DH:(dh + 1) * DH, :]
    m3 = np.zeros((128, 8 * 16), np.float32)
    for k in range(8):
        m3[:, k * 16:(k + 1) * 16] = arT[k * 128:(k + 1) * 128, :]
    d["arawt"] = m3
    d["dparb"] = _chunk_cols(np.asarray(inputs["L0_attn_b_D"], np.float32)[dh * DH:(dh + 1) * DH], 8)
    wo = np.asarray(inputs["L0_attn_b_W_out"], np.float32)
    d["w_ob"] = np.ascontiguousarray(wo.T[dh * DH:(dh + 1) * DH, :])
    d["evenm"] = np.full((128, 1), 1.0 if h == 0 else 0.0, np.float32)
    return d


def kernel(**inputs):
    cfg = make_cfg(4096)
    nc = build_program(cfg)
    in_maps = [host_inputs(cfg, inputs, c) for c in range(NCORES)]
    res = run_bass_kernel_spmd(nc, in_maps, core_ids=list(range(NCORES)))
    Tc, D = cfg["Tc"], cfg["D"]
    out = np.zeros((4, 2 * Tc, D), np.float32)
    for c in range(NCORES):
        b, h = c // 2, c % 2
        out[b, h * Tc:(h + 1) * Tc, :] = np.asarray(res.results[c]["out_fm"]).T
    return out
